# version 07
# baseline (speedup 1.0000x reference)
import numpy as np
import concourse.bass as bass
import concourse.mybir as mybir
from concourse.bass_utils import run_bass_kernel_spmd

F32 = mybir.dt.float32
BF16 = mybir.dt.bfloat16
F16 = mybir.dt.float16
AF = mybir.ActivationFunctionType
ALU = mybir.AluOpType
AX = mybir.AxisListType

HID, DIN, DSTATE = 2048, 4096, 128
CONVD = DIN + 2 * DSTATE
NH, HDIM, FF, KCONV = 64, 64, 8192, 4
EPS = 1e-5
MAGIC = 12582912.0
CH = 128
KH, KD, KC, KF, M_IN = 16, 32, 34, 64, 67


def cdiv(a, b):
    return (a + b - 1) // b


def nslices(total, step=512, off=0):
    out = []
    o = 0
    while o < total:
        out.append((off + o, min(step, total - o)))
        o += step
    return out


class Sem:
    def __init__(self, h):
        self.h = h
        self.n = 0

    def inc(self, k=1):
        self.n += k
        return ((self, k),)


_ESZ = {"dt.float32": 4, "dt.bfloat16": 2, "dt.float16": 2, "dt.int32": 4,
        "dt.uint8": 1, "dt.int8": 1, "dt.float8e4": 1, "dt.uint32": 4}


def _ap_span(a):
    try:
        name = a.tensor.name
        pairs = list(a.ap)
    except AttributeError:
        return None
    esz = _ESZ.get(str(a.dtype), 4)
    ext = 1
    if len(pairs) > 1:
        ext = sum(s * (n - 1) for s, n in pairs[1:]) + 1
    off = a.offset
    return (name, off * esz, (off + ext) * esz)


def _overlap(fp, fps):
    for name, lo, hi in fps:
        if fp[0] == name and fp[1] < hi and fp[2] > lo:
            return True
    return False


class EngQ:
    def __init__(self, auto_drain=False):
        self.ops = []
        self.auto = auto_drain
        self.pend = []
        self.pendr = []

    def c(self, method, *a, then=(), **kw):
        if self.auto:
            wr, rd = [], []
            outs = []
            if "out" in kw:
                outs.append(kw["out"])
            if a:
                outs.append(a[0])
            for i, x in enumerate(a[1:]):
                s = _ap_span(x)
                if s:
                    rd.append(s)
            for key in ("in_", "bias", "scale"):
                if key in kw:
                    s = _ap_span(kw[key])
                    if s:
                        rd.append(s)
            for x in kw.get("ins", ()):
                s = _ap_span(x)
                if s:
                    rd.append(s)
            for x in kw.get("outs", ()):
                outs.append(x)
            for x in outs:
                s = _ap_span(x)
                if s:
                    wr.append(s)
            hit = any(_overlap(s, self.pend) for s in rd) or \
                any(_overlap(s, self.pend) for s in wr) or \
                any(_overlap(s, self.pendr) for s in wr)
            if hit:
                if self.ops and self.ops[-1][0] != "drain":
                    self.ops.append(("drain", (), {}, ()))
                self.pend = []
                self.pendr = []
            self.pend.extend(wr)
            self.pendr.extend(rd)
        self.ops.append((method, a, kw, tuple(then)))

    def w(self, sem, n=None):
        n = sem.n if n is None else n
        if n > 0:
            self.ops.append(("wait_ge", (sem.h, n), {}, ()))

    def replay(self, eng):
        for m, a, kw, thens in self.ops:
            r = getattr(eng, m)(*a, **kw)
            for sem, k in thens:
                r.then_inc(sem.h, k)


def build_program(t_core=2048, pass_t=1024, fb=512, stop_after=None, debug=(), rg=None):
    assert t_core % pass_t == 0 and pass_t % CH == 0 and pass_t % fb == 0
    n_pass = t_core // pass_t
    n_chunks = t_core // CH
    nch_pass = pass_t // CH
    Te = pass_t + 3
    dbg = set(debug)

    nc = bass.Bass("TRN2", target_bir_lowering=False, debug=False)
    DT = nc.dram_tensor

    x_fm = DT("x_fm", [HID, t_core + 3], F32, kind="ExternalInput").ap()
    w_in_p = DT("w_in_p", [M_IN, HID, 128], F32, kind="ExternalInput").ap()
    w1_p = DT("w1_p", [KF, HID, 128], F32, kind="ExternalInput").ap()
    w2_p = DT("w2_p", [KH, FF, 128], F32, kind="ExternalInput").ap()
    wout_p = DT("wout_p", [KH, DIN, 128], F32, kind="ExternalInput").ap()
    conv_w_in = DT("conv_w", [128, KC * KCONV], F32, kind="ExternalInput").ap()
    conv_b_in = DT("conv_b", [128, KC], F32, kind="ExternalInput").ap()
    dtb_in = DT("dtb", [NH, 1], F32, kind="ExternalInput").ap()
    alog_in = DT("alog", [NH, 1], F32, kind="ExternalInput").ap()
    dvec_in = DT("dvec", [NH, 1], F32, kind="ExternalInput").ap()
    nw_in = DT("nw", [128, KD], F32, kind="ExternalInput").ap()
    b1_in = DT("b1", [128, KF], F32, kind="ExternalInput").ap()
    ga_in = DT("ga", [128, KF], F32, kind="ExternalInput").ap()
    be_in = DT("be", [128, KF], F32, kind="ExternalInput").ap()
    b2_in = DT("b2", [128, KH], F32, kind="ExternalInput").ap()
    idf_in = DT("idf", [128, 128], F32, kind="ExternalInput").ap()
    idb_in = DT("idb", [128, 128], BF16, kind="ExternalInput").ap()
    up01_in = DT("up01", [CH, CH], BF16, kind="ExternalInput").ap()
    hsel_in = DT("hsel", [NH, DIN], BF16, kind="ExternalInput").ap()
    psel_in = DT("psel", [NH, 128], BF16, kind="ExternalInput").ap()
    bsel_in = DT("bsel", [NH, KD], BF16, kind="ExternalInput").ap()
    amask_in = DT("amask", [128, 4], F32, kind="ExternalInput").ap()
    dmask_in = DT("dmask", [NH, 4], F32, kind="ExternalInput").ap()

    out_fm = DT("out_fm", [HID, t_core], F32, kind="ExternalOutput").ap()
    dbg_specs = {
        "d_z": ([KD, 128, t_core], F16),
        "d_xbc": ([KC, 128, t_core], BF16),
        "d_dt": ([NH, t_core], F32),
        "d_acs": ([NH, t_core], F32),
        "d_yp": ([KD, 128, t_core], F16),
        "d_state": ([128, DIN + 2], F32),
        "d_xq": ([KH, 128, t_core], F16),
        "d_h": ([KF, 128, t_core], F16),
        "d_g": ([KH, 128, t_core], F16),
        "d_ynorm": ([KD, 128, t_core], F16),
        "d_sin": ([128, DIN], F32),
        "d_xdtt": ([128, DIN], BF16),
        "d_mp": ([128, NH * CH], BF16),
        "d_xn": ([KH, 128, t_core], F16),
    }
    dout = {}
    for name in dbg:
        shp, dt = dbg_specs[name]
        dout[name] = DT(name, shp, dt, kind="ExternalOutput").ap()

    wbf_in = DT("wbf_in", [M_IN, HID, 128], F16, kind="Internal").ap()
    wbf_1 = DT("wbf_1", [KF, HID, 128], F16, kind="Internal").ap()
    wbf_2 = DT("wbf_2", [KH, FF, 128], F16, kind="Internal").ap()
    wbf_o = DT("wbf_o", [KH, DIN, 128], F16, kind="Internal").ap()
    z_sp = DT("z_sp", [KD, 128, t_core], F16, kind="Internal").ap()
    yp_sp = DT("yp_sp", [KD, 128, t_core], F16, kind="Internal").ap()
    g_sp = DT("g_sp", [KH, 128, t_core], F16, kind="Internal").ap()
    st_sp = DT("st_sp", [n_chunks, 128, DIN], F16, kind="Internal").ap()
    xq_sp = DT("xq_sp", [KH, 128, t_core], F16, kind="Internal").ap()
    c_sp = DT("c_sp", [n_chunks, 128, CH], BF16, kind="Internal").ap()
    cc_in = DT("cc_in", [128, DIN + 2], F32, kind="Internal").ap()
    cc_out = DT("cc_out", [4, 128, DIN + 2], F32, kind="Internal",
                addr_space="Local").ap()

    from contextlib import ExitStack
    ctx = ExitStack()

    def sb(name, shape, dt):
        return ctx.enter_context(nc.sbuf_tensor(name, shape, dt))

    def ps_(name, shape, dt):
        return ctx.enter_context(nc.psum_tensor(name, shape, dt))

    def prod(xs):
        r = 1
        for x in xs:
            r *= x
        return r

    def vw(arena, off_bytes, dtype, shape):
        esz = 4 if dtype == F32 else 2
        assert off_bytes % 8 == 0
        ap = arena[:] if dtype == F32 else arena[:].bitcast(dtype)
        start = off_bytes // esz
        per = prod(shape[1:])
        flat = ap[:, start:start + per]
        if len(shape) == 3:
            return flat.rearrange("q (a b) -> q a b", a=shape[1])
        return flat

    wA = max(KH * Te * 4, KC * Te * 2, KF * fb * 2, KD * pass_t * 2)
    wA = (wA + 31) // 32 * 32
    arA = sb("arA", [128, wA // 4], F32)
    xf32 = vw(arA, 0, F32, [128, KH, Te])
    big1 = vw(arA, 0, BF16, [128, KC, Te])
    h_sb = vw(arA, 0, F16, [128, KF, fb])
    ynb = vw(arA, 0, F16, [128, KD, pass_t])

    wB = max(KH * Te * 2,
             ((DIN + 2) * 4 + 31) // 32 * 32 + 2 * KD * 256)
    wB = (wB + 31) // 32 * 32
    arB = sb("arB", [128, wB // 4], F32)
    xn_sb = vw(arB, 0, F16, [128, KH, Te])
    ccld = vw(arB, 0, F32, [128, DIN + 2])
    gxl = vw(arB, 0, F16, [128, KH, 512])
    pb_ld2 = vw(arB, ((DIN + 2) * 4 + 31) // 32 * 32, F16, [128, KD, 128])
    pb_ld3 = vw(arB, ((DIN + 2) * 4 + 31) // 32 * 32 + KD * 256, F16,
                [128, KD, 128])


    arC = sb("arC", [128, 8192], F32)
    xdtT = vw(arC, 0, BF16, [128, KD, 128])
    mp_sb = vw(arC, 8192, BF16, [128, NH, CH])
    ystage = vw(arC, 24576, F16, [128, 8, 128])
    zstage = vw(arC, 26624, F16, [128, 2, 512])
    xqstage = zstage
    xq_sb = vw(arC, 0, F16, [128, KH, fb])
    wb2 = vw(arC, 0, F16, [128, KF, 128])
    ffnbc = vw(arC, 16384, F32, [128, 8, 512])
    stot = vw(arC, 0, F32, [128, KD, 128])
    stotT = vw(arC, 16384, BF16, [128, KD, 128])
    pb_ld = vw(arC, 24576, F16, [128, KD, 128])

    c_idf = sb("c_idf", [128, 128], F32)
    c_idb = sb("c_idb", [128, 128], BF16)
    c_up01 = sb("c_up01", [CH, CH], BF16)
    c_hsel = sb("c_hsel", [NH, DIN], BF16)
    c_psel = sb("c_psel", [NH, 128], BF16)
    c_bsel = sb("c_bsel", [NH, KD], BF16)
    cw_sb = sb("cw_sb", [128, KC * KCONV], F32)
    cb_sb = sb("cb_sb", [128, KC], F32)
    smallv = sb("smallv", [NH, 8], F32)
    small_bf = sb("small_bf", [NH, 4], BF16)
    seld = sb("seld", [NH, KD], BF16)
    ga_sb = sb("ga_sb", [128, KF], F32)
    be_sb = sb("be_sb", [128, KF], F32)

    stat4T = sb("stat4T", [16, 128], F32)
    thr_sb = sb("thr_sb", [128, 4], F32)
    nw_sb = sb("nw_sb", [128, KD], F32)
    b1_sb = sb("b1_sb", [128, KF], F32)
    b2_sb = sb("b2_sb", [128, KH], F32)
    amask_sb = sb("amask_sb", [128, 4], F32)
    dmask_sb = sb("dmask_sb", [NH, 4], F32)
    ones_r = sb("ones_r", [1, 128], F32)
    ones_c = sb("ones_c", [128, 1], F32)
    ones_cb = sb("ones_cb", [128, 1], F16)
    ones_cbb = sb("ones_cbb", [128, 1], BF16)
    ones64 = sb("ones64", [NH, CH], F32)
    dexp_sb = sb("dexp_sb", [128, KD], F32)
    mw_col = sb("mw_col", [128, 2], F32)
    wsacc = sb("wsacc", [128, 2], F32)
    acs_sb = sb("acs_sb", [NH, t_core], F32)
    cend_sb = sb("cend_sb", [NH, 48], F32)
    s_state = sb("s_state", [128, KD, 128], F16)
    wp_a = sb("wp_a", [128, KF, 128], F16)
    wp_b = sb("wp_b", [128, KD, 128], F16)
    gstage = wp_b[:].rearrange("q a b -> q (a b)").rearrange(
        "q (a b) -> q a b", a=2)[:, :, 0:512]
    scr1 = sb("scr1", [128, 1032], F32)
    scr2 = sb("scr2", [128, 1032], F32)
    scr3 = sb("scr3", [128, 1032], F32)
    ug_sb = scr3[:, 0:512]
    sq_sb = scr3[:].bitcast(BF16)[:, 1536:2048]
    ea2_sb = scr3[:].bitcast(BF16)[:, 1024:1536]
    flat_sb = sb("flat_sb", [1, 1032], F32)
    shB = sb("shB", [128, 1032], F32)
    dt_raw = shB[:]
    outst = shB[:, 0:1024].rearrange("q (a b) -> q a b", a=2)
    dtA = dt_raw
    acsT = sb("acsT", [128, NH], F32)
    tb1 = sb("tb1", [128, NH], BF16)
    tb2 = sb("tb2", [128, NH], BF16)
    gm_sb = sb("gm_sb", [128, CH], BF16)
    dcy_sb = sb("dcy_sb", [NH, CH], F32)
    adt_sb = dcy_sb
    stat4 = sb("stat4", [128, 16], F32)
    dcb = sb("dcb", [NH, 2 * CH], BF16)
    bt_sb = sb("bt_sb", [128, CH], BF16)
    c_ld = bt_sb
    dexp2_sb = sb("dexp2_sb", [128, KD], F32)
    cst = sb("cst", [128, 4], F32)

    pbk = [ps_(f"pb{i}", [128, 512], F32) for i in range(6)]
    pbf = [ps_(f"pbf{i}", [128, 1024], BF16) for i in range(2)]

    T = EngQ()
    V = EngQ(auto_drain=True)
    SC = EngQ(auto_drain=True)
    G = EngQ(auto_drain=True)
    SY = EngQ()
    def mksem(name):
        return Sem(ctx.enter_context(nc.semaphore(name)))
    s_t, s_v, s_sc = mksem("s_t"), mksem("s_v"), mksem("s_sc")
    s_small = mksem("s_small")
    s_x = mksem("s_x")
    s_wpa = mksem("s_wpa")
    s_wpb = mksem("s_wpb")
    s_wba = mksem("s_wba")
    s_wbb = mksem("s_wbb")
    s_wpc = mksem("s_wpc")
    s_wpd = mksem("s_wpd")
    s_wbc = mksem("s_wbc")
    s_wbd = mksem("s_wbd")
    s_zs0 = mksem("s_zs0")
    s_zs1 = mksem("s_zs1")
    s_dbg = mksem("s_dbg")
    s_o = mksem("s_o")
    s_yst = mksem("s_yst")
    s_stp = mksem("s_stp")
    s_gb = mksem("s_gb")
    s_cs = mksem("s_cs")
    s_cc = mksem("s_cc")
    s_ld2 = mksem("s_ld2")
    s_fl0 = mksem("s_fl0")
    s_fl1 = mksem("s_fl1")
    s_pre0 = mksem("s_pre0")
    s_pre1 = mksem("s_pre1")
    s_gsp = mksem("s_gsp")
    s_gl = mksem("s_gl")
    s_gc = mksem("s_gc")
    s_xq0 = mksem("s_xq0")
    s_xq1 = mksem("s_xq1")
    s_xql = mksem("s_xql")

    for dst, src in [
        (c_idf[:], idf_in), (c_idb[:], idb_in), (c_up01[:], up01_in),
        (c_hsel[:], hsel_in), (c_psel[:], psel_in), (c_bsel[:], bsel_in),
        (cw_sb[:], conv_w_in), (cb_sb[:], conv_b_in),
        (smallv[:, 0:1], dtb_in), (smallv[:, 1:2], alog_in),
        (smallv[:, 3:4], dvec_in),
        (nw_sb[:], nw_in), (b1_sb[:], b1_in), (b2_sb[:], b2_in),
        (ga_sb[:], ga_in), (be_sb[:], be_in),
        (amask_sb[:], amask_in), (dmask_sb[:], dmask_in),
    ]:
        SY.c("dma_start", out=dst, in_=src, then=s_small.inc(16))
    y_small = s_small.n

    V.c("memset", cst[:, 0:1], EPS)
    V.c("memset", cst[:, 1:2], 1.0)
    V.c("memset", cst[:, 2:3], -1.0)
    V.c("memset", cst[:, 3:4], 0.0)
    V.c("memset", ones_r[:], 1.0)
    V.c("memset", ones_c[:], 1.0)
    V.c("memset", ones_cb[:], 1.0)
    V.c("memset", ones_cbb[:], 1.0)
    V.c("memset", ones64[:], 1.0)
    V.c("memset", s_state[:], 0.0, then=s_v.inc())

    SC.w(s_small, y_small)
    SC.c("activation", smallv[:, 2:3], smallv[:, 1:2], AF.Exp, then=s_sc.inc())
    V.w(s_sc, s_sc.n)
    V.c("tensor_scalar_mul", smallv[:, 2:3], smallv[:, 2:3], -1.0)
    V.c("tensor_copy", small_bf[:, 0:1], smallv[:, 3:4], then=s_v.inc())

    def expand_hp(colvec_f32, dst_sb, bank):
        V.w(s_t, s_t.n)
        V.c("tensor_scalar", seld[:], c_bsel[:], colvec_f32, None, ALU.mult,
            then=s_v.inc())
        T.w(s_v, s_v.n)
        T.c("matmul", pbk[bank][:, 0:KD], c_psel[:], seld[:], start=True,
            stop=True, then=s_t.inc())
        V.w(s_t, s_t.n)
        V.c("tensor_copy", dst_sb, pbk[bank][:, 0:KD], then=s_v.inc())

    expand_hp(smallv[:, 3:4], dexp_sb[:], 4)

    wpf = wp_a[:].rearrange("q a b -> q (a b)").bitcast(F32)
    prep_units = [("w1", i, 0) for i in range(KF)] + \
                 [("w2", i, q) for i in range(KH) for q in range(4)]
    prep_state = {"i": 0, "vsum": 0}

    def wprep_step(n_units):
        for _ in range(n_units):
            i = prep_state["i"]
            if i >= len(prep_units):
                return
            prep_state["i"] = i + 1
            u = prep_units[i]
            half = i % 2
            hsem = s_pre0 if half == 0 else s_pre1
            dst = wpf[:, half * 2048:half * 2048 + 2048]
            dst2 = dst.rearrange("q (k c) -> q k c", c=128)
            if u[0] == "w1":
                src = w1_p[u[1]].rearrange("(k q) c -> q k c", q=128)
                col = 0
            else:
                src = w2_p[u[1]].rearrange("(k q) c -> q k c", q=128)[
                    :, u[2] * 16:u[2] * 16 + 16, :]
                col = 1
            if i == 0:
                SY.w(s_t, s_t.n)
                SY.w(s_wba, s_wba.n)
                SY.w(s_wbb, s_wbb.n)
                V.c("memset", wsacc[:], 0.0)
            SY.w(s_v, prep_state["vsum"])
            SY.c("dma_start", out=dst2, in_=src, then=hsem.inc(16))
            V.w(hsem, hsem.n)
            V.c("tensor_reduce", scr2[:, 1030:1031], dst, AX.X, ALU.add,
                apply_absolute_value=True)
            V.c("tensor_tensor", wsacc[:, col:col + 1], wsacc[:, col:col + 1],
                scr2[:, 1030:1031], ALU.add, then=s_v.inc())
            prep_state["vsum"] = s_v.n

    def wprep_finalize():
        V.w(s_v, prep_state["vsum"])
        T.w(s_v, prep_state["vsum"])
        T.c("matmul", pbk[0][0:1, 504:506], ones_c[:], wsacc[:, 0:2],
            start=True, stop=True, then=s_t.inc())
        V.w(s_t, s_t.n)
        V.c("tensor_scalar", scr3[0:1, 1024:1026], pbk[0][0:1, 504:506],
            1.0 / (FF * HID), EPS, ALU.mult, ALU.max, then=s_v.inc())
        T.w(s_v, s_v.n)
        T.c("matmul", pbk[1][:, 504:506], ones_r[:], scr3[0:1, 1024:1026],
            start=True, stop=True, then=s_t.inc())
        V.w(s_t, s_t.n)
        V.c("tensor_scalar_mul", thr_sb[:, 0:1], pbk[1][:, 504:505], 0.5)
        V.c("tensor_scalar_mul", thr_sb[:, 1:2], pbk[1][:, 504:505], -0.5)
        V.c("tensor_scalar_mul", thr_sb[:, 2:3], pbk[1][:, 505:506], 0.5)
        V.c("tensor_scalar_mul", thr_sb[:, 3:4], pbk[1][:, 505:506], -0.5)
        V.c("tensor_copy", mw_col[:, 0:2], pbk[1][:, 504:506],
            then=s_v.inc())

    def stage_a(p):
        c0 = p * pass_t
        first = (p == 0)
        sl_x = nslices(Te)
        sl_a = nslices(pass_t, off=3)

        if gate:
            SY.w(s_v, gate["v"])
            V.w(s_t, gate["t"])
            V.w(s_sc, gate["sc"])
            SC.w(s_v, gate["v"])
        SY.c("dma_start", out=xf32[:],
             in_=x_fm.rearrange("(k q) t -> q k t", q=128)[:, :, c0:c0 + Te],
             then=s_x.inc(16))
        x_rdy = s_x.n

        V.w(s_x, x_rdy)
        V.c("tensor_tensor", scr1[:, 0:Te], xf32[:, 0, :], xf32[:, 1, :], ALU.add)
        for k in range(2, KH):
            V.c("tensor_tensor", scr1[:, 0:Te], scr1[:, 0:Te], xf32[:, k, :],
                ALU.add, then=(s_v.inc() if k == KH - 1 else ()))
        for k in range(KH):
            V.c("tensor_tensor", scr2[:, 0:Te], xf32[:, k, :], xf32[:, k, :],
                ALU.mult)
            if k == 0:
                V.c("tensor_copy", scr3[:, 0:Te], scr2[:, 0:Te])
            else:
                V.c("tensor_tensor", scr3[:, 0:Te], scr3[:, 0:Te],
                    scr2[:, 0:Te], ALU.add,
                    then=(s_v.inc() if k == KH - 1 else ()))
        v_stats = s_v.n

        T.w(s_v, v_stats)
        regions = {}
        items = []
        for stat, src, bks in [(0, scr1, (2, 3)), (1, scr3, (4, 5))]:
            for si, (o, w) in enumerate(sl_x):
                if si == 0:
                    reg = pbk[bks[0]][0:1, 0:w]
                elif si == 1:
                    reg = pbk[bks[1]][0:1, 0:w]
                else:
                    reg = pbk[bks[0]][0:1, 505:505 + w]
                regions[(stat, si)] = (reg, o, w)
                items.append((reg, src, o, w))
        for i, (reg, src, o, w) in enumerate(items):
            T.c("matmul", reg, ones_c[:], src[:, o:o + w], start=True,
                stop=True, then=(s_t.inc() if i == len(items) - 1 else ()))
        t_stats = s_t.n

        V.w(s_t, t_stats)
        for stat, dst in [(0, scr2), (1, flat_sb)]:
            for si in range(len(sl_x)):
                reg, o, w = regions[(stat, si)]
                V.c("tensor_scalar_mul", dst[0:1, o:o + w], reg, 1.0 / HID)
        V.c("tensor_tensor", dt_raw[0:1, 0:Te], scr2[0:1, 0:Te],
            scr2[0:1, 0:Te], ALU.mult)
        V.c("tensor_tensor", flat_sb[0:1, 0:Te], flat_sb[0:1, 0:Te],
            dt_raw[0:1, 0:Te], ALU.subtract, then=s_v.inc())
        SC.w(s_v, s_v.n)
        SC.c("activation", dt_raw[0:1, 0:Te], flat_sb[0:1, 0:Te], AF.Sqrt,
             bias=cst[0:1, 0:1], then=s_sc.inc())
        V.w(s_sc, s_sc.n)
        V.c("reciprocal", flat_sb[0:1, 0:Te], dt_raw[0:1, 0:Te])
        V.c("tensor_tensor", dt_raw[0:1, 0:Te], scr2[0:1, 0:Te],
            flat_sb[0:1, 0:Te], ALU.mult)
        V.c("tensor_scalar_mul", dt_raw[0:1, 0:Te], dt_raw[0:1, 0:Te], -1.0,
            then=s_v.inc())
        v_ab = s_v.n

        T.w(s_v, v_ab)
        v_apply = 0
        for si, (o, w) in enumerate(sl_x):
            if si > 0:
                T.w(s_v, v_apply)
            T.c("matmul", pbk[4][:, 0:w], ones_r[:], flat_sb[0:1, o:o + w],
                start=True, stop=True)
            T.c("matmul", pbk[5][:, 0:w], ones_r[:], dt_raw[0:1, o:o + w],
                start=True, stop=True, then=s_t.inc())
            V.w(s_t, s_t.n)
            for k in range(KH):
                hof = 512 * (k % 2)
                V.c("tensor_tensor", scr3[:, hof:hof + w],
                    xf32[:, k, o:o + w], pbk[4][:, 0:w], ALU.mult)
                V.c("tensor_tensor", xf32[:, k, o:o + w],
                    scr3[:, hof:hof + w], pbk[5][:, 0:w], ALU.add,
                    then=s_v.inc())
                SC.w(s_v, s_v.n)
                SC.c("activation", xn_sb[:, k, o:o + w],
                     xf32[:, k, o:o + w], AF.Copy, bias=0.0,
                     then=s_sc.inc())
            xsw = xf32[:, :, o:o + w].rearrange("q k t -> q t k")
            V.c("tensor_reduce", scr1[:, o:o + w], xsw, AX.X, ALU.max)
            V.c("tensor_reduce", scr3[:, 0:w], xsw, AX.X, ALU.min)
            V.c("scalar_tensor_tensor", scr1[:, o:o + w], scr3[:, 0:w],
                -1.0, scr1[:, o:o + w], ALU.mult, ALU.max,
                then=s_v.inc())
            v_apply = s_v.n
        v_xn = s_v.n
        sc_xn = s_sc.n

        nj = pass_t // 128
        for jg in range(cdiv(nj, 4)):
            ng = min(4, nj - jg * 4)
            T.w(s_v, v_xn if jg == 0 else s_v.n)
            for j4 in range(ng):
                j = jg * 4 + j4
                T.c("transpose", pbk[4][:, j4 * 128:(j4 + 1) * 128],
                    scr1[:, 3 + j * 128:3 + (j + 1) * 128], c_idf[:],
                    then=(s_t.inc() if j4 == ng - 1 else ()))
            V.w(s_t, s_t.n)
            for j4 in range(ng):
                j = jg * 4 + j4
                V.c("tensor_reduce", stat4[:, 8 + j:9 + j],
                    pbk[4][:, j4 * 128:(j4 + 1) * 128], AX.X, ALU.max,
                    then=(s_v.inc() if j4 == ng - 1 else ()))
        T.w(s_v, s_v.n)
        T.c("transpose", pbk[4][0:nj, 0:128], stat4[:, 8:8 + nj], c_idf[:],
            then=s_t.inc())
        V.w(s_t, s_t.n)
        V.c("tensor_copy", stat4T[0:nj, :], pbk[4][0:nj, 0:128],
            then=s_v.inc())
        G.w(s_v, s_v.n)
        G.c("dma_start", out=scr3[0:1, 0:pass_t], in_=stat4T[0:nj, :],
            then=s_gb.inc(16))
        V.w(s_gb, s_gb.n)
        V.c("tensor_scalar_max", scr3[0:1, 0:pass_t], scr3[0:1, 0:pass_t],
            EPS)
        V.c("reciprocal", scr2[0:1, 0:pass_t], scr3[0:1, 0:pass_t])
        V.c("tensor_scalar_mul", dt_raw[0:1, 3:3 + pass_t],
            scr2[0:1, 0:pass_t], 127.0, then=s_v.inc())
        v_qrows = s_v.n

        for si2, (oq, wq) in enumerate(nslices(pass_t, 512, off=3)):
            T.w(s_v, v_qrows if si2 == 0 else s_v.n)
            T.c("matmul", pbk[4][:, 0:wq], ones_r[:],
                dt_raw[0:1, oq:oq + wq], start=True, stop=True,
                then=s_t.inc())
            V.w(s_t, s_t.n)
            for k in range(KH):
                slot = k % 2
                hof = 512 * slot
                qsem = s_xq0 if slot == 0 else s_xq1
                V.c("tensor_tensor", scr3[:, hof:hof + wq],
                    xf32[:, k, oq:oq + wq], pbk[4][:, 0:wq], ALU.mult)
                if qsem.n:
                    V.w(qsem, qsem.n)
                V.c("tensor_scalar", xqstage[:, slot, 0:wq],
                    scr3[:, hof:hof + wq],
                    MAGIC, -MAGIC, ALU.add, ALU.add, then=s_v.inc())
                SY.w(s_v, s_v.n)
                SY.c("dma_start",
                     out=xq_sp[k, :, c0 + oq - 3:c0 + oq - 3 + wq],
                     in_=xqstage[:, slot, 0:wq], then=qsem.inc(16))
        v_xq_done = s_v.n

        if "d_xn" in dout:
            SY.w(s_v, v_xn)
            SY.c("dma_start",
                 out=dout["d_xn"].rearrange("k q t -> q k t")[:, :, c0:c0 + pass_t],
                 in_=xn_sb[:, :, 3:Te], then=s_dbg.inc(16))

        V.w(s_xq0, s_xq0.n)
        V.w(s_xq1, s_xq1.n)
        SC.w(s_xq0, s_xq0.n)
        SC.w(s_xq1, s_xq1.n)
        panel_rdy = {}
        mm_done = {}
        evac_done = {}
        zspill = {}

        def load_panel(m):
            buf = wp_a if m % 2 == 0 else wp_b
            sem = (s_wpa if m % 2 == 0 else s_wpb) if first else \
                  (s_wpc if m % 2 == 0 else s_wpd)
            wbsem = s_wba if m % 2 == 0 else s_wbb
            if m - 2 in mm_done:
                (G if first else SY).w(s_t, mm_done[m - 2])
            src_f = w_in_p[m].rearrange("(k p) c -> p k c", p=128)
            src_b = wbf_in[m].rearrange("(k p) c -> p k c", p=128)
            if first:
                G.w(wbsem, wbsem.n)
                G.c("dma_start", out=buf[:, 0:KH, :], in_=src_f,
                    then=sem.inc(16))
                panel_rdy[m] = (sem, sem.n)
                G.w(sem, sem.n)
                G.c("dma_start", out=src_b, in_=buf[:, 0:KH, :],
                    then=wbsem.inc(16))
            else:
                SY.w(s_wba, s_wba.n)
                SY.w(s_wbb, s_wbb.n)
                SY.w(s_wbc, s_wbc.n)
                SY.w(s_wbd, s_wbd.n)
                SY.c("dma_start", out=buf[:, 0:KH, :], in_=src_b,
                    then=sem.inc(16))
                panel_rdy[m] = (sem, sem.n)

        load_panel(0)
        for m in range(M_IN):
            if m + 1 < M_IN:
                load_panel(m + 1)
            buf = wp_a if m % 2 == 0 else wp_b
            mcols = 128 if m < M_IN - 1 else 64
            kind = "z" if m < KD else ("xbc" if m < KD + KC else "dt")
            qsem, qv = panel_rdy[m]
            T.w(qsem, qv)
            T.w(s_v, v_xn)
            T.w(s_sc, sc_xn)
            if m - 2 in evac_done:
                T.w(s_sc, evac_done[m - 2])
            bsel = (0, 1) if m % 2 == 0 else (2, 3)
            tbank = pbk[4] if m % 2 == 0 else pbk[5]
            sls = sl_x if kind == "xbc" else sl_a
            for si, (o, w) in enumerate(sls):
                if si < 2:
                    reg = pbk[bsel[si]][0:mcols, 0:w]
                else:
                    reg = tbank[0:mcols, 0:w]
                for k in range(KH):
                    T.c("matmul", reg, buf[:, k, 0:mcols],
                        xn_sb[:, k, o:o + w], start=(k == 0),
                        stop=(k == KH - 1),
                        then=(s_t.inc() if (k == KH - 1 and si == len(sls) - 1)
                              else ()))
            mm_done[m] = s_t.n

            SC.w(s_t, mm_done[m])
            for si, (o, w) in enumerate(sls):
                if si < 2:
                    reg = pbk[bsel[si]][0:mcols, 0:w]
                else:
                    reg = tbank[0:mcols, 0:w]
                if kind == "z":
                    zsem = s_zs0 if si == 0 else s_zs1
                    if zspill.get(si):
                        SC.w(zsem, zspill[si])
                    SC.c("activation", zstage[:, si, 0:w], reg, AF.Copy,
                         bias=0.0, then=s_sc.inc())
                    SY.w(s_sc, s_sc.n)
                    SY.c("dma_start", out=z_sp[m, :, c0 + o - 3:c0 + o - 3 + w],
                         in_=zstage[:, si, 0:w], then=zsem.inc(16))
                    zspill[si] = zsem.n
                elif kind == "xbc":
                    SC.c("activation", big1[:, m - KD, o:o + w], reg,
                         AF.Copy, bias=0.0,
                         then=(s_sc.inc() if si == len(sls) - 1 else ()))
                else:
                    SC.c("activation", dt_raw[0:64, o:o + w], reg, AF.Copy,
                         bias=0.0,
                         then=(s_sc.inc() if si == len(sls) - 1 else ()))
            evac_done[m] = s_sc.n
        v_inproj = s_v.n
        sc_inproj = s_sc.n

        if stop_after == "inproj":
            return

        conv_sc = {}
        for c in range(KC):
            if c % 2 == 0:
                acc = scr1
                V.w(s_sc, evac_done[KD + c])
                if c - 2 in conv_sc:
                    V.w(s_sc, conv_sc[c - 2])
                V.c("tensor_scalar_mul", acc[:, 0:pass_t],
                    big1[:, c, 0:pass_t], cw_sb[:, c * KCONV:c * KCONV + 1])
                for i in range(1, KCONV):
                    V.c("scalar_tensor_tensor", acc[:, 0:pass_t],
                        big1[:, c, i:i + pass_t],
                        cw_sb[:, c * KCONV + i:c * KCONV + i + 1],
                        acc[:, 0:pass_t], ALU.mult, ALU.add,
                        then=(s_v.inc() if i == KCONV - 1 else ()))
                SC.w(s_v, s_v.n)
            else:
                acc = scr3
                tmp = scr3[:, 512:512 + pass_t] if pass_t <= 512 else None
                G.w(s_sc, evac_done[KD + c])
                if c - 2 in conv_sc:
                    G.w(s_sc, conv_sc[c - 2])

                def cwb(i):
                    return cw_sb[:, c * KCONV + i:c * KCONV + i + 1]\
                        .broadcast_to([128, pass_t])
                G.c("tensor_tensor", acc[:, 0:pass_t],
                    big1[:, c, 0:pass_t], cwb(0), ALU.mult)
                for i in range(1, KCONV):
                    G.c("tensor_tensor", tmp, big1[:, c, i:i + pass_t],
                        cwb(i), ALU.mult)
                    G.c("tensor_tensor", acc[:, 0:pass_t], acc[:, 0:pass_t],
                        tmp, ALU.add,
                        then=(s_gc.inc() if i == KCONV - 1 else ()))
                SC.w(s_gc, s_gc.n)
            SC.c("activation", big1[:, c, 3:Te], acc[:, 0:pass_t], AF.Silu,
                 bias=cb_sb[:, c:c + 1], then=s_sc.inc())
            conv_sc[c] = s_sc.n
        sc_conv = s_sc.n

        SY.w(s_sc, sc_conv)
        for ci in range(nch_pass):
            SY.c("dma_start", out=c_sp[p * nch_pass + ci],
                 in_=big1[:, 33, 3 + ci * CH:3 + (ci + 1) * CH],
                 then=s_cs.inc(16))

        V.w(s_sc, sc_inproj)
        V.w(s_sc, sc_conv)
        V.c("tensor_scalar", scr1[0:64, 0:pass_t], dt_raw[0:64, 3:Te],
            smallv[:, 0:1], None, ALU.add, then=s_v.inc())
        SC.w(s_v, s_v.n)
        SC.c("activation", scr2[0:64, 0:pass_t], scr1[0:64, 0:pass_t], AF.Abs)
        SC.c("activation", scr3[0:64, 0:pass_t], scr2[0:64, 0:pass_t], AF.Exp,
             scale=cst[0:64, 2:3])
        SC.c("activation", scr2[0:64, 0:pass_t], scr3[0:64, 0:pass_t], AF.Ln,
             bias=cst[0:64, 1:2], then=s_sc.inc())
        V.w(s_sc, s_sc.n)
        V.c("tensor_scalar_max", dtA[0:64, 0:pass_t], scr1[0:64, 0:pass_t],
            0.0)
        V.c("tensor_tensor", dtA[0:64, 0:pass_t], dtA[0:64, 0:pass_t],
            scr2[0:64, 0:pass_t], ALU.add, then=s_v.inc())
        v_dt = s_v.n

        if stop_after == "conv":
            if "d_dt" in dout:
                SY.w(s_v, v_dt)
                SY.c("dma_start", out=dout["d_dt"][:, c0:c0 + pass_t],
                     in_=dtA[0:64, 0:pass_t], then=s_dbg.inc(16))
            if "d_xbc" in dout:
                SY.w(s_sc, sc_conv)
                SY.c("dma_start",
                     out=dout["d_xbc"].rearrange("k q t -> q k t")[:, :, c0:c0 + pass_t],
                     in_=big1[:, :, 3:Te], then=s_dbg.inc(16))
            return

        for ci in range(nch_pass):
            if p == 0:
                wprep_step(cdiv(128, nch_pass))
            gc = p * nch_pass + ci
            tc = 3 + ci * CH
            cc = ci * CH
            gcol = gc * CH
            t_chunk_start = s_t.n

            V.w(s_v, v_dt)
            V.w(s_t, t_chunk_start)
            V.c("tensor_scalar", adt_sb[0:64, :], dtA[0:64, cc:cc + CH],
                smallv[:, 2:3], None, ALU.mult)
            V.c("tensor_tensor_scan", acs_sb[0:64, gcol:gcol + CH],
                ones64[:, :], adt_sb[0:64, :], 0.0, ALU.mult, ALU.add)
            V.c("tensor_scalar", dcy_sb[0:64, :], acs_sb[0:64, gcol:gcol + CH],
                acs_sb[0:64, gcol + CH - 1:gcol + CH], -1.0,
                ALU.subtract, ALU.mult)
            V.c("tensor_copy", cend_sb[:, gc:gc + 1],
                acs_sb[0:64, gcol + CH - 1:gcol + CH])
            V.c("tensor_copy", dcb[:, 0:CH], dtA[0:64, cc:cc + CH],
                then=s_v.inc())
            v_sc1 = s_v.n
            SC.w(s_v, v_sc1)
            SC.c("activation", dcb[:, CH:2 * CH], dcy_sb[0:64, :], AF.Exp)
            SC.c("activation", smallv[:, 6:7],
                 acs_sb[0:64, gcol + CH - 1:gcol + CH], AF.Exp,
                 then=s_sc.inc())
            sc_sc1 = s_sc.n
            V.w(s_sc, sc_sc1)
            expand_hp(smallv[:, 6:7], dexp2_sb[:], 4)
            v_dexp2 = s_v.n

            T.w(s_v, v_sc1)
            T.c("transpose", pbk[5][:, 0:64], acs_sb[0:64, gcol:gcol + CH],
                c_idf[0:64, 0:64], then=s_t.inc())
            V.w(s_t, s_t.n)
            V.c("tensor_copy", acsT[:, 0:64], pbk[5][:, 0:64], then=s_v.inc())
            v_acsT = s_v.n

            T.w(s_sc, sc_conv)
            T.w(s_v, v_acsT)
            T.c("matmul", pbk[5][:, 128:256], big1[:, 32, tc:tc + CH],
                big1[:, 33, tc:tc + CH], start=True, stop=True,
                then=s_t.inc())
            V.w(s_t, s_t.n)
            V.c("tensor_tensor", gm_sb[:], pbk[5][:, 128:256], c_up01[:],
                ALU.mult, then=s_v.inc())
            v_gm = s_v.n

            T.w(s_sc, sc_sc1)
            T.c("transpose", pbf[0][:, 0:64], dcb[:, 0:CH],
                c_idb[0:64, 0:64])
            T.c("transpose", pbf[0][:, 128:192], dcb[:, CH:2 * CH],
                c_idb[0:64, 0:64])
            T.c("transpose", pbf[0][:, 256:384], big1[:, 32, tc:tc + CH],
                c_idb[:], then=s_t.inc())
            t_r0 = s_t.n
            V.w(s_t, t_r0)
            V.c("tensor_copy", tb1[:, 0:64], pbf[0][:, 0:64])
            V.c("tensor_copy", tb2[:, 0:64], pbf[0][:, 128:192])
            V.c("tensor_copy", bt_sb[:], pbf[0][:, 256:384], then=s_v.inc())
            v_r0 = s_v.n

            v_xtt = {}
            for r in range(4):
                bk = pbf[r % 2]
                if r >= 2:
                    T.w(s_v, v_xtt[r - 2])
                else:
                    T.w(s_v, v_r0)
                for j in range(8):
                    blk = r * 8 + j
                    T.c("transpose", bk[:, j * 128:(j + 1) * 128],
                        big1[:, blk, tc:tc + CH], c_idb[:],
                        then=(s_t.inc() if j == 7 else ()))
                t_r = s_t.n
                V.w(s_t, t_r)
                for gsub in range(2):
                    b0 = r * 8 + gsub * 4
                    src = bk[:, gsub * 512:(gsub + 1) * 512].rearrange(
                        "p (g o) -> p g o", g=8)
                    dst = xdtT[:, b0:b0 + 4, :].rearrange(
                        "p b (h o) -> p (b h) o", h=2)
                    mult = tb1[:, b0 * 2:b0 * 2 + 8].rearrange(
                        "p (h o) -> p h o", o=1).broadcast_to([128, 8, 64])
                    V.c("tensor_tensor", dst, src, mult, ALU.mult,
                        then=(s_v.inc() if gsub == 1 else ()))
                v_xtt[r] = s_v.n
            v_xdtT = s_v.n

            v_seg = {}
            sc_exp = {}
            t_bc_of = {}
            for g4 in range(16):
                h0 = g4 * 4
                slot = g4 % 2
                flsem = s_fl0 if slot == 0 else s_fl1
                foff = slot * 512
                if g4 >= 2:
                    G.w(s_t, t_bc_of[g4 - 2])
                G.w(s_v, v_acsT)
                G.c("dma_start", out=flat_sb[0:1, foff:foff + 512],
                    in_=acs_sb[h0:h0 + 4, gcol:gcol + CH], then=flsem.inc(16))
                T.w(flsem, flsem.n)
                if g4 >= 2:
                    T.w(s_v, v_seg[g4 - 2])
                T.c("matmul", pbk[slot][:, 0:512], ones_r[:],
                    flat_sb[0:1, foff:foff + 512], start=True, stop=True,
                    then=s_t.inc())
                t_bc = s_t.n
                t_bc_of[g4] = t_bc
                V.w(s_t, t_bc)
                if g4 >= 1:
                    V.w(s_sc, sc_exp.get(g4 - 1, 0))
                for j in range(4):
                    h = h0 + j
                    V.c("tensor_scalar", scr2[:, j * 128:(j + 1) * 128],
                        pbk[slot][:, j * 128:(j + 1) * 128],
                        acsT[:, h:h + 1], 0.0, ALU.subtract, ALU.min,
                        then=(s_v.inc() if j == 3 else ()))
                v_seg[g4] = s_v.n
                SC.w(s_v, v_seg[g4])
                SC.c("activation",
                     mp_sb[:, h0:h0 + 4, :].rearrange("p a b -> p (a b)"),
                     scr2[:, 0:512], AF.Exp, then=s_sc.inc())
                sc_exp[g4] = s_sc.n
                V.w(s_sc, sc_exp[g4])
                V.w(s_v, v_gm)
                V.c("tensor_tensor", mp_sb[:, h0:h0 + 4, :],
                    mp_sb[:, h0:h0 + 4, :],
                    gm_sb[:].rearrange("p (o t) -> p o t", o=1).broadcast_to(
                        [128, 4, CH]),
                    ALU.mult, then=s_v.inc())
            v_mp = s_v.n

            yd_done = {}
            v_yd = {}
            for grp in range(8):
                bank = pbk[2 + grp % 2]
                T.w(s_v, v_mp)
                T.w(s_v, v_xdtT)
                if grp >= 2:
                    T.w(s_v, v_yd[grp - 2])
                for slotj in range(4):
                    blk = grp * 4 + slotj
                    h_even, h_odd = 2 * blk, 2 * blk + 1
                    for par, h in ((0, h_even), (1, h_odd)):
                        lhs = xdtT[:, blk, par * 64:par * 64 + 64]
                        T.c("matmul", bank[par * 64:par * 64 + 64,
                                           slotj * 128:(slotj + 1) * 128],
                            lhs, mp_sb[:, h, :], start=True, stop=True,
                            tile_position=(0, par * 64),
                            then=(s_t.inc() if (slotj == 3 and par == 1)
                                  else ()))
                yd_done[grp] = s_t.n
                V.w(s_t, yd_done[grp])
                V.w(s_yst, s_yst.n)
                for slotj in range(4):
                    blk = grp * 4 + slotj
                    V.c("scalar_tensor_tensor",
                        ystage[:, (grp % 2) * 4 + slotj, :],
                        big1[:, blk, tc:tc + CH], dexp_sb[:, blk:blk + 1],
                        bank[:, slotj * 128:(slotj + 1) * 128],
                        ALU.mult, ALU.add,
                        then=(s_v.inc() if slotj == 3 else ()))
                v_yd[grp] = s_v.n
                if grp % 2 == 1:
                    b0 = (grp - 1) * 4
                    SY.w(s_v, v_yd[grp])
                    SY.c("dma_start",
                         out=yp_sp[b0:b0 + 8, :, gcol:gcol + CH].rearrange(
                             "b q t -> q b t"),
                         in_=ystage[:], then=s_yst.inc(16))

            V.w(s_t, yd_done[7])
            for g in range(8):
                b0 = g * 4
                dst = xdtT[:, b0:b0 + 4, :].rearrange(
                    "p b (h o) -> p (b h) o", h=2)
                mult = tb2[:, b0 * 2:b0 * 2 + 8].rearrange(
                    "p (h o) -> p h o", o=1).broadcast_to([128, 8, 64])
                V.c("tensor_tensor", dst, dst, mult, ALU.mult,
                    then=(s_v.inc() if g == 7 else ()))
            v_xdtd = s_v.n

            G.w(s_v, v_dexp2)
            G.c("dma_start", out=st_sp[gc], in_=s_state[:].rearrange(
                "p a b -> p (a b)"), then=s_stp.inc(16))
            cs_done = {}
            for grp in range(8):
                bank = pbk[4 + grp % 2]
                T.w(s_v, v_xdtd)
                T.w(s_v, v_r0)
                if grp == 0:
                    T.w(s_v, v_dexp2)
                if grp >= 2:
                    T.w(s_v, s_v.n)
                for slotj in range(4):
                    blk = grp * 4 + slotj
                    T.c("matmul", bank[:, slotj * 128:(slotj + 1) * 128],
                        xdtT[:, blk, :], bt_sb[:], start=True, stop=True,
                        then=(s_t.inc() if slotj == 3 else ()))
                cs_done[grp] = s_t.n
                V.w(s_t, cs_done[grp])
                V.w(s_stp, s_stp.n)
                for slotj in range(4):
                    blk = grp * 4 + slotj
                    V.c("scalar_tensor_tensor", s_state[:, blk, :],
                        s_state[:, blk, :], dexp2_sb[:, blk:blk + 1],
                        bank[:, slotj * 128:(slotj + 1) * 128],
                        ALU.mult, ALU.add,
                        then=(s_v.inc() if slotj == 3 else ()))
        v_ssda = s_v.n
        if p == 0:
            wprep_step(128)
            wprep_finalize()

        if stop_after == "ssda":
            if "d_dt" in dout:
                SY.w(s_v, v_dt)
                SY.c("dma_start", out=dout["d_dt"][:, c0:c0 + pass_t],
                     in_=dtA[0:64, 0:pass_t], then=s_dbg.inc(16))
            return

        scr1b = scr1[:].bitcast(F16)
        scr2b = scr2[:].bitcast(F16)
        V.w(s_yst, s_yst.n)
        V.w(s_zs0, s_zs0.n)
        V.w(s_zs1, s_zs1.n)
        V.w(s_xq0, s_xq0.n)
        V.w(s_xq1, s_xq1.n)
        V.w(s_cs, s_cs.n)
        SC.w(s_cs, s_cs.n)
        nfb = pass_t // fb

        def col_stats_to_row(src_scr, op, flat_off, v_ready):
            nj = fb // 128
            T.w(s_v, v_ready)
            for j in range(nj):
                T.c("transpose", pbk[5][:, j * 128:(j + 1) * 128],
                    src_scr[:, j * 128:(j + 1) * 128], c_idf[:],
                    then=(s_t.inc() if j == nj - 1 else ()))
            V.w(s_t, s_t.n)
            for j in range(nj):
                V.c("tensor_reduce", stat4[:, j:j + 1],
                    pbk[5][:, j * 128:(j + 1) * 128], AX.X, op,
                    then=(s_v.inc() if j == nj - 1 else ()))
            T.w(s_v, s_v.n)
            T.c("transpose", pbk[4][0:nj, 0:128], stat4[:, 0:nj],
                c_idf[:], then=s_t.inc())
            V.w(s_t, s_t.n)
            V.c("tensor_copy", stat4T[0:nj, :], pbk[4][0:nj, 0:128],
                then=s_v.inc())
            G.w(s_v, s_v.n)
            G.c("dma_start", out=flat_sb[0:1, flat_off:flat_off + fb],
                in_=stat4T[0:nj, :], then=s_gb.inc(16))

        def bcast_row(row_ap, slot, v_ready):
            T.w(s_v, v_ready)
            T.c("matmul", pbk[4][:, 0:fb], ones_r[:], row_ap,
                start=True, stop=True, then=s_t.inc())
            V.w(s_t, s_t.n)
            V.c("tensor_copy", ffnbc[:, slot, 0:fb], pbk[4][:, 0:fb],
                then=s_v.inc())

        for fbi in range(nfb):
            fbg = p * nfb + fbi
            fc = 3 + fbi * fb
            gf = c0 + fbi * fb

            nj2 = fb // 128
            jb = 8 + fbi * nj2
            T.w(s_v, v_ssda)
            T.c("transpose", pbk[4][0:nj2, 0:128], stat4[:, jb:jb + nj2],
                c_idf[:], then=s_t.inc())
            V.w(s_t, s_t.n)
            V.c("tensor_copy", stat4T[0:nj2, :], pbk[4][0:nj2, 0:128],
                then=s_v.inc())
            G.w(s_t, s_t.n)
            G.w(s_v, s_v.n)
            G.c("dma_start", out=flat_sb[0:1, 0:fb], in_=stat4T[0:nj2, :],
                then=s_gb.inc(16))
            V.w(s_gb, s_gb.n)
            V.c("tensor_scalar_max", flat_sb[0:1, 0:fb], flat_sb[0:1, 0:fb],
                EPS)
            V.c("tensor_scalar_mul", flat_sb[0:1, 0:fb], flat_sb[0:1, 0:fb],
                1.0 / 127.0)
            V.c("tensor_tensor", flat_sb[0:1, 0:fb], flat_sb[0:1, 0:fb],
                scr3[0:1, 1024:1025].broadcast_to([1, fb]), ALU.mult,
                then=s_v.inc())
            bcast_row(flat_sb[0:1, 0:fb], 1, s_v.n)
            v_bc01 = s_v.n

            SY.w(s_xq0, s_xq0.n)
            SY.w(s_xq1, s_xq1.n)
            SY.w(s_v, v_ssda)
            SY.w(s_t, s_t.n)
            SY.c("dma_start", out=xq_sb[:, :, 0:fb],
                 in_=xq_sp.rearrange("k q t -> q k t")[:, :, gf:gf + fb],
                 then=s_xql.inc(16))
            v_xq = s_xql.n
            if "d_xq" in dout:
                SY.w(s_xql, s_xql.n)
                SY.c("dma_start",
                     out=dout["d_xq"].rearrange("k q t -> q k t")[:, :, gf:gf + fb],
                     in_=xq_sb[:, :, 0:fb], then=s_dbg.inc(16))

            p_rdy = {}
            mmd = {}
            evd = {}

            arCf = arC[:]
            w2half = {"v": 0}

            def ffn_panel(mi, which):
                if which == "w2":
                    if fbg == 0:
                        buf = wp_a
                    else:
                        buf = wp_a if mi % 2 == 0 else wb2
                else:
                    if fbg == 0:
                        buf = wp_a
                    else:
                        buf = wp_a if mi % 2 == 0 else wp_b
                sem = s_wpc if mi % 2 == 0 else s_wpd
                wbsem = s_wbc if mi % 2 == 0 else s_wbd
                if which == "w1":
                    dep, raw, cache = KH, w1_p, wbf_1
                else:
                    dep, raw, cache = KF, w2_p, wbf_2
                idx = mi if which == "w1" else mi - KF
                if fbg == 0:
                    if mi - 1 in mmd:
                        SY.w(s_t, mmd[mi - 1])
                elif mi - 2 in mmd:
                    SY.w(s_t, mmd[mi - 2])
                src_f = raw[idx].rearrange("(k q) c -> q k c", q=128)
                src_b = cache[idx].rearrange("(k q) c -> q k c", q=128)
                if fbg > 0:
                    SY.w(s_wbc, s_wbc.n)
                    SY.w(s_wbd, s_wbd.n)
                    SY.w(s_gsp, s_gsp.n)
                    if which == "w2" and mi % 2 == 1:
                        SY.w(s_t, mmd.get(KF - 1, 0))
                        SY.w(s_v, s_v.n)
                    SY.c("dma_start", out=buf[:, 0:dep, :], in_=src_b,
                        then=sem.inc(16))
                    p_rdy[mi] = (sem, sem.n)
                    return
                tcol = 0 if which == "w1" else 2
                SY.w(s_wbc, s_wbc.n)
                SY.w(s_wbd, s_wbd.n)
                if which == "w1":
                    SY.w(s_v, s_v.n)
                bflat = buf[:].rearrange("q a b -> q (a b)")
                if which == "w1":
                    fst = bflat.bitcast(F32)[:, 2048:4096].rearrange(
                        "q (k c) -> q k c", c=128)
                    SY.c("dma_start", out=fst, in_=src_f, then=sem.inc(16))
                    V.w(sem, sem.n)
                    V.w(s_t, s_t.n)
                    fsl = bflat.bitcast(F32)[:, 2048:4096]
                    V.c("tensor_scalar", scr1b[:, 0:2048], fsl,
                        thr_sb[:, tcol:tcol + 1], None, ALU.is_gt)
                    V.c("tensor_scalar", scr2b[:, 0:2048], fsl,
                        thr_sb[:, tcol + 1:tcol + 2], None, ALU.is_lt)
                    V.c("tensor_tensor", bflat[:, 0:2048], scr1b[:, 0:2048],
                        scr2b[:, 0:2048], ALU.subtract, then=s_v.inc())
                else:
                    for hf in range(4):
                        slot2 = hf % 2
                        hsem = s_pre0 if slot2 == 0 else s_pre1
                        SY.w(s_v, w2half["v"])
                        SY.w(s_t, mmd.get(KF - 1, 0))
                        SY.w(s_dbg, s_dbg.n)
                        SY.c("dma_start",
                             out=arCf[:, slot2 * 2048:(slot2 + 1) * 2048].rearrange(
                                 "q (k c) -> q k c", c=128),
                             in_=src_f[:, hf * 16:(hf + 1) * 16, :],
                             then=hsem.inc(16))
                        V.w(hsem, hsem.n)
                        V.w(s_t, s_t.n)
                        fsl = arCf[:, slot2 * 2048:(slot2 + 1) * 2048]
                        V.c("tensor_scalar", scr1b[:, 0:2048], fsl,
                            thr_sb[:, tcol:tcol + 1], None, ALU.is_gt)
                        V.c("tensor_scalar", scr2b[:, 0:2048], fsl,
                            thr_sb[:, tcol + 1:tcol + 2], None, ALU.is_lt)
                        V.c("tensor_tensor",
                            bflat[:, hf * 2048:hf * 2048 + 2048],
                            scr1b[:, 0:2048], scr2b[:, 0:2048], ALU.subtract,
                            then=s_v.inc())
                        w2half["v"] = s_v.n
                p_rdy[mi] = ("v", s_v.n)
                SY.w(s_v, s_v.n)
                SY.c("dma_start", out=src_b, in_=buf[:, 0:dep, :],
                     then=wbsem.inc(16))

            t_sq_mm = {}
            if fbg > 0:
                ffn_panel(0, "w1")
            for m in range(KF):
                if fbg == 0:
                    ffn_panel(m, "w1")
                elif m + 1 < KF:
                    ffn_panel(m + 1, "w1")
                if fbg == 0:
                    buf = wp_a
                else:
                    buf = wp_a if m % 2 == 0 else wp_b
                bank = pbk[m % 2]
                sr, sv = p_rdy[m]
                if sr == "v":
                    T.w(s_v, sv)
                else:
                    T.w(sr, sv)
                T.w(s_xql, v_xq)
                T.w(s_v, v_bc01)
                if m - 2 in evd:
                    T.w(s_v, evd[m - 2])
                for k in range(KH):
                    T.c("matmul", bank[:, 0:fb], buf[:, k, 0:128],
                        xq_sb[:, k, :], start=(k == 0), stop=(k == KH - 1),
                        then=(s_t.inc() if k == KH - 1 else ()))
                mmd[m] = s_t.n
                V.w(s_t, mmd[m])
                V.w(s_sc, s_sc.n)
                V.c("tensor_tensor", ug_sb[:, 0:fb], bank[:, 0:fb],
                    ffnbc[:, 1, 0:fb], ALU.mult, then=s_v.inc())
                evd[m] = s_v.n
                SC.w(s_v, evd[m])
                SC.c("activation", h_sb[:, m, :], ug_sb[:, 0:fb], AF.Gelu,
                     bias=b1_sb[:, m:m + 1], then=s_sc.inc())
                V.w(s_sc, s_sc.n)
                if m > 0:
                    V.w(s_t, t_sq_mm[m - 1])
                V.c("tensor_tensor", sq_sb[:, 0:fb], h_sb[:, m, :],
                    h_sb[:, m, :], ALU.mult, then=s_v.inc())
                v_sq = s_v.n
                T.w(s_sc, s_sc.n)
                T.c("matmul", pbk[2][0:1, 0:fb], ones_cb[:], h_sb[:, m, :],
                    start=(m == 0), stop=(m == KF - 1))
                T.w(s_v, v_sq)
                T.c("matmul", pbk[3][0:1, 0:fb], ones_cbb[:], sq_sb[:, 0:fb],
                    start=(m == 0), stop=(m == KF - 1), then=s_t.inc())
                t_sq_mm[m] = s_t.n
            t_hstats = s_t.n

            V.w(s_t, t_hstats)
            V.c("tensor_scalar_mul", scr3[0:1, 0:fb], pbk[2][0:1, 0:fb],
                1.0 / FF)
            V.c("tensor_scalar_mul", scr3[0:1, 512:512 + fb],
                pbk[3][0:1, 0:fb], 1.0 / FF)
            V.c("tensor_tensor", dt_raw[0:1, 0:fb], scr3[0:1, 0:fb],
                scr3[0:1, 0:fb], ALU.mult)
            V.c("tensor_tensor", scr3[0:1, 512:512 + fb],
                scr3[0:1, 512:512 + fb], dt_raw[0:1, 0:fb],
                ALU.subtract, then=s_v.inc())
            SC.w(s_v, s_v.n)
            SC.c("activation", dt_raw[0:1, 0:fb], scr3[0:1, 512:512 + fb],
                 AF.Sqrt, bias=cst[0:1, 0:1], then=s_sc.inc())
            V.w(s_sc, s_sc.n)
            V.c("reciprocal", flat_sb[0:1, 512:512 + fb],
                dt_raw[0:1, 0:fb])
            V.c("tensor_tensor", scr3[0:1, 0:fb], scr3[0:1, 0:fb],
                flat_sb[0:1, 512:512 + fb], ALU.mult)
            V.c("tensor_scalar_mul", scr3[0:1, 0:fb], scr3[0:1, 0:fb],
                -1.0, then=s_v.inc())
            bcast_row(flat_sb[0:1, 512:512 + fb], 2, s_v.n)
            bcast_row(scr3[0:1, 0:fb], 3, s_v.n)
            v_bc23 = s_v.n

            for m in range(KF):
                stg = scr1b if m % 2 == 0 else scr2b
                V.c("tensor_tensor", stg[:, 0:fb], h_sb[:, m, :],
                    ffnbc[:, 2, 0:fb], ALU.mult)
                V.c("tensor_tensor", h_sb[:, m, :], stg[:, 0:fb],
                    ffnbc[:, 3, 0:fb], ALU.add, then=s_v.inc())
            v_h2 = s_v.n
            SC.w(s_v, v_h2)
            for m in range(KF):
                SC.c("activation", h_sb[:, m, :], h_sb[:, m, :],
                     AF.Identity, bias=be_sb[:, m:m + 1],
                     scale=ga_sb[:, m:m + 1],
                     then=(s_sc.inc() if m == KF - 1 else ()))
            sc_hn = s_sc.n
            V.w(s_sc, sc_hn)
            if "d_h" in dout:
                SY.w(s_sc, sc_hn)
                SY.c("dma_start",
                     out=dout["d_h"].rearrange("k q t -> q k t")[:, :, gf:gf + fb],
                     in_=h_sb[:], then=s_dbg.inc(16))

            hsw = h_sb[:, :, 0:fb].rearrange("q k t -> q t k")
            V.w(s_gc, s_gc.n)
            V.c("tensor_reduce", scr3[:, 0:fb], hsw, AX.X, ALU.max)
            V.c("tensor_reduce", scr2[:, 0:fb], hsw, AX.X, ALU.min)
            V.c("scalar_tensor_tensor", scr3[:, 0:fb], scr2[:, 0:fb],
                -1.0, scr3[:, 0:fb], ALU.mult, ALU.max, then=s_v.inc())
            col_stats_to_row(scr3, ALU.max, 0, s_v.n)
            V.w(s_gb, s_gb.n)
            V.c("tensor_scalar_max", scr3[0:1, 0:fb], flat_sb[0:1, 0:fb], EPS)
            V.c("reciprocal", scr3[0:1, 512:512 + fb], scr3[0:1, 0:fb])
            V.c("tensor_scalar_mul", flat_sb[0:1, 512:512 + fb],
                scr3[0:1, 512:512 + fb], 127.0)
            V.c("tensor_scalar_mul", scr3[0:1, 0:fb], scr3[0:1, 0:fb],
                1.0 / 127.0)
            V.c("tensor_tensor", flat_sb[0:1, 0:fb],
                scr3[0:1, 0:fb],
                scr3[0:1, 1025:1026].broadcast_to([1, fb]), ALU.mult,
                then=s_v.inc())
            bcast_row(flat_sb[0:1, 512:512 + fb], 4, s_v.n)
            bcast_row(flat_sb[0:1, 0:fb], 5, s_v.n)

            if "d_h" in dout:
                V.w(s_dbg, s_dbg.n)
            for m in range(KF):
                stg = scr1 if m % 2 == 0 else scr2
                V.c("tensor_tensor", stg[:, 0:fb], h_sb[:, m, :],
                    ffnbc[:, 4, 0:fb], ALU.mult)
                V.c("tensor_scalar", h_sb[:, m, :], stg[:, 0:fb], MAGIC,
                    -MAGIC, ALU.add, ALU.add,
                    then=(s_v.inc() if m == KF - 1 else ()))
            v_hq = s_v.n

            evd2 = {}
            if fbg > 0:
                ffn_panel(KF + 0, "w2")
            for m in range(KH):
                if fbg == 0:
                    ffn_panel(KF + m, "w2")
                elif m + 1 < KH:
                    ffn_panel(KF + m + 1, "w2")
                if fbg == 0:
                    buf = wp_a
                else:
                    buf = wp_a if m % 2 == 0 else wb2
                bank = pbk[m % 2]
                sr, sv = p_rdy[KF + m]
                if sr == "v":
                    T.w(s_v, sv)
                else:
                    T.w(sr, sv)
                T.w(s_v, v_hq)
                if m - 2 >= 0 and (m - 2) in evd2:
                    T.w(s_v, evd2[m - 2])
                for k in range(KF):
                    T.c("matmul", bank[:, 0:fb], buf[:, k, 0:128],
                        h_sb[:, k, :], start=(k == 0), stop=(k == KF - 1),
                        then=(s_t.inc() if k == KF - 1 else ()))
                mmd[KF + m] = s_t.n
                V.w(s_t, mmd[KF + m])
                V.w(s_sc, s_sc.n)
                V.c("tensor_tensor", ug_sb[:, 0:fb], bank[:, 0:fb],
                    ffnbc[:, 5, 0:fb], ALU.mult, then=s_v.inc())
                evd2[m] = s_v.n
                SC.w(s_v, evd2[m])
                SC.w(s_gsp, s_gsp.n)
                SC.c("activation", gstage[:, m % 2, 0:fb], ug_sb[:, 0:fb],
                     AF.Silu, bias=b2_sb[:, m:m + 1], then=s_sc.inc())
                SY.w(s_sc, s_sc.n)
                SY.c("dma_start", out=g_sp[m, :, gf:gf + fb],
                     in_=gstage[:, m % 2, 0:fb], then=s_gsp.inc(16))

        if "d_dt" in dout:
            SY.w(s_v, v_dt)
            SY.c("dma_start", out=dout["d_dt"][:, c0:c0 + pass_t],
                 in_=dtA[0:64, 0:pass_t], then=s_dbg.inc(16))
        if "d_xbc" in dout:
            SY.w(s_sc, sc_conv)
            SY.c("dma_start",
                 out=dout["d_xbc"].rearrange("k q t -> q k t")[:, :, c0:c0 + pass_t],
                 in_=big1[:, :, 3:Te], then=s_dbg.inc(16))

    gate = {}
    for p in range(n_pass):
        stage_a(p)
        gate = {"t": s_t.n, "v": s_v.n, "sc": s_sc.n}
        if stop_after == "inproj" and p == 0:
            break

    if stop_after not in ("inproj", "conv", "ssda", "ffn"):
        scr1b = scr1[:].bitcast(BF16)
        V.w(s_sc, s_sc.n)
        V.c("tensor_reduce", smallv[:, 5:6], cend_sb[:, 0:n_chunks], AX.X,
            ALU.add)
        V.c("tensor_tensor_scan", cend_sb[:, 17:17 + n_chunks],
            ones64[:, 0:n_chunks], cend_sb[:, 0:n_chunks], 0.0,
            ALU.mult, ALU.add)
        V.c("memset", cend_sb[:, 16:17], 0.0)
        V.c("memset", scr3[:, 0:2], 0.0)
        V.c("tensor_copy", scr3[0:64, 0:1], smallv[:, 5:6], then=s_v.inc())
        v_cc = s_v.n
        G.w(s_v, v_cc)
        G.c("dma_start", out=cc_in[:, 0:DIN],
            in_=s_state[:].rearrange("p a b -> p (a b)"), then=s_gb.inc(16))
        G.c("dma_start", out=cc_in[:, DIN:DIN + 2], in_=scr3[:, 0:2],
            then=s_gb.inc(16))
        G.w(s_gb, s_gb.n)
        G.c("collective_compute", "AllGather", ALU.bypass,
            replica_groups=(rg if rg is not None else [[0, 1, 2, 3], [4, 5, 6, 7]]),
            ins=[cc_in[:]], outs=[cc_out[:]], then=s_cc.inc())
        cc_done = s_cc.n

        V.w(s_cc, cc_done)
        V.c("memset", s_state[:], 0.0, then=s_v.inc())
        for j in range(4):
            SY.w(s_cc, cc_done)
            if j >= 1:
                SY.w(s_v, s_v.n)
            SY.w(s_dbg, s_dbg.n)
            SY.c("dma_start", out=ccld[:], in_=cc_out[j], then=s_ld2.inc(16))
            V.w(s_ld2, s_ld2.n)
            V.c("tensor_scalar", smallv[:, 5:6], ccld[0:64, DIN:DIN + 1],
                dmask_sb[:, j:j + 1], None, ALU.mult, then=s_v.inc())
            SC.w(s_v, s_v.n)
            SC.c("activation", smallv[:, 6:7], smallv[:, 5:6], AF.Exp,
                 then=s_sc.inc())
            V.w(s_sc, s_sc.n)
            expand_hp(smallv[:, 6:7], dexp2_sb[:], 4)
            for g8 in range(8):
                V.c("tensor_scalar",
                    scr1[:, 0:512], ccld[:, g8 * 512:(g8 + 1) * 512],
                    amask_sb[:, j:j + 1], None, ALU.mult)
                for s4 in range(4):
                    blk = g8 * 4 + s4
                    V.c("scalar_tensor_tensor", s_state[:, blk, :],
                        s_state[:, blk, :], dexp2_sb[:, blk:blk + 1],
                        scr1[:, s4 * 128:(s4 + 1) * 128], ALU.mult, ALU.add,
                        then=(s_v.inc() if s4 == 3 else ()))
        v_sin = s_v.n
        if "d_sin" in dout:
            SY.w(s_v, v_sin)
            SY.c("dma_start", out=dout["d_sin"][:],
                 in_=s_state[:].rearrange("p a b -> p (a b)"),
                 then=s_dbg.inc(16))

        wo_rdy = {}
        wo_mmd = {}

        def wout_panel(m, firstb):
            buf = wp_a if m % 2 == 0 else wp_b
            wbsem = s_wba if m % 2 == 0 else s_wbb
            src_f = wout_p[m].rearrange("(k q) c -> q k c", q=128)
            src_b = wbf_o[m].rearrange("(k q) c -> q k c", q=128)
            if m - 2 in wo_mmd:
                (G if firstb else SY).w(s_t, wo_mmd[m - 2])
            else:
                (G if firstb else SY).w(s_t, s_t.n)
            if firstb:
                G.w(wbsem, wbsem.n)
                G.w(s_wbc, s_wbc.n)
                G.w(s_wbd, s_wbd.n)
                G.w(s_gsp, s_gsp.n)
                G.c("dma_start", out=buf[:, 0:KD, :], in_=src_f,
                    then=s_wpa.inc(16) if m % 2 == 0 else s_wpb.inc(16))
                sem = s_wpa if m % 2 == 0 else s_wpb
                wo_rdy[m] = (sem, sem.n)
                G.w(sem, sem.n)
                G.c("dma_start", out=src_b, in_=buf[:, 0:KD, :],
                    then=wbsem.inc(16))
            else:
                SY.w(s_wba, s_wba.n)
                SY.w(s_wbb, s_wbb.n)
                SY.w(s_wbc, s_wbc.n)
                SY.w(s_wbd, s_wbd.n)
                SY.w(s_gsp, s_gsp.n)
                sem = s_wpc if m % 2 == 0 else s_wpd
                SY.c("dma_start", out=buf[:, 0:KD, :], in_=src_b,
                    then=sem.inc(16))
                wo_rdy[m] = (sem, sem.n)

        ld_free = {"yp": v_sin, "z": v_sin, "st": v_sin}
        for p2 in range(n_pass):
            c0 = p2 * pass_t
            if "d_ynorm" in dout:
                V.w(s_dbg, s_dbg.n)
            for ci in range(nch_pass):
                gc = p2 * nch_pass + ci
                gcol = gc * CH
                SY.w(s_v, ld_free["st"])
                SY.w(s_stp, s_stp.n)
                SY.c("dma_start", out=pb_ld[:],
                     in_=st_sp[gc].rearrange("q (a b) -> q a b", a=KD),
                     then=s_ld2.inc(16))
                r_st = s_ld2.n
                SY.w(s_v, ld_free["z"])
                SY.w(s_sc, ld_free.get("zsc", 0))
                SY.c("dma_start", out=pb_ld2[:],
                     in_=z_sp.rearrange("a q t -> q a t")[:, :, gcol:gcol + CH],
                     then=s_zs0.inc(16))
                r_z = s_zs0.n
                SY.w(s_v, ld_free["yp"])
                SY.c("dma_start", out=pb_ld3[:],
                     in_=yp_sp.rearrange("a q t -> q a t")[:, :, gcol:gcol + CH],
                     then=s_yst.inc(16))
                r_yp = s_yst.n
                SY.w(s_t, ld_free.get("ct", 0))
                SY.w(s_cs, s_cs.n)
                SY.c("dma_start", out=c_ld[:], in_=c_sp[gc],
                     then=s_cs.inc(16))
                r_c = s_cs.n

                V.w(s_v, v_sin)
                V.c("tensor_copy", smallv[:, 5:6], cend_sb[:, 16 + gc:17 + gc],
                    then=s_v.inc())
                SC.w(s_v, s_v.n)
                SC.c("activation", smallv[:, 6:7], smallv[:, 5:6], AF.Exp,
                     then=s_sc.inc())
                V.w(s_sc, s_sc.n)
                expand_hp(smallv[:, 6:7], dexp2_sb[:], 4)

                V.w(s_ld2, r_st)
                for blk in range(KD):
                    V.c("scalar_tensor_tensor", stot[:, blk, :],
                        s_state[:, blk, :], dexp2_sb[:, blk:blk + 1],
                        pb_ld[:, blk, :], ALU.mult, ALU.add,
                        then=(s_v.inc() if blk == KD - 1 else ()))
                v_stot = s_v.n

                for tg in range(8):
                    bank = pbk[4 + tg % 2]
                    T.w(s_v, v_stot)
                    if tg >= 2:
                        T.w(s_v, s_v.n)
                    for s4 in range(4):
                        blk = tg * 4 + s4
                        T.c("transpose", bank[:, s4 * 128:(s4 + 1) * 128],
                            stot[:, blk, :], c_idf[:],
                            then=(s_t.inc() if s4 == 3 else ()))
                    V.w(s_t, s_t.n)
                    for s4 in range(4):
                        blk = tg * 4 + s4
                        V.c("tensor_copy", stotT[:, blk, :],
                            bank[:, s4 * 128:(s4 + 1) * 128],
                            then=(s_v.inc() if s4 == 3 else ()))
                v_stT = s_v.n

                SC.w(s_v, v_stT)
                SC.c("activation", dcb[:, 0:CH], acs_sb[0:64, gcol:gcol + CH],
                     AF.Exp, then=s_sc.inc())
                sc_ea = s_sc.n

                v_grp = 0
                for g8 in range(8):
                    b_ea = pbk[2]
                    b_yo = pbk[3]
                    T.w(s_sc, sc_ea)
                    T.w(s_v, v_stT)
                    T.w(s_cs, r_c)
                    if g8 >= 1:
                        T.w(s_v, v_grp)
                    for s4 in range(4):
                        blk = g8 * 4 + s4
                        T.c("matmul", b_ea[:, s4 * 128:(s4 + 1) * 128],
                            c_hsel[:, blk * 128:(blk + 1) * 128],
                            dcb[:, 0:CH], start=True, stop=True)
                    for s4 in range(4):
                        blk = g8 * 4 + s4
                        T.c("matmul", b_yo[:, s4 * 128:(s4 + 1) * 128],
                            stotT[:, blk, :], c_ld[:],
                            start=True, stop=True,
                            then=(s_t.inc() if s4 == 3 else ()))
                    t_grp = s_t.n
                    V.w(s_t, t_grp)
                    V.w(s_yst, r_yp)
                    V.w(s_zs0, r_z)
                    V.c("tensor_copy", ea2_sb[:, 0:512], b_ea[:, 0:512],
                        then=s_v.inc())
                    SC.w(s_v, s_v.n)
                    for s4 in range(4):
                        blk = g8 * 4 + s4
                        SC.c("activation", pb_ld2[:, blk, :],
                             pb_ld2[:, blk, :], AF.Silu,
                             then=(s_sc.inc() if s4 == 3 else ()))
                    sc_z = s_sc.n
                    V.c("tensor_tensor", scr1[:, 0:512],
                        b_yo[:, 0:512],
                        ea2_sb[:, 0:512], ALU.mult)
                    V.c("tensor_tensor", scr1[:, 0:512], scr1[:, 0:512],
                        pb_ld3[:, g8 * 4:(g8 + 1) * 4, :].rearrange(
                            "q a b -> q (a b)"), ALU.add)
                    V.w(s_sc, sc_z)
                    V.c("tensor_tensor",
                        pb_ld3[:, g8 * 4:(g8 + 1) * 4, :].rearrange(
                            "q a b -> q (a b)"),
                        scr1[:, 0:512],
                        pb_ld2[:, g8 * 4:(g8 + 1) * 4, :].rearrange(
                            "q a b -> q (a b)"), ALU.mult)
                    V.c("tensor_tensor", scr1b[:, 0:512],
                        pb_ld3[:, g8 * 4:(g8 + 1) * 4, :].rearrange(
                            "q a b -> q (a b)"),
                        pb_ld3[:, g8 * 4:(g8 + 1) * 4, :].rearrange(
                            "q a b -> q (a b)"), ALU.mult,
                        then=s_v.inc())
                    v_grp = s_v.n
                    T.w(s_v, v_grp)
                    for s4 in range(4):
                        T.c("matmul", pbk[0][0:1, 0:CH], ones_cbb[:],
                            scr1b[:, s4 * 128:(s4 + 1) * 128],
                            start=(g8 == 0 and s4 == 0),
                            stop=(g8 == 7 and s4 == 3),
                            then=(s_t.inc() if (g8 == 7 and s4 == 3) else ()))
                t_ssq = s_t.n
                ld_free["z"] = s_v.n
                ld_free["zsc"] = s_sc.n
                ld_free["st"] = v_stot
                ld_free["ct"] = s_t.n

                V.w(s_t, t_ssq)
                V.c("tensor_scalar_mul", scr3[0:1, 0:CH], pbk[0][0:1, 0:CH],
                    1.0 / DIN, then=s_v.inc())
                SC.w(s_v, s_v.n)
                SC.c("activation", scr3[0:1, 256:256 + CH], scr3[0:1, 0:CH],
                     AF.Sqrt, bias=cst[0:1, 0:1], then=s_sc.inc())
                V.w(s_sc, s_sc.n)
                V.c("reciprocal", scr3[0:1, 0:CH], scr3[0:1, 256:256 + CH],
                    then=s_v.inc())
                T.w(s_v, s_v.n)
                T.c("matmul", pbk[1][:, 0:CH], ones_r[:], scr3[0:1, 0:CH],
                    start=True, stop=True, then=s_t.inc())
                t_rs = s_t.n
                V.w(s_t, t_rs)
                for blk in range(KD):
                    V.c("scalar_tensor_tensor", ynb[:, blk, ci * CH:(ci + 1) * CH],
                        pb_ld3[:, blk, :], nw_sb[:, blk:blk + 1],
                        pbk[1][:, 0:CH], ALU.mult, ALU.mult,
                        then=(s_v.inc() if blk == KD - 1 else ()))
                ld_free["yp"] = s_v.n
            v_ynb = s_v.n
            if "d_ynorm" in dout:
                SY.w(s_v, v_ynb)
                SY.c("dma_start",
                     out=dout["d_ynorm"].rearrange("k q t -> q k t")[:, :, c0:c0 + pass_t],
                     in_=ynb[:], then=s_dbg.inc(16))

            firstb = (p2 == 0)
            SY.w(s_dbg, s_dbg.n)
            SY.w(s_gsp, s_gsp.n)
            wout_panel(0, firstb)
            ev_out = {}
            for m in range(KH):
                if m + 1 < KH:
                    wout_panel(m + 1, firstb)
                buf = wp_a if m % 2 == 0 else wp_b
                sem, sv = wo_rdy[m]
                nslc = nslices(pass_t, 512)
                for nb, (oo, wnb) in enumerate(nslc):
                    bank = pbk[(m + nb) % 2]
                    T.w(sem, sv)
                    T.w(s_v, v_ynb)
                    prev = (m * len(nslc) + nb) - 2
                    if prev >= 0 and prev in ev_out:
                        T.w(s_v, ev_out[prev])
                    for k in range(KD):
                        T.c("matmul", bank[:, 0:wnb], buf[:, k, 0:128],
                            ynb[:, k, oo:oo + wnb], start=(k == 0),
                            stop=(k == KD - 1),
                            then=(s_t.inc() if k == KD - 1 else ()))
                    wo_mmd[m] = s_t.n
                    SY.w(s_v, ev_out.get(prev, 0))
                    SY.c("dma_start", out=gxl[:, m % 2, 0:wnb],
                         in_=g_sp[m, :, c0 + oo:c0 + oo + wnb],
                         then=s_gl.inc(16))
                    r_g = s_gl.n
                    SY.c("dma_start", out=outst[:, m % 2, 0:wnb],
                         in_=x_fm[m * 128:(m + 1) * 128,
                                  3 + c0 + oo:3 + c0 + oo + wnb],
                         then=s_x.inc(16))
                    r_x = s_x.n
                    V.w(s_t, wo_mmd[m])
                    V.w(s_gl, r_g)
                    V.w(s_x, r_x)
                    V.w(s_o, s_o.n)
                    V.c("tensor_tensor", scr1[:, 0:wnb], bank[:, 0:wnb],
                        gxl[:, m % 2, 0:wnb], ALU.mult)
                    V.c("tensor_tensor", scr2[:, 0:wnb], scr1[:, 0:wnb],
                        outst[:, m % 2, 0:wnb], ALU.add, then=s_v.inc())
                    ev_out[m * len(nslc) + nb] = s_v.n
                    SY.w(s_v, s_v.n)
                    SY.c("dma_start",
                         out=out_fm[m * 128:(m + 1) * 128, c0 + oo:c0 + oo + wnb],
                         in_=scr2[:, 0:wnb], then=s_o.inc(16))

    if "d_z" in dout:
        zt = scr1[:].bitcast(F16)
        SY.w(s_zs0, s_zs0.n)
        SY.w(s_zs1, s_zs1.n)
        for kk in range(KD):
            SY.c("dma_start", out=zt[:, 0:t_core], in_=z_sp[kk],
                 then=s_dbg.inc(16))
            SY.w(s_dbg, s_dbg.n)
            SY.c("dma_start", out=dout["d_z"][kk], in_=zt[:, 0:t_core],
                 then=s_dbg.inc(16))
            SY.w(s_dbg, s_dbg.n)

    if "d_g" in dout:
        zt3 = scr1[:].bitcast(F16)
        SY.w(s_gsp, s_gsp.n)
        for kk in range(KH):
            SY.c("dma_start", out=zt3[:, 0:t_core], in_=g_sp[kk],
                 then=s_dbg.inc(16))
            SY.w(s_dbg, s_dbg.n)
            SY.c("dma_start", out=dout["d_g"][kk], in_=zt3[:, 0:t_core],
                 then=s_dbg.inc(16))
            SY.w(s_dbg, s_dbg.n)

    if "d_acs" in dout:
        SY.w(s_v, s_v.n)
        SY.c("dma_start", out=dout["d_acs"][:, :], in_=acs_sb[0:64, :],
             then=s_dbg.inc(16))
    if "d_state" in dout:
        G.w(s_v, s_v.n)
        G.c("dma_start", out=dout["d_state"][:, 0:DIN],
            in_=s_state[:].rearrange("p a b -> p (a b)"), then=s_gb.inc(16))
        V.c("memset", scr3[:, 0:2], 0.0, then=s_v.inc())
        SY.w(s_v, s_v.n)
        SY.c("dma_start", out=dout["d_state"][:, DIN:DIN + 2],
             in_=scr3[:, 0:2], then=s_dbg.inc(16))
    if "d_yp" in dout:
        zt2 = scr1[:].bitcast(F16)
        SY.w(s_yst, s_yst.n)
        for kk in range(KD):
            SY.c("dma_start", out=zt2[:, 0:t_core], in_=yp_sp[kk],
                 then=s_dbg.inc(16))
            SY.w(s_dbg, s_dbg.n)
            SY.c("dma_start", out=dout["d_yp"][kk], in_=zt2[:, 0:t_core],
                 then=s_dbg.inc(16))
            SY.w(s_dbg, s_dbg.n)

    if stop_after in ("inproj", "conv", "ssda", "ffn"):
        V.c("memset", scr2[:, 0:512], 0.0, then=s_v.inc())
        SY.w(s_v, s_v.n)
        SY.w(s_dbg, s_dbg.n)
        for k in range(KH):
            for o, w in nslices(t_core):
                SY.c("dma_start",
                     out=out_fm.rearrange("(k q) t -> q k t", q=128)[:, k, o:o + w],
                     in_=scr2[:, 0:w], then=s_o.inc(16))
    SY.w(s_o, s_o.n)
    SY.w(s_dbg, s_dbg.n)
    G.w(s_gb, s_gb.n)

    with nc.Block() as block:
        block.tensor(lambda e: T.replay(e))
        block.vector(lambda e: V.replay(e))
        block.scalar(lambda e: SC.replay(e))
        block.gpsimd(lambda e: G.replay(e))
        block.sync(lambda e: SY.replay(e))

    ctx.close()
    return nc



def stage_weight_panels(w):
    outF, inF = w.shape
    n_p = cdiv(outF, 128)
    pad = n_p * 128 - outF
    wt = np.ascontiguousarray(w.T)
    if pad:
        wt = np.concatenate([wt, np.zeros((inF, pad), w.dtype)], 1)
    return np.ascontiguousarray(wt.reshape(inF, n_p, 128).transpose(1, 0, 2))


def stage_all(inputs, t_core=2048, n_cores=8):
    import ml_dtypes
    x = np.asarray(inputs["L0_attn_hidden_states"])
    B, L, D = x.shape
    segs = n_cores // B
    sh = {}
    sh["w_in_p"] = stage_weight_panels(np.asarray(inputs["L0_attn_in_proj_w"]))
    sh["w1_p"] = stage_weight_panels(np.asarray(inputs["L0_ffn_w1"]))
    sh["w2_p"] = stage_weight_panels(np.asarray(inputs["L0_ffn_w2"]))
    sh["wout_p"] = stage_weight_panels(np.asarray(inputs["L0_attn_out_proj_w"]))
    cw = np.asarray(inputs["L0_attn_conv_w"])
    sh["conv_w"] = np.ascontiguousarray(
        cw.reshape(KC, 128, KCONV).transpose(1, 0, 2)).reshape(128, KC * KCONV)
    sh["conv_b"] = np.ascontiguousarray(
        np.asarray(inputs["L0_attn_conv_b"]).reshape(KC, 128).T)
    sh["dtb"] = np.asarray(inputs["L0_attn_dt_bias"]).reshape(NH, 1).astype(np.float32)
    sh["alog"] = np.asarray(inputs["L0_attn_A_log"]).reshape(NH, 1).astype(np.float32)
    sh["dvec"] = np.asarray(inputs["L0_attn_D"]).reshape(NH, 1).astype(np.float32)
    sh["nw"] = np.ascontiguousarray(
        np.asarray(inputs["L0_attn_norm_w"]).reshape(KD, 128).T)
    sh["b1"] = np.ascontiguousarray(
        np.asarray(inputs["L0_ffn_b1"]).reshape(KF, 128).T)
    sh["ga"] = np.ascontiguousarray(
        np.asarray(inputs["L0_ffn_gamma"]).reshape(KF, 128).T)
    sh["be"] = np.ascontiguousarray(
        np.asarray(inputs["L0_ffn_beta"]).reshape(KF, 128).T)
    sh["b2"] = np.ascontiguousarray(
        np.asarray(inputs["L0_ffn_b2"]).reshape(KH, 128).T)
    sh["idf"] = np.eye(128, dtype=np.float32)
    sh["idb"] = np.eye(128, dtype=ml_dtypes.bfloat16)
    sh["up01"] = np.triu(np.ones((CH, CH))).astype(ml_dtypes.bfloat16)
    hsel = np.zeros((NH, DIN), dtype=ml_dtypes.bfloat16)
    for h in range(NH):
        hsel[h, h * HDIM:(h + 1) * HDIM] = 1
    sh["hsel"] = hsel
    psel = np.zeros((NH, 128), dtype=ml_dtypes.bfloat16)
    for h in range(NH):
        psel[h, (h % 2) * 64:(h % 2) * 64 + 64] = 1
    sh["psel"] = psel
    bsel = np.zeros((NH, KD), dtype=ml_dtypes.bfloat16)
    for h in range(NH):
        bsel[h, h // 2] = 1
    sh["bsel"] = bsel

    cores = []
    for c in range(n_cores):
        b, q = divmod(c, segs)
        t0 = q * t_core
        xs = x[b, max(0, t0 - 3):t0 + t_core]
        if t0 == 0:
            xs = np.concatenate([np.zeros((3, D), x.dtype), xs], 0)
        d = dict(sh)
        d["x_fm"] = np.ascontiguousarray(xs.T)
        am = np.zeros((128, 4), np.float32)
        dm = np.zeros((NH, 4), np.float32)
        am[:, :q] = 1.0
        dm[:, :q] = 1.0
        d["amask"] = am
        d["dmask"] = dm
        cores.append(d)
    return cores


def assemble_output(results, t_core=2048, n_cores=8):
    B, L, D = 2, 8192, 2048
    segs = n_cores // B
    out = np.empty((B, L, D), np.float32)
    for c in range(n_cores):
        b, q = divmod(c, segs)
        out[b, q * t_core:(q + 1) * t_core] = results[c]["out_fm"].T
    return out


def kernel(**inputs):
    t_core = 2048
    cores = stage_all(inputs, t_core=t_core, n_cores=8)
    nc = build_program(t_core=t_core, pass_t=512, fb=512)
    res = run_bass_kernel_spmd(nc, cores, core_ids=list(range(8)))
    return assemble_output(res.results, t_core=t_core, n_cores=8)


# version 08
# speedup vs baseline: 1.0081x; 1.0081x over previous
import numpy as np
import concourse.bass as bass
import concourse.mybir as mybir
from concourse.bass_utils import run_bass_kernel_spmd

F32 = mybir.dt.float32
BF16 = mybir.dt.bfloat16
F16 = mybir.dt.float16
AF = mybir.ActivationFunctionType
ALU = mybir.AluOpType
AX = mybir.AxisListType

HID, DIN, DSTATE = 2048, 4096, 128
CONVD = DIN + 2 * DSTATE
NH, HDIM, FF, KCONV = 64, 64, 8192, 4
EPS = 1e-5
MAGIC = 12582912.0
CH = 128
KH, KD, KC, KF, M_IN = 16, 32, 34, 64, 67


def cdiv(a, b):
    return (a + b - 1) // b


def nslices(total, step=512, off=0):
    out = []
    o = 0
    while o < total:
        out.append((off + o, min(step, total - o)))
        o += step
    return out


class Sem:
    def __init__(self, h):
        self.h = h
        self.n = 0

    def inc(self, k=1):
        self.n += k
        return ((self, k),)


_ESZ = {"dt.float32": 4, "dt.bfloat16": 2, "dt.float16": 2, "dt.int32": 4,
        "dt.uint8": 1, "dt.int8": 1, "dt.float8e4": 1, "dt.uint32": 4}


def _ap_span(a):
    try:
        name = a.tensor.name
        pairs = list(a.ap)
    except AttributeError:
        return None
    esz = _ESZ.get(str(a.dtype), 4)
    ext = 1
    if len(pairs) > 1:
        ext = sum(s * (n - 1) for s, n in pairs[1:]) + 1
    off = a.offset
    return (name, off * esz, (off + ext) * esz)


def _overlap(fp, fps):
    for name, lo, hi in fps:
        if fp[0] == name and fp[1] < hi and fp[2] > lo:
            return True
    return False


class EngQ:
    def __init__(self, auto_drain=False):
        self.ops = []
        self.auto = auto_drain
        self.pend = []
        self.pendr = []

    def c(self, method, *a, then=(), **kw):
        if self.auto:
            wr, rd = [], []
            outs = []
            if "out" in kw:
                outs.append(kw["out"])
            if a:
                outs.append(a[0])
            for i, x in enumerate(a[1:]):
                s = _ap_span(x)
                if s:
                    rd.append(s)
            for key in ("in_", "bias", "scale"):
                if key in kw:
                    s = _ap_span(kw[key])
                    if s:
                        rd.append(s)
            for x in kw.get("ins", ()):
                s = _ap_span(x)
                if s:
                    rd.append(s)
            for x in kw.get("outs", ()):
                outs.append(x)
            for x in outs:
                s = _ap_span(x)
                if s:
                    wr.append(s)
            hit = any(_overlap(s, self.pend) for s in rd) or \
                any(_overlap(s, self.pend) for s in wr) or \
                any(_overlap(s, self.pendr) for s in wr)
            if hit:
                if self.ops and self.ops[-1][0] != "drain":
                    self.ops.append(("drain", (), {}, ()))
                self.pend = []
                self.pendr = []
            self.pend.extend(wr)
            self.pendr.extend(rd)
        self.ops.append((method, a, kw, tuple(then)))

    def w(self, sem, n=None):
        n = sem.n if n is None else n
        if n > 0:
            self.ops.append(("wait_ge", (sem.h, n), {}, ()))

    def replay(self, eng):
        for m, a, kw, thens in self.ops:
            r = getattr(eng, m)(*a, **kw)
            for sem, k in thens:
                r.then_inc(sem.h, k)


def build_program(t_core=2048, pass_t=1024, fb=512, stop_after=None, debug=(), rg=None):
    assert t_core % pass_t == 0 and pass_t % CH == 0 and pass_t % fb == 0
    n_pass = t_core // pass_t
    n_chunks = t_core // CH
    nch_pass = pass_t // CH
    Te = pass_t + 3
    dbg = set(debug)

    nc = bass.Bass("TRN2", target_bir_lowering=False, debug=False)
    DT = nc.dram_tensor

    x_fm = DT("x_fm", [HID, t_core + 3], F32, kind="ExternalInput").ap()
    w_in_p = DT("w_in_p", [M_IN, HID, 128], F32, kind="ExternalInput").ap()
    w1_p = DT("w1_p", [KF, HID, 128], F32, kind="ExternalInput").ap()
    w2_p = DT("w2_p", [KH, FF, 128], F32, kind="ExternalInput").ap()
    wout_p = DT("wout_p", [KH, DIN, 128], F32, kind="ExternalInput").ap()
    conv_w_in = DT("conv_w", [128, KC * KCONV], F32, kind="ExternalInput").ap()
    conv_b_in = DT("conv_b", [128, KC], F32, kind="ExternalInput").ap()
    dtb_in = DT("dtb", [NH, 1], F32, kind="ExternalInput").ap()
    alog_in = DT("alog", [NH, 1], F32, kind="ExternalInput").ap()
    dvec_in = DT("dvec", [NH, 1], F32, kind="ExternalInput").ap()
    nw_in = DT("nw", [128, KD], F32, kind="ExternalInput").ap()
    b1_in = DT("b1", [128, KF], F32, kind="ExternalInput").ap()
    ga_in = DT("ga", [128, KF], F32, kind="ExternalInput").ap()
    be_in = DT("be", [128, KF], F32, kind="ExternalInput").ap()
    b2_in = DT("b2", [128, KH], F32, kind="ExternalInput").ap()
    idf_in = DT("idf", [128, 128], F32, kind="ExternalInput").ap()
    idb_in = DT("idb", [128, 128], BF16, kind="ExternalInput").ap()
    up01_in = DT("up01", [CH, CH], BF16, kind="ExternalInput").ap()
    hsel_in = DT("hsel", [NH, DIN], BF16, kind="ExternalInput").ap()
    psel_in = DT("psel", [NH, 128], BF16, kind="ExternalInput").ap()
    bsel_in = DT("bsel", [NH, KD], BF16, kind="ExternalInput").ap()
    amask_in = DT("amask", [128, 4], F32, kind="ExternalInput").ap()
    dmask_in = DT("dmask", [NH, 4], F32, kind="ExternalInput").ap()

    out_fm = DT("out_fm", [HID, t_core], F32, kind="ExternalOutput").ap()
    dbg_specs = {
        "d_z": ([KD, 128, t_core], F16),
        "d_xbc": ([KC, 128, t_core], BF16),
        "d_dt": ([NH, t_core], F32),
        "d_acs": ([NH, t_core], F32),
        "d_yp": ([KD, 128, t_core], F16),
        "d_state": ([128, DIN + 2], F32),
        "d_xq": ([KH, 128, t_core], F16),
        "d_h": ([KF, 128, t_core], F16),
        "d_g": ([KH, 128, t_core], F16),
        "d_ynorm": ([KD, 128, t_core], F16),
        "d_sin": ([128, DIN], F32),
        "d_xdtt": ([128, DIN], BF16),
        "d_mp": ([128, NH * CH], BF16),
        "d_xn": ([KH, 128, t_core], F16),
    }
    dout = {}
    for name in dbg:
        shp, dt = dbg_specs[name]
        dout[name] = DT(name, shp, dt, kind="ExternalOutput").ap()

    wbf_in = DT("wbf_in", [M_IN, HID, 128], F16, kind="Internal").ap()
    wbf_1 = DT("wbf_1", [KF, HID, 128], F16, kind="Internal").ap()
    wbf_2 = DT("wbf_2", [KH, FF, 128], F16, kind="Internal").ap()
    wbf_o = DT("wbf_o", [KH, DIN, 128], F16, kind="Internal").ap()
    z_sp = DT("z_sp", [KD, 128, t_core], F16, kind="Internal").ap()
    yp_sp = DT("yp_sp", [KD, 128, t_core], F16, kind="Internal").ap()
    g_sp = DT("g_sp", [KH, 128, t_core], F16, kind="Internal").ap()
    st_sp = DT("st_sp", [n_chunks, 128, DIN], F16, kind="Internal").ap()
    xq_sp = DT("xq_sp", [KH, 128, t_core], F16, kind="Internal").ap()
    c_sp = DT("c_sp", [n_chunks, 128, CH], BF16, kind="Internal").ap()
    cc_in = DT("cc_in", [128, DIN + 2], F32, kind="Internal").ap()
    cc_out = DT("cc_out", [4, 128, DIN + 2], F32, kind="Internal",
                addr_space="Local").ap()

    from contextlib import ExitStack
    ctx = ExitStack()

    def sb(name, shape, dt):
        return ctx.enter_context(nc.sbuf_tensor(name, shape, dt))

    def ps_(name, shape, dt):
        return ctx.enter_context(nc.psum_tensor(name, shape, dt))

    def prod(xs):
        r = 1
        for x in xs:
            r *= x
        return r

    def vw(arena, off_bytes, dtype, shape):
        esz = 4 if dtype == F32 else 2
        assert off_bytes % 8 == 0
        ap = arena[:] if dtype == F32 else arena[:].bitcast(dtype)
        start = off_bytes // esz
        per = prod(shape[1:])
        flat = ap[:, start:start + per]
        if len(shape) == 3:
            return flat.rearrange("q (a b) -> q a b", a=shape[1])
        return flat

    wA = max(KH * Te * 4, KC * Te * 2, KF * fb * 2, KD * pass_t * 2)
    wA = (wA + 31) // 32 * 32
    arA = sb("arA", [128, wA // 4], F32)
    xf32 = vw(arA, 0, F32, [128, KH, Te])
    big1 = vw(arA, 0, BF16, [128, KC, Te])
    h_sb = vw(arA, 0, F16, [128, KF, fb])
    ynb = vw(arA, 0, F16, [128, KD, pass_t])

    wB = max(KH * Te * 2,
             ((DIN + 2) * 4 + 31) // 32 * 32 + 2 * KD * 256)
    wB = (wB + 31) // 32 * 32
    arB = sb("arB", [128, wB // 4], F32)
    xn_sb = vw(arB, 0, F16, [128, KH, Te])
    ccld = vw(arB, 0, F32, [128, DIN + 2])
    gxl = vw(arB, 0, F16, [128, KH, 512])
    pb_ld2 = vw(arB, ((DIN + 2) * 4 + 31) // 32 * 32, F16, [128, KD, 128])
    pb_ld3 = vw(arB, ((DIN + 2) * 4 + 31) // 32 * 32 + KD * 256, F16,
                [128, KD, 128])


    arC = sb("arC", [128, 8192], F32)
    xdtT = vw(arC, 0, BF16, [128, KD, 128])
    mp_sb = vw(arC, 8192, BF16, [128, NH, CH])
    ystage = vw(arC, 24576, F16, [128, 8, 128])
    zstage = vw(arC, 26624, F16, [128, 2, 512])
    xqstage = zstage
    xq_sb = vw(arC, 0, F16, [128, KH, fb])
    wb2 = vw(arC, 0, F16, [128, KF, 128])
    ffnbc = vw(arC, 16384, F32, [128, 8, 512])
    stot = vw(arC, 0, F32, [128, KD, 128])
    stotT = vw(arC, 16384, BF16, [128, KD, 128])
    pb_ld = vw(arC, 24576, F16, [128, KD, 128])

    c_idf = sb("c_idf", [128, 128], F32)
    c_idb = sb("c_idb", [128, 128], BF16)
    c_up01 = sb("c_up01", [CH, CH], BF16)
    c_hsel = sb("c_hsel", [NH, DIN], BF16)
    c_psel = sb("c_psel", [NH, 128], BF16)
    c_bsel = sb("c_bsel", [NH, KD], BF16)
    cw_sb = sb("cw_sb", [128, KC * KCONV], F32)
    cb_sb = sb("cb_sb", [128, KC], F32)
    smallv = sb("smallv", [NH, 8], F32)
    small_bf = sb("small_bf", [NH, 4], BF16)
    seld = sb("seld", [NH, KD], BF16)
    ga_sb = sb("ga_sb", [128, KF], F32)
    be_sb = sb("be_sb", [128, KF], F32)

    stat4T = sb("stat4T", [16, 128], F32)
    thr_sb = sb("thr_sb", [128, 4], F32)
    nw_sb = sb("nw_sb", [128, KD], F32)
    b1_sb = sb("b1_sb", [128, KF], F32)
    b2_sb = sb("b2_sb", [128, KH], F32)
    amask_sb = sb("amask_sb", [128, 4], F32)
    dmask_sb = sb("dmask_sb", [NH, 4], F32)
    ones_r = sb("ones_r", [1, 128], F32)
    ones_c = sb("ones_c", [128, 1], F32)
    ones_cb = sb("ones_cb", [128, 1], F16)
    ones_cbb = sb("ones_cbb", [128, 1], BF16)
    ones64 = sb("ones64", [NH, CH], F32)
    dexp_sb = sb("dexp_sb", [128, KD], F32)
    mw_col = sb("mw_col", [128, 2], F32)
    wsacc = sb("wsacc", [128, 2], F32)
    acs_sb = sb("acs_sb", [NH, t_core], F32)
    cend_sb = sb("cend_sb", [NH, 48], F32)
    s_state = sb("s_state", [128, KD, 128], F16)
    wp_a = sb("wp_a", [128, KF, 128], F16)
    wp_b = sb("wp_b", [128, KD, 128], F16)
    gstage = wp_b[:].rearrange("q a b -> q (a b)").rearrange(
        "q (a b) -> q a b", a=2)[:, :, 0:512]
    scr1 = sb("scr1", [128, 1032], F32)
    scr2 = sb("scr2", [128, 1032], F32)
    scr3 = sb("scr3", [128, 1032], F32)
    ug_sb = scr3[:, 0:512]
    sq_sb = scr3[:].bitcast(BF16)[:, 1536:2048]
    ea2_sb = scr3[:].bitcast(BF16)[:, 1024:1536]
    flat_sb = sb("flat_sb", [1, 1032], F32)
    shB = sb("shB", [128, 1032], F32)
    dt_raw = shB[:]
    outst = shB[:, 0:1024].rearrange("q (a b) -> q a b", a=2)
    dtA = dt_raw
    acsT = sb("acsT", [128, NH], F32)
    tb1 = sb("tb1", [128, NH], BF16)
    tb2 = sb("tb2", [128, NH], BF16)
    gm_sb = sb("gm_sb", [128, CH], BF16)
    dcy_sb = sb("dcy_sb", [NH, CH], F32)
    adt_sb = dcy_sb
    stat4 = sb("stat4", [128, 16], F32)
    dcb = sb("dcb", [NH, 2 * CH], BF16)
    bt_sb = sb("bt_sb", [128, CH], BF16)
    c_ld = bt_sb
    dexp2_sb = sb("dexp2_sb", [128, KD], F32)
    cst = sb("cst", [128, 4], F32)

    pbk = [ps_(f"pb{i}", [128, 512], F32) for i in range(6)]
    pbf = [ps_(f"pbf{i}", [128, 1024], BF16) for i in range(2)]

    T = EngQ()
    V = EngQ(auto_drain=True)
    SC = EngQ(auto_drain=True)
    G = EngQ(auto_drain=True)
    SY = EngQ()
    def mksem(name):
        return Sem(ctx.enter_context(nc.semaphore(name)))
    s_t, s_v, s_sc = mksem("s_t"), mksem("s_v"), mksem("s_sc")
    s_small = mksem("s_small")
    s_x = mksem("s_x")
    s_wpa = mksem("s_wpa")
    s_wpb = mksem("s_wpb")
    s_wba = mksem("s_wba")
    s_wbb = mksem("s_wbb")
    s_wpc = mksem("s_wpc")
    s_wpd = mksem("s_wpd")
    s_wbc = mksem("s_wbc")
    s_wbd = mksem("s_wbd")
    s_zs0 = mksem("s_zs0")
    s_zs1 = mksem("s_zs1")
    s_dbg = mksem("s_dbg")
    s_o = mksem("s_o")
    s_yst = mksem("s_yst")
    s_stp = mksem("s_stp")
    s_gb = mksem("s_gb")
    s_cs = mksem("s_cs")
    s_cc = mksem("s_cc")
    s_ld2 = mksem("s_ld2")
    s_fl0 = mksem("s_fl0")
    s_fl1 = mksem("s_fl1")
    s_pre0 = mksem("s_pre0")
    s_pre1 = mksem("s_pre1")
    s_gsp = mksem("s_gsp")
    s_gl = mksem("s_gl")
    s_gc = mksem("s_gc")
    s_xq0 = mksem("s_xq0")
    s_xq1 = mksem("s_xq1")
    s_xql = mksem("s_xql")

    for dst, src in [
        (c_idf[:], idf_in), (c_idb[:], idb_in), (c_up01[:], up01_in),
        (c_hsel[:], hsel_in), (c_psel[:], psel_in), (c_bsel[:], bsel_in),
        (cw_sb[:], conv_w_in), (cb_sb[:], conv_b_in),
        (smallv[:, 0:1], dtb_in), (smallv[:, 1:2], alog_in),
        (smallv[:, 3:4], dvec_in),
        (nw_sb[:], nw_in), (b1_sb[:], b1_in), (b2_sb[:], b2_in),
        (ga_sb[:], ga_in), (be_sb[:], be_in),
        (amask_sb[:], amask_in), (dmask_sb[:], dmask_in),
    ]:
        SY.c("dma_start", out=dst, in_=src, then=s_small.inc(16))
    y_small = s_small.n

    V.c("memset", cst[:, 0:1], EPS)
    V.c("memset", cst[:, 1:2], 1.0)
    V.c("memset", cst[:, 2:3], -1.0)
    V.c("memset", cst[:, 3:4], 0.0)
    V.c("memset", ones_r[:], 1.0)
    V.c("memset", ones_c[:], 1.0)
    V.c("memset", ones_cb[:], 1.0)
    V.c("memset", ones_cbb[:], 1.0)
    V.c("memset", ones64[:], 1.0)
    V.c("memset", s_state[:], 0.0, then=s_v.inc())

    SC.w(s_small, y_small)
    SC.c("activation", smallv[:, 2:3], smallv[:, 1:2], AF.Exp, then=s_sc.inc())
    V.w(s_sc, s_sc.n)
    V.c("tensor_scalar_mul", smallv[:, 2:3], smallv[:, 2:3], -1.0)
    V.c("tensor_copy", small_bf[:, 0:1], smallv[:, 3:4], then=s_v.inc())

    def expand_hp(colvec_f32, dst_sb, bank):
        V.w(s_t, s_t.n)
        V.c("tensor_scalar", seld[:], c_bsel[:], colvec_f32, None, ALU.mult,
            then=s_v.inc())
        T.w(s_v, s_v.n)
        T.c("matmul", pbk[bank][:, 0:KD], c_psel[:], seld[:], start=True,
            stop=True, then=s_t.inc())
        V.w(s_t, s_t.n)
        V.c("tensor_copy", dst_sb, pbk[bank][:, 0:KD], then=s_v.inc())

    expand_hp(smallv[:, 3:4], dexp_sb[:], 4)

    wpf = wp_a[:].rearrange("q a b -> q (a b)").bitcast(F32)
    prep_units = [("w1", i, 0) for i in range(KF)] + \
                 [("w2", i, q) for i in range(KH) for q in range(4)]
    prep_state = {"i": 0, "vsum": 0}

    def wprep_step(n_units):
        for _ in range(n_units):
            i = prep_state["i"]
            if i >= len(prep_units):
                return
            prep_state["i"] = i + 1
            u = prep_units[i]
            half = i % 2
            hsem = s_pre0 if half == 0 else s_pre1
            dst = wpf[:, half * 2048:half * 2048 + 2048]
            dst2 = dst.rearrange("q (k c) -> q k c", c=128)
            if u[0] == "w1":
                src = w1_p[u[1]].rearrange("(k q) c -> q k c", q=128)
                col = 0
            else:
                src = w2_p[u[1]].rearrange("(k q) c -> q k c", q=128)[
                    :, u[2] * 16:u[2] * 16 + 16, :]
                col = 1
            if i == 0:
                SY.w(s_t, s_t.n)
                SY.w(s_wba, s_wba.n)
                SY.w(s_wbb, s_wbb.n)
                V.c("memset", wsacc[:], 0.0)
            SY.w(s_v, prep_state["vsum"])
            SY.c("dma_start", out=dst2, in_=src, then=hsem.inc(16))
            V.w(hsem, hsem.n)
            V.c("tensor_reduce", scr2[:, 1030:1031], dst, AX.X, ALU.add,
                apply_absolute_value=True)
            V.c("tensor_tensor", wsacc[:, col:col + 1], wsacc[:, col:col + 1],
                scr2[:, 1030:1031], ALU.add, then=s_v.inc())
            prep_state["vsum"] = s_v.n

    def wprep_finalize():
        V.w(s_v, prep_state["vsum"])
        T.w(s_v, prep_state["vsum"])
        T.c("matmul", pbk[0][0:1, 504:506], ones_c[:], wsacc[:, 0:2],
            start=True, stop=True, then=s_t.inc())
        V.w(s_t, s_t.n)
        V.c("tensor_scalar", scr3[0:1, 1024:1026], pbk[0][0:1, 504:506],
            1.0 / (FF * HID), EPS, ALU.mult, ALU.max, then=s_v.inc())
        T.w(s_v, s_v.n)
        T.c("matmul", pbk[1][:, 504:506], ones_r[:], scr3[0:1, 1024:1026],
            start=True, stop=True, then=s_t.inc())
        V.w(s_t, s_t.n)
        V.c("tensor_scalar_mul", thr_sb[:, 0:1], pbk[1][:, 504:505], 0.5)
        V.c("tensor_scalar_mul", thr_sb[:, 1:2], pbk[1][:, 504:505], -0.5)
        V.c("tensor_scalar_mul", thr_sb[:, 2:3], pbk[1][:, 505:506], 0.5)
        V.c("tensor_scalar_mul", thr_sb[:, 3:4], pbk[1][:, 505:506], -0.5)
        V.c("tensor_copy", mw_col[:, 0:2], pbk[1][:, 504:506],
            then=s_v.inc())

    def stage_a(p):
        c0 = p * pass_t
        first = (p == 0)
        sl_x = nslices(Te)
        sl_a = nslices(pass_t, off=3)

        if gate:
            SY.w(s_v, gate["v"])
            V.w(s_t, gate["t"])
            V.w(s_sc, gate["sc"])
            SC.w(s_v, gate["v"])
        SY.c("dma_start", out=xf32[:],
             in_=x_fm.rearrange("(k q) t -> q k t", q=128)[:, :, c0:c0 + Te],
             then=s_x.inc(16))
        x_rdy = s_x.n

        V.w(s_x, x_rdy)
        V.c("tensor_tensor", scr1[:, 0:Te], xf32[:, 0, :], xf32[:, 1, :], ALU.add)
        for k in range(2, KH):
            V.c("tensor_tensor", scr1[:, 0:Te], scr1[:, 0:Te], xf32[:, k, :],
                ALU.add, then=(s_v.inc() if k == KH - 1 else ()))
        for k in range(KH):
            V.c("tensor_tensor", scr2[:, 0:Te], xf32[:, k, :], xf32[:, k, :],
                ALU.mult)
            if k == 0:
                V.c("tensor_copy", scr3[:, 0:Te], scr2[:, 0:Te])
            else:
                V.c("tensor_tensor", scr3[:, 0:Te], scr3[:, 0:Te],
                    scr2[:, 0:Te], ALU.add,
                    then=(s_v.inc() if k == KH - 1 else ()))
        v_stats = s_v.n

        T.w(s_v, v_stats)
        regions = {}
        items = []
        for stat, src, bks in [(0, scr1, (2, 3)), (1, scr3, (4, 5))]:
            for si, (o, w) in enumerate(sl_x):
                if si == 0:
                    reg = pbk[bks[0]][0:1, 0:w]
                elif si == 1:
                    reg = pbk[bks[1]][0:1, 0:w]
                else:
                    reg = pbk[bks[0]][0:1, 505:505 + w]
                regions[(stat, si)] = (reg, o, w)
                items.append((reg, src, o, w))
        for i, (reg, src, o, w) in enumerate(items):
            T.c("matmul", reg, ones_c[:], src[:, o:o + w], start=True,
                stop=True, then=(s_t.inc() if i == len(items) - 1 else ()))
        t_stats = s_t.n

        V.w(s_t, t_stats)
        for stat, dst in [(0, scr2), (1, flat_sb)]:
            for si in range(len(sl_x)):
                reg, o, w = regions[(stat, si)]
                V.c("tensor_scalar_mul", dst[0:1, o:o + w], reg, 1.0 / HID)
        V.c("tensor_tensor", dt_raw[0:1, 0:Te], scr2[0:1, 0:Te],
            scr2[0:1, 0:Te], ALU.mult)
        V.c("tensor_tensor", flat_sb[0:1, 0:Te], flat_sb[0:1, 0:Te],
            dt_raw[0:1, 0:Te], ALU.subtract, then=s_v.inc())
        SC.w(s_v, s_v.n)
        SC.c("activation", dt_raw[0:1, 0:Te], flat_sb[0:1, 0:Te], AF.Sqrt,
             bias=cst[0:1, 0:1], then=s_sc.inc())
        V.w(s_sc, s_sc.n)
        V.c("reciprocal", flat_sb[0:1, 0:Te], dt_raw[0:1, 0:Te])
        V.c("tensor_tensor", dt_raw[0:1, 0:Te], scr2[0:1, 0:Te],
            flat_sb[0:1, 0:Te], ALU.mult)
        V.c("tensor_scalar_mul", dt_raw[0:1, 0:Te], dt_raw[0:1, 0:Te], -1.0,
            then=s_v.inc())
        v_ab = s_v.n

        T.w(s_v, v_ab)
        v_apply = 0
        for si, (o, w) in enumerate(sl_x):
            if si > 0:
                T.w(s_v, v_apply)
            T.c("matmul", pbk[4][:, 0:w], ones_r[:], flat_sb[0:1, o:o + w],
                start=True, stop=True)
            T.c("matmul", pbk[5][:, 0:w], ones_r[:], dt_raw[0:1, o:o + w],
                start=True, stop=True, then=s_t.inc())
            V.w(s_t, s_t.n)
            for k in range(KH):
                hof = 512 * (k % 2)
                V.c("tensor_tensor", scr3[:, hof:hof + w],
                    xf32[:, k, o:o + w], pbk[4][:, 0:w], ALU.mult)
                V.c("tensor_tensor", xf32[:, k, o:o + w],
                    scr3[:, hof:hof + w], pbk[5][:, 0:w], ALU.add,
                    then=s_v.inc())
                SC.w(s_v, s_v.n)
                SC.c("activation", xn_sb[:, k, o:o + w],
                     xf32[:, k, o:o + w], AF.Copy, bias=0.0,
                     then=s_sc.inc())
            xsw = xf32[:, :, o:o + w].rearrange("q k t -> q t k")
            V.c("tensor_reduce", scr1[:, o:o + w], xsw, AX.X, ALU.max)
            V.c("tensor_reduce", scr3[:, 0:w], xsw, AX.X, ALU.min)
            V.c("scalar_tensor_tensor", scr1[:, o:o + w], scr3[:, 0:w],
                -1.0, scr1[:, o:o + w], ALU.mult, ALU.max,
                then=s_v.inc())
            v_apply = s_v.n
        v_xn = s_v.n
        sc_xn = s_sc.n

        nj = pass_t // 128
        for jg in range(cdiv(nj, 4)):
            ng = min(4, nj - jg * 4)
            T.w(s_v, v_xn if jg == 0 else s_v.n)
            for j4 in range(ng):
                j = jg * 4 + j4
                T.c("transpose", pbk[4][:, j4 * 128:(j4 + 1) * 128],
                    scr1[:, 3 + j * 128:3 + (j + 1) * 128], c_idf[:],
                    then=(s_t.inc() if j4 == ng - 1 else ()))
            V.w(s_t, s_t.n)
            for j4 in range(ng):
                j = jg * 4 + j4
                V.c("tensor_reduce", stat4[:, 8 + j:9 + j],
                    pbk[4][:, j4 * 128:(j4 + 1) * 128], AX.X, ALU.max,
                    then=(s_v.inc() if j4 == ng - 1 else ()))
        T.w(s_v, s_v.n)
        T.c("transpose", pbk[4][0:nj, 0:128], stat4[:, 8:8 + nj], c_idf[:],
            then=s_t.inc())
        V.w(s_t, s_t.n)
        V.c("tensor_copy", stat4T[0:nj, :], pbk[4][0:nj, 0:128],
            then=s_v.inc())
        G.w(s_v, s_v.n)
        G.c("dma_start", out=scr3[0:1, 0:pass_t], in_=stat4T[0:nj, :],
            then=s_gb.inc(16))
        V.w(s_gb, s_gb.n)
        V.c("tensor_scalar_max", scr3[0:1, 0:pass_t], scr3[0:1, 0:pass_t],
            EPS)
        V.c("reciprocal", scr2[0:1, 0:pass_t], scr3[0:1, 0:pass_t])
        V.c("tensor_scalar_mul", dt_raw[0:1, 3:3 + pass_t],
            scr2[0:1, 0:pass_t], 127.0, then=s_v.inc())
        v_qrows = s_v.n

        for si2, (oq, wq) in enumerate(nslices(pass_t, 512, off=3)):
            T.w(s_v, v_qrows if si2 == 0 else s_v.n)
            T.c("matmul", pbk[4][:, 0:wq], ones_r[:],
                dt_raw[0:1, oq:oq + wq], start=True, stop=True,
                then=s_t.inc())
            V.w(s_t, s_t.n)
            for k in range(KH):
                slot = k % 2
                hof = 512 * slot
                qsem = s_xq0 if slot == 0 else s_xq1
                V.c("tensor_tensor", scr3[:, hof:hof + wq],
                    xf32[:, k, oq:oq + wq], pbk[4][:, 0:wq], ALU.mult)
                if qsem.n:
                    V.w(qsem, qsem.n)
                V.c("tensor_scalar", xqstage[:, slot, 0:wq],
                    scr3[:, hof:hof + wq],
                    MAGIC, -MAGIC, ALU.add, ALU.add, then=s_v.inc())
                SY.w(s_v, s_v.n)
                SY.c("dma_start",
                     out=xq_sp[k, :, c0 + oq - 3:c0 + oq - 3 + wq],
                     in_=xqstage[:, slot, 0:wq], then=qsem.inc(16))
        v_xq_done = s_v.n

        if "d_xn" in dout:
            SY.w(s_v, v_xn)
            SY.c("dma_start",
                 out=dout["d_xn"].rearrange("k q t -> q k t")[:, :, c0:c0 + pass_t],
                 in_=xn_sb[:, :, 3:Te], then=s_dbg.inc(16))

        V.w(s_xq0, s_xq0.n)
        V.w(s_xq1, s_xq1.n)
        SC.w(s_xq0, s_xq0.n)
        SC.w(s_xq1, s_xq1.n)
        panel_rdy = {}
        mm_done = {}
        evac_done = {}
        zspill = {}

        def load_panel(m):
            buf = wp_a if m % 2 == 0 else wp_b
            sem = (s_wpa if m % 2 == 0 else s_wpb) if first else \
                  (s_wpc if m % 2 == 0 else s_wpd)
            wbsem = s_wba if m % 2 == 0 else s_wbb
            if m - 2 in mm_done:
                (G if first else SY).w(s_t, mm_done[m - 2])
            src_f = w_in_p[m].rearrange("(k p) c -> p k c", p=128)
            src_b = wbf_in[m].rearrange("(k p) c -> p k c", p=128)
            if first:
                G.w(wbsem, wbsem.n)
                G.c("dma_start", out=buf[:, 0:KH, :], in_=src_f,
                    then=sem.inc(16))
                panel_rdy[m] = (sem, sem.n)
                G.w(sem, sem.n)
                G.c("dma_start", out=src_b, in_=buf[:, 0:KH, :],
                    then=wbsem.inc(16))
            else:
                SY.w(s_wba, s_wba.n)
                SY.w(s_wbb, s_wbb.n)
                SY.w(s_wbc, s_wbc.n)
                SY.w(s_wbd, s_wbd.n)
                SY.c("dma_start", out=buf[:, 0:KH, :], in_=src_b,
                    then=sem.inc(16))
                panel_rdy[m] = (sem, sem.n)

        load_panel(0)
        for m in range(M_IN):
            if m + 1 < M_IN:
                load_panel(m + 1)
            buf = wp_a if m % 2 == 0 else wp_b
            mcols = 128 if m < M_IN - 1 else 64
            kind = "z" if m < KD else ("xbc" if m < KD + KC else "dt")
            qsem, qv = panel_rdy[m]
            T.w(qsem, qv)
            T.w(s_v, v_xn)
            T.w(s_sc, sc_xn)
            if m - 2 in evac_done:
                T.w(s_sc, evac_done[m - 2])
            bsel = (0, 1) if m % 2 == 0 else (2, 3)
            tbank = pbk[4] if m % 2 == 0 else pbk[5]
            sls = sl_x if kind == "xbc" else sl_a
            for si, (o, w) in enumerate(sls):
                if si < 2:
                    reg = pbk[bsel[si]][0:mcols, 0:w]
                else:
                    reg = tbank[0:mcols, 0:w]
                for k in range(KH):
                    T.c("matmul", reg, buf[:, k, 0:mcols],
                        xn_sb[:, k, o:o + w], start=(k == 0),
                        stop=(k == KH - 1),
                        then=(s_t.inc() if (k == KH - 1 and si == len(sls) - 1)
                              else ()))
            mm_done[m] = s_t.n

            SC.w(s_t, mm_done[m])
            for si, (o, w) in enumerate(sls):
                if si < 2:
                    reg = pbk[bsel[si]][0:mcols, 0:w]
                else:
                    reg = tbank[0:mcols, 0:w]
                if kind == "z":
                    zsem = s_zs0 if si == 0 else s_zs1
                    if zspill.get(si):
                        SC.w(zsem, zspill[si])
                    SC.c("activation", zstage[:, si, 0:w], reg, AF.Copy,
                         bias=0.0, then=s_sc.inc())
                    SY.w(s_sc, s_sc.n)
                    SY.c("dma_start", out=z_sp[m, :, c0 + o - 3:c0 + o - 3 + w],
                         in_=zstage[:, si, 0:w], then=zsem.inc(16))
                    zspill[si] = zsem.n
                elif kind == "xbc":
                    SC.c("activation", big1[:, m - KD, o:o + w], reg,
                         AF.Copy, bias=0.0,
                         then=(s_sc.inc() if si == len(sls) - 1 else ()))
                else:
                    SC.c("activation", dt_raw[0:64, o:o + w], reg, AF.Copy,
                         bias=0.0,
                         then=(s_sc.inc() if si == len(sls) - 1 else ()))
            evac_done[m] = s_sc.n
        v_inproj = s_v.n
        sc_inproj = s_sc.n

        if stop_after == "inproj":
            return

        conv_sc = {}
        for c in range(KC):
            if c % 2 == 0:
                acc = scr1
                V.w(s_sc, evac_done[KD + c])
                if c - 2 in conv_sc:
                    V.w(s_sc, conv_sc[c - 2])
                V.c("tensor_scalar_mul", acc[:, 0:pass_t],
                    big1[:, c, 0:pass_t], cw_sb[:, c * KCONV:c * KCONV + 1])
                for i in range(1, KCONV):
                    V.c("scalar_tensor_tensor", acc[:, 0:pass_t],
                        big1[:, c, i:i + pass_t],
                        cw_sb[:, c * KCONV + i:c * KCONV + i + 1],
                        acc[:, 0:pass_t], ALU.mult, ALU.add,
                        then=(s_v.inc() if i == KCONV - 1 else ()))
                SC.w(s_v, s_v.n)
            else:
                acc = scr3
                tmp = scr3[:, 512:512 + pass_t] if pass_t <= 512 else None
                G.w(s_sc, evac_done[KD + c])
                if c - 2 in conv_sc:
                    G.w(s_sc, conv_sc[c - 2])

                def cwb(i):
                    return cw_sb[:, c * KCONV + i:c * KCONV + i + 1]\
                        .broadcast_to([128, pass_t])
                G.c("tensor_tensor", acc[:, 0:pass_t],
                    big1[:, c, 0:pass_t], cwb(0), ALU.mult)
                for i in range(1, KCONV):
                    G.c("tensor_tensor", tmp, big1[:, c, i:i + pass_t],
                        cwb(i), ALU.mult)
                    G.c("tensor_tensor", acc[:, 0:pass_t], acc[:, 0:pass_t],
                        tmp, ALU.add,
                        then=(s_gc.inc() if i == KCONV - 1 else ()))
                SC.w(s_gc, s_gc.n)
            SC.c("activation", big1[:, c, 3:Te], acc[:, 0:pass_t], AF.Silu,
                 bias=cb_sb[:, c:c + 1], then=s_sc.inc())
            conv_sc[c] = s_sc.n
        sc_conv = s_sc.n

        SY.w(s_sc, sc_conv)
        for ci in range(nch_pass):
            SY.c("dma_start", out=c_sp[p * nch_pass + ci],
                 in_=big1[:, 33, 3 + ci * CH:3 + (ci + 1) * CH],
                 then=s_cs.inc(16))

        V.w(s_sc, sc_inproj)
        V.w(s_sc, sc_conv)
        V.c("tensor_scalar", scr1[0:64, 0:pass_t], dt_raw[0:64, 3:Te],
            smallv[:, 0:1], None, ALU.add, then=s_v.inc())
        SC.w(s_v, s_v.n)
        SC.c("activation", scr2[0:64, 0:pass_t], scr1[0:64, 0:pass_t], AF.Abs)
        SC.c("activation", scr3[0:64, 0:pass_t], scr2[0:64, 0:pass_t], AF.Exp,
             scale=cst[0:64, 2:3])
        SC.c("activation", scr2[0:64, 0:pass_t], scr3[0:64, 0:pass_t], AF.Ln,
             bias=cst[0:64, 1:2], then=s_sc.inc())
        V.w(s_sc, s_sc.n)
        V.c("tensor_scalar_max", dtA[0:64, 0:pass_t], scr1[0:64, 0:pass_t],
            0.0)
        V.c("tensor_tensor", dtA[0:64, 0:pass_t], dtA[0:64, 0:pass_t],
            scr2[0:64, 0:pass_t], ALU.add, then=s_v.inc())
        v_dt = s_v.n

        if stop_after == "conv":
            if "d_dt" in dout:
                SY.w(s_v, v_dt)
                SY.c("dma_start", out=dout["d_dt"][:, c0:c0 + pass_t],
                     in_=dtA[0:64, 0:pass_t], then=s_dbg.inc(16))
            if "d_xbc" in dout:
                SY.w(s_sc, sc_conv)
                SY.c("dma_start",
                     out=dout["d_xbc"].rearrange("k q t -> q k t")[:, :, c0:c0 + pass_t],
                     in_=big1[:, :, 3:Te], then=s_dbg.inc(16))
            return

        for ci in range(nch_pass):
            if p == 0:
                wprep_step(cdiv(128, nch_pass))
            gc = p * nch_pass + ci
            tc = 3 + ci * CH
            cc = ci * CH
            gcol = gc * CH
            t_chunk_start = s_t.n

            V.w(s_v, v_dt)
            V.w(s_t, t_chunk_start)
            V.c("tensor_scalar", adt_sb[0:64, :], dtA[0:64, cc:cc + CH],
                smallv[:, 2:3], None, ALU.mult)
            V.c("tensor_tensor_scan", acs_sb[0:64, gcol:gcol + CH],
                ones64[:, :], adt_sb[0:64, :], 0.0, ALU.mult, ALU.add)
            V.c("tensor_scalar", dcy_sb[0:64, :], acs_sb[0:64, gcol:gcol + CH],
                acs_sb[0:64, gcol + CH - 1:gcol + CH], -1.0,
                ALU.subtract, ALU.mult)
            V.c("tensor_copy", cend_sb[:, gc:gc + 1],
                acs_sb[0:64, gcol + CH - 1:gcol + CH])
            V.c("tensor_copy", dcb[:, 0:CH], dtA[0:64, cc:cc + CH],
                then=s_v.inc())
            v_sc1 = s_v.n
            SC.w(s_v, v_sc1)
            SC.c("activation", dcb[:, CH:2 * CH], dcy_sb[0:64, :], AF.Exp)
            SC.c("activation", smallv[:, 6:7],
                 acs_sb[0:64, gcol + CH - 1:gcol + CH], AF.Exp,
                 then=s_sc.inc())
            sc_sc1 = s_sc.n
            V.w(s_sc, sc_sc1)
            expand_hp(smallv[:, 6:7], dexp2_sb[:], 4)
            v_dexp2 = s_v.n

            T.w(s_v, v_sc1)
            T.c("transpose", pbk[5][:, 0:64], acs_sb[0:64, gcol:gcol + CH],
                c_idf[0:64, 0:64], then=s_t.inc())
            V.w(s_t, s_t.n)
            V.c("tensor_copy", acsT[:, 0:64], pbk[5][:, 0:64], then=s_v.inc())
            v_acsT = s_v.n

            T.w(s_sc, sc_conv)
            T.w(s_v, v_acsT)
            T.c("matmul", pbk[5][:, 128:256], big1[:, 32, tc:tc + CH],
                big1[:, 33, tc:tc + CH], start=True, stop=True,
                then=s_t.inc())
            V.w(s_t, s_t.n)
            V.c("tensor_tensor", gm_sb[:], pbk[5][:, 128:256], c_up01[:],
                ALU.mult, then=s_v.inc())
            v_gm = s_v.n

            T.w(s_sc, sc_sc1)
            T.c("transpose", pbf[0][:, 0:64], dcb[:, 0:CH],
                c_idb[0:64, 0:64])
            T.c("transpose", pbf[0][:, 128:192], dcb[:, CH:2 * CH],
                c_idb[0:64, 0:64])
            T.c("transpose", pbf[0][:, 256:384], big1[:, 32, tc:tc + CH],
                c_idb[:], then=s_t.inc())
            t_r0 = s_t.n
            V.w(s_t, t_r0)
            V.c("tensor_copy", tb1[:, 0:64], pbf[0][:, 0:64])
            V.c("tensor_copy", tb2[:, 0:64], pbf[0][:, 128:192])
            V.c("tensor_copy", bt_sb[:], pbf[0][:, 256:384], then=s_v.inc())
            v_r0 = s_v.n

            v_xtt = {}
            for r in range(4):
                bk = pbf[r % 2]
                if r >= 2:
                    T.w(s_v, v_xtt[r - 2])
                else:
                    T.w(s_v, v_r0)
                for j in range(8):
                    blk = r * 8 + j
                    T.c("transpose", bk[:, j * 128:(j + 1) * 128],
                        big1[:, blk, tc:tc + CH], c_idb[:],
                        then=(s_t.inc() if j == 7 else ()))
                t_r = s_t.n
                V.w(s_t, t_r)
                for gsub in range(2):
                    b0 = r * 8 + gsub * 4
                    src = bk[:, gsub * 512:(gsub + 1) * 512].rearrange(
                        "p (g o) -> p g o", g=8)
                    dst = xdtT[:, b0:b0 + 4, :].rearrange(
                        "p b (h o) -> p (b h) o", h=2)
                    mult = tb1[:, b0 * 2:b0 * 2 + 8].rearrange(
                        "p (h o) -> p h o", o=1).broadcast_to([128, 8, 64])
                    V.c("tensor_tensor", dst, src, mult, ALU.mult,
                        then=(s_v.inc() if gsub == 1 else ()))
                v_xtt[r] = s_v.n
            v_xdtT = s_v.n

            v_seg = {}
            sc_exp = {}
            t_bc_of = {}
            for g4 in range(16):
                h0 = g4 * 4
                slot = g4 % 2
                flsem = s_fl0 if slot == 0 else s_fl1
                foff = slot * 512
                if g4 >= 2:
                    G.w(s_t, t_bc_of[g4 - 2])
                G.w(s_v, v_acsT)
                G.c("dma_start", out=flat_sb[0:1, foff:foff + 512],
                    in_=acs_sb[h0:h0 + 4, gcol:gcol + CH], then=flsem.inc(16))
                T.w(flsem, flsem.n)
                if g4 >= 2:
                    T.w(s_v, v_seg[g4 - 2])
                T.c("matmul", pbk[slot][:, 0:512], ones_r[:],
                    flat_sb[0:1, foff:foff + 512], start=True, stop=True,
                    then=s_t.inc())
                t_bc = s_t.n
                t_bc_of[g4] = t_bc
                V.w(s_t, t_bc)
                if g4 >= 2:
                    V.w(s_sc, sc_exp.get(g4 - 2, 0))
                aw = acsT[:, h0:h0 + 4].rearrange(
                    "p (g o) -> p g o", o=1).broadcast_to([128, 4, CH])
                s2 = scr2[:, foff:foff + 512]
                V.c("tensor_tensor",
                    s2.rearrange("p (g t) -> p g t", g=4),
                    pbk[slot][:, 0:512].rearrange("p (g t) -> p g t", g=4),
                    aw, ALU.subtract)
                V.c("tensor_scalar_min", s2, s2, 0.0, then=s_v.inc())
                v_seg[g4] = s_v.n
                SC.w(s_v, v_seg[g4])
                SC.c("activation",
                     mp_sb[:, h0:h0 + 4, :].rearrange("p a b -> p (a b)"),
                     s2, AF.Exp, then=s_sc.inc())
                sc_exp[g4] = s_sc.n
            V.w(s_sc, s_sc.n)
            V.w(s_v, v_gm)
            V.c("tensor_tensor", mp_sb[:, 0:64, :], mp_sb[:, 0:64, :],
                gm_sb[:].rearrange("p (o t) -> p o t", o=1).broadcast_to(
                    [128, 64, CH]),
                ALU.mult, then=s_v.inc())
            v_mp = s_v.n

            yd_done = {}
            v_yd = {}
            for grp in range(8):
                bank = pbk[2 + grp % 2]
                T.w(s_v, v_mp)
                T.w(s_v, v_xdtT)
                if grp >= 2:
                    T.w(s_v, v_yd[grp - 2])
                for slotj in range(4):
                    blk = grp * 4 + slotj
                    h_even, h_odd = 2 * blk, 2 * blk + 1
                    for par, h in ((0, h_even), (1, h_odd)):
                        lhs = xdtT[:, blk, par * 64:par * 64 + 64]
                        T.c("matmul", bank[par * 64:par * 64 + 64,
                                           slotj * 128:(slotj + 1) * 128],
                            lhs, mp_sb[:, h, :], start=True, stop=True,
                            tile_position=(0, par * 64),
                            then=(s_t.inc() if (slotj == 3 and par == 1)
                                  else ()))
                yd_done[grp] = s_t.n
                V.w(s_t, yd_done[grp])
                V.w(s_yst, s_yst.n)
                for slotj in range(4):
                    blk = grp * 4 + slotj
                    V.c("scalar_tensor_tensor",
                        ystage[:, (grp % 2) * 4 + slotj, :],
                        big1[:, blk, tc:tc + CH], dexp_sb[:, blk:blk + 1],
                        bank[:, slotj * 128:(slotj + 1) * 128],
                        ALU.mult, ALU.add,
                        then=(s_v.inc() if slotj == 3 else ()))
                v_yd[grp] = s_v.n
                if grp % 2 == 1:
                    b0 = (grp - 1) * 4
                    SY.w(s_v, v_yd[grp])
                    SY.c("dma_start",
                         out=yp_sp[b0:b0 + 8, :, gcol:gcol + CH].rearrange(
                             "b q t -> q b t"),
                         in_=ystage[:], then=s_yst.inc(16))

            V.w(s_t, yd_done[7])
            for g in range(8):
                b0 = g * 4
                dst = xdtT[:, b0:b0 + 4, :].rearrange(
                    "p b (h o) -> p (b h) o", h=2)
                mult = tb2[:, b0 * 2:b0 * 2 + 8].rearrange(
                    "p (h o) -> p h o", o=1).broadcast_to([128, 8, 64])
                V.c("tensor_tensor", dst, dst, mult, ALU.mult,
                    then=(s_v.inc() if g == 7 else ()))
            v_xdtd = s_v.n

            G.w(s_v, v_dexp2)
            G.c("dma_start", out=st_sp[gc], in_=s_state[:].rearrange(
                "p a b -> p (a b)"), then=s_stp.inc(16))
            cs_done = {}
            for grp in range(8):
                bank = pbk[4 + grp % 2]
                T.w(s_v, v_xdtd)
                T.w(s_v, v_r0)
                if grp == 0:
                    T.w(s_v, v_dexp2)
                if grp >= 2:
                    T.w(s_v, s_v.n)
                for slotj in range(4):
                    blk = grp * 4 + slotj
                    T.c("matmul", bank[:, slotj * 128:(slotj + 1) * 128],
                        xdtT[:, blk, :], bt_sb[:], start=True, stop=True,
                        then=(s_t.inc() if slotj == 3 else ()))
                cs_done[grp] = s_t.n
                V.w(s_t, cs_done[grp])
                V.w(s_stp, s_stp.n)
                for slotj in range(4):
                    blk = grp * 4 + slotj
                    V.c("scalar_tensor_tensor", s_state[:, blk, :],
                        s_state[:, blk, :], dexp2_sb[:, blk:blk + 1],
                        bank[:, slotj * 128:(slotj + 1) * 128],
                        ALU.mult, ALU.add,
                        then=(s_v.inc() if slotj == 3 else ()))
        v_ssda = s_v.n
        if p == 0:
            wprep_step(128)
            wprep_finalize()

        if stop_after == "ssda":
            if "d_dt" in dout:
                SY.w(s_v, v_dt)
                SY.c("dma_start", out=dout["d_dt"][:, c0:c0 + pass_t],
                     in_=dtA[0:64, 0:pass_t], then=s_dbg.inc(16))
            return

        scr1b = scr1[:].bitcast(F16)
        scr2b = scr2[:].bitcast(F16)
        V.w(s_yst, s_yst.n)
        V.w(s_zs0, s_zs0.n)
        V.w(s_zs1, s_zs1.n)
        V.w(s_xq0, s_xq0.n)
        V.w(s_xq1, s_xq1.n)
        V.w(s_cs, s_cs.n)
        SC.w(s_cs, s_cs.n)
        nfb = pass_t // fb

        def col_stats_to_row(src_scr, op, flat_off, v_ready):
            nj = fb // 128
            T.w(s_v, v_ready)
            for j in range(nj):
                T.c("transpose", pbk[5][:, j * 128:(j + 1) * 128],
                    src_scr[:, j * 128:(j + 1) * 128], c_idf[:],
                    then=(s_t.inc() if j == nj - 1 else ()))
            V.w(s_t, s_t.n)
            for j in range(nj):
                V.c("tensor_reduce", stat4[:, j:j + 1],
                    pbk[5][:, j * 128:(j + 1) * 128], AX.X, op,
                    then=(s_v.inc() if j == nj - 1 else ()))
            T.w(s_v, s_v.n)
            T.c("transpose", pbk[4][0:nj, 0:128], stat4[:, 0:nj],
                c_idf[:], then=s_t.inc())
            V.w(s_t, s_t.n)
            V.c("tensor_copy", stat4T[0:nj, :], pbk[4][0:nj, 0:128],
                then=s_v.inc())
            G.w(s_v, s_v.n)
            G.c("dma_start", out=flat_sb[0:1, flat_off:flat_off + fb],
                in_=stat4T[0:nj, :], then=s_gb.inc(16))

        def bcast_row(row_ap, slot, v_ready):
            T.w(s_v, v_ready)
            T.c("matmul", pbk[4][:, 0:fb], ones_r[:], row_ap,
                start=True, stop=True, then=s_t.inc())
            V.w(s_t, s_t.n)
            V.c("tensor_copy", ffnbc[:, slot, 0:fb], pbk[4][:, 0:fb],
                then=s_v.inc())

        for fbi in range(nfb):
            fbg = p * nfb + fbi
            fc = 3 + fbi * fb
            gf = c0 + fbi * fb

            nj2 = fb // 128
            jb = 8 + fbi * nj2
            T.w(s_v, v_ssda)
            T.c("transpose", pbk[4][0:nj2, 0:128], stat4[:, jb:jb + nj2],
                c_idf[:], then=s_t.inc())
            V.w(s_t, s_t.n)
            V.c("tensor_copy", stat4T[0:nj2, :], pbk[4][0:nj2, 0:128],
                then=s_v.inc())
            G.w(s_t, s_t.n)
            G.w(s_v, s_v.n)
            G.c("dma_start", out=flat_sb[0:1, 0:fb], in_=stat4T[0:nj2, :],
                then=s_gb.inc(16))
            V.w(s_gb, s_gb.n)
            V.c("tensor_scalar_max", flat_sb[0:1, 0:fb], flat_sb[0:1, 0:fb],
                EPS)
            V.c("tensor_scalar_mul", flat_sb[0:1, 0:fb], flat_sb[0:1, 0:fb],
                1.0 / 127.0)
            V.c("tensor_tensor", flat_sb[0:1, 0:fb], flat_sb[0:1, 0:fb],
                scr3[0:1, 1024:1025].broadcast_to([1, fb]), ALU.mult,
                then=s_v.inc())
            bcast_row(flat_sb[0:1, 0:fb], 1, s_v.n)
            v_bc01 = s_v.n

            SY.w(s_xq0, s_xq0.n)
            SY.w(s_xq1, s_xq1.n)
            SY.w(s_v, v_ssda)
            SY.w(s_t, s_t.n)
            SY.c("dma_start", out=xq_sb[:, :, 0:fb],
                 in_=xq_sp.rearrange("k q t -> q k t")[:, :, gf:gf + fb],
                 then=s_xql.inc(16))
            v_xq = s_xql.n
            if "d_xq" in dout:
                SY.w(s_xql, s_xql.n)
                SY.c("dma_start",
                     out=dout["d_xq"].rearrange("k q t -> q k t")[:, :, gf:gf + fb],
                     in_=xq_sb[:, :, 0:fb], then=s_dbg.inc(16))

            p_rdy = {}
            mmd = {}
            evd = {}

            arCf = arC[:]
            w2half = {"v": 0}

            def ffn_panel(mi, which):
                if which == "w2":
                    if fbg == 0:
                        buf = wp_a
                    else:
                        buf = wp_a if mi % 2 == 0 else wb2
                else:
                    if fbg == 0:
                        buf = wp_a
                    else:
                        buf = wp_a if mi % 2 == 0 else wp_b
                sem = s_wpc if mi % 2 == 0 else s_wpd
                wbsem = s_wbc if mi % 2 == 0 else s_wbd
                if which == "w1":
                    dep, raw, cache = KH, w1_p, wbf_1
                else:
                    dep, raw, cache = KF, w2_p, wbf_2
                idx = mi if which == "w1" else mi - KF
                if fbg == 0:
                    if mi - 1 in mmd:
                        SY.w(s_t, mmd[mi - 1])
                elif mi - 2 in mmd:
                    SY.w(s_t, mmd[mi - 2])
                src_f = raw[idx].rearrange("(k q) c -> q k c", q=128)
                src_b = cache[idx].rearrange("(k q) c -> q k c", q=128)
                if fbg > 0:
                    SY.w(s_wbc, s_wbc.n)
                    SY.w(s_wbd, s_wbd.n)
                    SY.w(s_gsp, s_gsp.n)
                    if which == "w2" and mi % 2 == 1:
                        SY.w(s_t, mmd.get(KF - 1, 0))
                        SY.w(s_v, s_v.n)
                    SY.c("dma_start", out=buf[:, 0:dep, :], in_=src_b,
                        then=sem.inc(16))
                    p_rdy[mi] = (sem, sem.n)
                    return
                tcol = 0 if which == "w1" else 2
                SY.w(s_wbc, s_wbc.n)
                SY.w(s_wbd, s_wbd.n)
                if which == "w1":
                    SY.w(s_v, s_v.n)
                bflat = buf[:].rearrange("q a b -> q (a b)")
                if which == "w1":
                    fst = bflat.bitcast(F32)[:, 2048:4096].rearrange(
                        "q (k c) -> q k c", c=128)
                    SY.c("dma_start", out=fst, in_=src_f, then=sem.inc(16))
                    V.w(sem, sem.n)
                    V.w(s_t, s_t.n)
                    fsl = bflat.bitcast(F32)[:, 2048:4096]
                    V.c("tensor_scalar", scr1b[:, 0:2048], fsl,
                        thr_sb[:, tcol:tcol + 1], None, ALU.is_gt)
                    V.c("tensor_scalar", scr2b[:, 0:2048], fsl,
                        thr_sb[:, tcol + 1:tcol + 2], None, ALU.is_lt)
                    V.c("tensor_tensor", bflat[:, 0:2048], scr1b[:, 0:2048],
                        scr2b[:, 0:2048], ALU.subtract, then=s_v.inc())
                else:
                    for hf in range(4):
                        slot2 = hf % 2
                        hsem = s_pre0 if slot2 == 0 else s_pre1
                        SY.w(s_v, w2half["v"])
                        SY.w(s_t, mmd.get(KF - 1, 0))
                        SY.w(s_dbg, s_dbg.n)
                        SY.c("dma_start",
                             out=arCf[:, slot2 * 2048:(slot2 + 1) * 2048].rearrange(
                                 "q (k c) -> q k c", c=128),
                             in_=src_f[:, hf * 16:(hf + 1) * 16, :],
                             then=hsem.inc(16))
                        V.w(hsem, hsem.n)
                        V.w(s_t, s_t.n)
                        fsl = arCf[:, slot2 * 2048:(slot2 + 1) * 2048]
                        V.c("tensor_scalar", scr1b[:, 0:2048], fsl,
                            thr_sb[:, tcol:tcol + 1], None, ALU.is_gt)
                        V.c("tensor_scalar", scr2b[:, 0:2048], fsl,
                            thr_sb[:, tcol + 1:tcol + 2], None, ALU.is_lt)
                        V.c("tensor_tensor",
                            bflat[:, hf * 2048:hf * 2048 + 2048],
                            scr1b[:, 0:2048], scr2b[:, 0:2048], ALU.subtract,
                            then=s_v.inc())
                        w2half["v"] = s_v.n
                p_rdy[mi] = ("v", s_v.n)
                SY.w(s_v, s_v.n)
                SY.c("dma_start", out=src_b, in_=buf[:, 0:dep, :],
                     then=wbsem.inc(16))

            t_sq_mm = {}
            if fbg > 0:
                ffn_panel(0, "w1")
            for m in range(KF):
                if fbg == 0:
                    ffn_panel(m, "w1")
                elif m + 1 < KF:
                    ffn_panel(m + 1, "w1")
                if fbg == 0:
                    buf = wp_a
                else:
                    buf = wp_a if m % 2 == 0 else wp_b
                bank = pbk[m % 2]
                sr, sv = p_rdy[m]
                if sr == "v":
                    T.w(s_v, sv)
                else:
                    T.w(sr, sv)
                T.w(s_xql, v_xq)
                T.w(s_v, v_bc01)
                if m - 2 in evd:
                    T.w(s_v, evd[m - 2])
                for k in range(KH):
                    T.c("matmul", bank[:, 0:fb], buf[:, k, 0:128],
                        xq_sb[:, k, :], start=(k == 0), stop=(k == KH - 1),
                        then=(s_t.inc() if k == KH - 1 else ()))
                mmd[m] = s_t.n
                V.w(s_t, mmd[m])
                V.w(s_sc, s_sc.n)
                V.c("tensor_tensor", ug_sb[:, 0:fb], bank[:, 0:fb],
                    ffnbc[:, 1, 0:fb], ALU.mult, then=s_v.inc())
                evd[m] = s_v.n
                SC.w(s_v, evd[m])
                SC.c("activation", h_sb[:, m, :], ug_sb[:, 0:fb], AF.Gelu,
                     bias=b1_sb[:, m:m + 1], then=s_sc.inc())
                V.w(s_sc, s_sc.n)
                if m > 0:
                    V.w(s_t, t_sq_mm[m - 1])
                V.c("tensor_tensor", sq_sb[:, 0:fb], h_sb[:, m, :],
                    h_sb[:, m, :], ALU.mult, then=s_v.inc())
                v_sq = s_v.n
                T.w(s_sc, s_sc.n)
                T.c("matmul", pbk[2][0:1, 0:fb], ones_cb[:], h_sb[:, m, :],
                    start=(m == 0), stop=(m == KF - 1))
                T.w(s_v, v_sq)
                T.c("matmul", pbk[3][0:1, 0:fb], ones_cbb[:], sq_sb[:, 0:fb],
                    start=(m == 0), stop=(m == KF - 1), then=s_t.inc())
                t_sq_mm[m] = s_t.n
            t_hstats = s_t.n

            V.w(s_t, t_hstats)
            V.c("tensor_scalar_mul", scr3[0:1, 0:fb], pbk[2][0:1, 0:fb],
                1.0 / FF)
            V.c("tensor_scalar_mul", scr3[0:1, 512:512 + fb],
                pbk[3][0:1, 0:fb], 1.0 / FF)
            V.c("tensor_tensor", dt_raw[0:1, 0:fb], scr3[0:1, 0:fb],
                scr3[0:1, 0:fb], ALU.mult)
            V.c("tensor_tensor", scr3[0:1, 512:512 + fb],
                scr3[0:1, 512:512 + fb], dt_raw[0:1, 0:fb],
                ALU.subtract, then=s_v.inc())
            SC.w(s_v, s_v.n)
            SC.c("activation", dt_raw[0:1, 0:fb], scr3[0:1, 512:512 + fb],
                 AF.Sqrt, bias=cst[0:1, 0:1], then=s_sc.inc())
            V.w(s_sc, s_sc.n)
            V.c("reciprocal", flat_sb[0:1, 512:512 + fb],
                dt_raw[0:1, 0:fb])
            V.c("tensor_tensor", scr3[0:1, 0:fb], scr3[0:1, 0:fb],
                flat_sb[0:1, 512:512 + fb], ALU.mult)
            V.c("tensor_scalar_mul", scr3[0:1, 0:fb], scr3[0:1, 0:fb],
                -1.0, then=s_v.inc())
            bcast_row(flat_sb[0:1, 512:512 + fb], 2, s_v.n)
            bcast_row(scr3[0:1, 0:fb], 3, s_v.n)
            v_bc23 = s_v.n

            for m in range(KF):
                stg = scr1b if m % 2 == 0 else scr2b
                V.c("tensor_tensor", stg[:, 0:fb], h_sb[:, m, :],
                    ffnbc[:, 2, 0:fb], ALU.mult)
                V.c("tensor_tensor", h_sb[:, m, :], stg[:, 0:fb],
                    ffnbc[:, 3, 0:fb], ALU.add, then=s_v.inc())
            v_h2 = s_v.n
            SC.w(s_v, v_h2)
            for m in range(KF):
                SC.c("activation", h_sb[:, m, :], h_sb[:, m, :],
                     AF.Identity, bias=be_sb[:, m:m + 1],
                     scale=ga_sb[:, m:m + 1],
                     then=(s_sc.inc() if m == KF - 1 else ()))
            sc_hn = s_sc.n
            V.w(s_sc, sc_hn)
            if "d_h" in dout:
                SY.w(s_sc, sc_hn)
                SY.c("dma_start",
                     out=dout["d_h"].rearrange("k q t -> q k t")[:, :, gf:gf + fb],
                     in_=h_sb[:], then=s_dbg.inc(16))

            hsw = h_sb[:, :, 0:fb].rearrange("q k t -> q t k")
            V.w(s_gc, s_gc.n)
            V.c("tensor_reduce", scr3[:, 0:fb], hsw, AX.X, ALU.max)
            V.c("tensor_reduce", scr2[:, 0:fb], hsw, AX.X, ALU.min)
            V.c("scalar_tensor_tensor", scr3[:, 0:fb], scr2[:, 0:fb],
                -1.0, scr3[:, 0:fb], ALU.mult, ALU.max, then=s_v.inc())
            col_stats_to_row(scr3, ALU.max, 0, s_v.n)
            V.w(s_gb, s_gb.n)
            V.c("tensor_scalar_max", scr3[0:1, 0:fb], flat_sb[0:1, 0:fb], EPS)
            V.c("reciprocal", scr3[0:1, 512:512 + fb], scr3[0:1, 0:fb])
            V.c("tensor_scalar_mul", flat_sb[0:1, 512:512 + fb],
                scr3[0:1, 512:512 + fb], 127.0)
            V.c("tensor_scalar_mul", scr3[0:1, 0:fb], scr3[0:1, 0:fb],
                1.0 / 127.0)
            V.c("tensor_tensor", flat_sb[0:1, 0:fb],
                scr3[0:1, 0:fb],
                scr3[0:1, 1025:1026].broadcast_to([1, fb]), ALU.mult,
                then=s_v.inc())
            bcast_row(flat_sb[0:1, 512:512 + fb], 4, s_v.n)
            bcast_row(flat_sb[0:1, 0:fb], 5, s_v.n)

            if "d_h" in dout:
                V.w(s_dbg, s_dbg.n)
            for m in range(KF):
                stg = scr1 if m % 2 == 0 else scr2
                V.c("tensor_tensor", stg[:, 0:fb], h_sb[:, m, :],
                    ffnbc[:, 4, 0:fb], ALU.mult)
                V.c("tensor_scalar", h_sb[:, m, :], stg[:, 0:fb], MAGIC,
                    -MAGIC, ALU.add, ALU.add,
                    then=(s_v.inc() if m == KF - 1 else ()))
            v_hq = s_v.n

            evd2 = {}
            if fbg > 0:
                ffn_panel(KF + 0, "w2")
            for m in range(KH):
                if fbg == 0:
                    ffn_panel(KF + m, "w2")
                elif m + 1 < KH:
                    ffn_panel(KF + m + 1, "w2")
                if fbg == 0:
                    buf = wp_a
                else:
                    buf = wp_a if m % 2 == 0 else wb2
                bank = pbk[m % 2]
                sr, sv = p_rdy[KF + m]
                if sr == "v":
                    T.w(s_v, sv)
                else:
                    T.w(sr, sv)
                T.w(s_v, v_hq)
                if m - 2 >= 0 and (m - 2) in evd2:
                    T.w(s_v, evd2[m - 2])
                for k in range(KF):
                    T.c("matmul", bank[:, 0:fb], buf[:, k, 0:128],
                        h_sb[:, k, :], start=(k == 0), stop=(k == KF - 1),
                        then=(s_t.inc() if k == KF - 1 else ()))
                mmd[KF + m] = s_t.n
                V.w(s_t, mmd[KF + m])
                V.w(s_sc, s_sc.n)
                V.c("tensor_tensor", ug_sb[:, 0:fb], bank[:, 0:fb],
                    ffnbc[:, 5, 0:fb], ALU.mult, then=s_v.inc())
                evd2[m] = s_v.n
                SC.w(s_v, evd2[m])
                SC.w(s_gsp, s_gsp.n)
                SC.c("activation", gstage[:, m % 2, 0:fb], ug_sb[:, 0:fb],
                     AF.Silu, bias=b2_sb[:, m:m + 1], then=s_sc.inc())
                SY.w(s_sc, s_sc.n)
                SY.c("dma_start", out=g_sp[m, :, gf:gf + fb],
                     in_=gstage[:, m % 2, 0:fb], then=s_gsp.inc(16))

        if "d_dt" in dout:
            SY.w(s_v, v_dt)
            SY.c("dma_start", out=dout["d_dt"][:, c0:c0 + pass_t],
                 in_=dtA[0:64, 0:pass_t], then=s_dbg.inc(16))
        if "d_xbc" in dout:
            SY.w(s_sc, sc_conv)
            SY.c("dma_start",
                 out=dout["d_xbc"].rearrange("k q t -> q k t")[:, :, c0:c0 + pass_t],
                 in_=big1[:, :, 3:Te], then=s_dbg.inc(16))

    gate = {}
    for p in range(n_pass):
        stage_a(p)
        gate = {"t": s_t.n, "v": s_v.n, "sc": s_sc.n}
        if stop_after == "inproj" and p == 0:
            break

    if stop_after not in ("inproj", "conv", "ssda", "ffn"):
        scr1b = scr1[:].bitcast(BF16)
        V.w(s_sc, s_sc.n)
        V.c("tensor_reduce", smallv[:, 5:6], cend_sb[:, 0:n_chunks], AX.X,
            ALU.add)
        V.c("tensor_tensor_scan", cend_sb[:, 17:17 + n_chunks],
            ones64[:, 0:n_chunks], cend_sb[:, 0:n_chunks], 0.0,
            ALU.mult, ALU.add)
        V.c("memset", cend_sb[:, 16:17], 0.0)
        V.c("memset", scr3[:, 0:2], 0.0)
        V.c("tensor_copy", scr3[0:64, 0:1], smallv[:, 5:6], then=s_v.inc())
        v_cc = s_v.n
        G.w(s_v, v_cc)
        G.c("dma_start", out=cc_in[:, 0:DIN],
            in_=s_state[:].rearrange("p a b -> p (a b)"), then=s_gb.inc(16))
        G.c("dma_start", out=cc_in[:, DIN:DIN + 2], in_=scr3[:, 0:2],
            then=s_gb.inc(16))
        G.w(s_gb, s_gb.n)
        G.c("collective_compute", "AllGather", ALU.bypass,
            replica_groups=(rg if rg is not None else [[0, 1, 2, 3], [4, 5, 6, 7]]),
            ins=[cc_in[:]], outs=[cc_out[:]], then=s_cc.inc())
        cc_done = s_cc.n

        V.w(s_cc, cc_done)
        V.c("memset", s_state[:], 0.0, then=s_v.inc())
        for j in range(4):
            SY.w(s_cc, cc_done)
            if j >= 1:
                SY.w(s_v, s_v.n)
            SY.w(s_dbg, s_dbg.n)
            SY.c("dma_start", out=ccld[:], in_=cc_out[j], then=s_ld2.inc(16))
            V.w(s_ld2, s_ld2.n)
            V.c("tensor_scalar", smallv[:, 5:6], ccld[0:64, DIN:DIN + 1],
                dmask_sb[:, j:j + 1], None, ALU.mult, then=s_v.inc())
            SC.w(s_v, s_v.n)
            SC.c("activation", smallv[:, 6:7], smallv[:, 5:6], AF.Exp,
                 then=s_sc.inc())
            V.w(s_sc, s_sc.n)
            expand_hp(smallv[:, 6:7], dexp2_sb[:], 4)
            for g8 in range(8):
                V.c("tensor_scalar",
                    scr1[:, 0:512], ccld[:, g8 * 512:(g8 + 1) * 512],
                    amask_sb[:, j:j + 1], None, ALU.mult)
                for s4 in range(4):
                    blk = g8 * 4 + s4
                    V.c("scalar_tensor_tensor", s_state[:, blk, :],
                        s_state[:, blk, :], dexp2_sb[:, blk:blk + 1],
                        scr1[:, s4 * 128:(s4 + 1) * 128], ALU.mult, ALU.add,
                        then=(s_v.inc() if s4 == 3 else ()))
        v_sin = s_v.n
        if "d_sin" in dout:
            SY.w(s_v, v_sin)
            SY.c("dma_start", out=dout["d_sin"][:],
                 in_=s_state[:].rearrange("p a b -> p (a b)"),
                 then=s_dbg.inc(16))

        wo_rdy = {}
        wo_mmd = {}

        def wout_panel(m, firstb):
            buf = wp_a if m % 2 == 0 else wp_b
            wbsem = s_wba if m % 2 == 0 else s_wbb
            src_f = wout_p[m].rearrange("(k q) c -> q k c", q=128)
            src_b = wbf_o[m].rearrange("(k q) c -> q k c", q=128)
            if m - 2 in wo_mmd:
                (G if firstb else SY).w(s_t, wo_mmd[m - 2])
            else:
                (G if firstb else SY).w(s_t, s_t.n)
            if firstb:
                G.w(wbsem, wbsem.n)
                G.w(s_wbc, s_wbc.n)
                G.w(s_wbd, s_wbd.n)
                G.w(s_gsp, s_gsp.n)
                G.c("dma_start", out=buf[:, 0:KD, :], in_=src_f,
                    then=s_wpa.inc(16) if m % 2 == 0 else s_wpb.inc(16))
                sem = s_wpa if m % 2 == 0 else s_wpb
                wo_rdy[m] = (sem, sem.n)
                G.w(sem, sem.n)
                G.c("dma_start", out=src_b, in_=buf[:, 0:KD, :],
                    then=wbsem.inc(16))
            else:
                SY.w(s_wba, s_wba.n)
                SY.w(s_wbb, s_wbb.n)
                SY.w(s_wbc, s_wbc.n)
                SY.w(s_wbd, s_wbd.n)
                SY.w(s_gsp, s_gsp.n)
                sem = s_wpc if m % 2 == 0 else s_wpd
                SY.c("dma_start", out=buf[:, 0:KD, :], in_=src_b,
                    then=sem.inc(16))
                wo_rdy[m] = (sem, sem.n)

        ld_free = {"yp": v_sin, "z": v_sin, "st": v_sin}
        for p2 in range(n_pass):
            c0 = p2 * pass_t
            if "d_ynorm" in dout:
                V.w(s_dbg, s_dbg.n)
            for ci in range(nch_pass):
                gc = p2 * nch_pass + ci
                gcol = gc * CH
                SY.w(s_v, ld_free["st"])
                SY.w(s_stp, s_stp.n)
                SY.c("dma_start", out=pb_ld[:],
                     in_=st_sp[gc].rearrange("q (a b) -> q a b", a=KD),
                     then=s_ld2.inc(16))
                r_st = s_ld2.n
                SY.w(s_v, ld_free["z"])
                SY.w(s_sc, ld_free.get("zsc", 0))
                SY.c("dma_start", out=pb_ld2[:],
                     in_=z_sp.rearrange("a q t -> q a t")[:, :, gcol:gcol + CH],
                     then=s_zs0.inc(16))
                r_z = s_zs0.n
                SY.w(s_v, ld_free["yp"])
                SY.c("dma_start", out=pb_ld3[:],
                     in_=yp_sp.rearrange("a q t -> q a t")[:, :, gcol:gcol + CH],
                     then=s_yst.inc(16))
                r_yp = s_yst.n
                SY.w(s_t, ld_free.get("ct", 0))
                SY.w(s_cs, s_cs.n)
                SY.c("dma_start", out=c_ld[:], in_=c_sp[gc],
                     then=s_cs.inc(16))
                r_c = s_cs.n

                V.w(s_v, v_sin)
                V.c("tensor_copy", smallv[:, 5:6], cend_sb[:, 16 + gc:17 + gc],
                    then=s_v.inc())
                SC.w(s_v, s_v.n)
                SC.c("activation", smallv[:, 6:7], smallv[:, 5:6], AF.Exp,
                     then=s_sc.inc())
                V.w(s_sc, s_sc.n)
                expand_hp(smallv[:, 6:7], dexp2_sb[:], 4)

                V.w(s_ld2, r_st)
                for blk in range(KD):
                    V.c("scalar_tensor_tensor", stot[:, blk, :],
                        s_state[:, blk, :], dexp2_sb[:, blk:blk + 1],
                        pb_ld[:, blk, :], ALU.mult, ALU.add,
                        then=(s_v.inc() if blk == KD - 1 else ()))
                v_stot = s_v.n

                for tg in range(8):
                    bank = pbk[4 + tg % 2]
                    T.w(s_v, v_stot)
                    if tg >= 2:
                        T.w(s_v, s_v.n)
                    for s4 in range(4):
                        blk = tg * 4 + s4
                        T.c("transpose", bank[:, s4 * 128:(s4 + 1) * 128],
                            stot[:, blk, :], c_idf[:],
                            then=(s_t.inc() if s4 == 3 else ()))
                    V.w(s_t, s_t.n)
                    for s4 in range(4):
                        blk = tg * 4 + s4
                        V.c("tensor_copy", stotT[:, blk, :],
                            bank[:, s4 * 128:(s4 + 1) * 128],
                            then=(s_v.inc() if s4 == 3 else ()))
                v_stT = s_v.n

                SC.w(s_v, v_stT)
                SC.c("activation", dcb[:, 0:CH], acs_sb[0:64, gcol:gcol + CH],
                     AF.Exp, then=s_sc.inc())
                sc_ea = s_sc.n

                v_grp = 0
                for g8 in range(8):
                    b_ea = pbk[2]
                    b_yo = pbk[3]
                    T.w(s_sc, sc_ea)
                    T.w(s_v, v_stT)
                    T.w(s_cs, r_c)
                    if g8 >= 1:
                        T.w(s_v, v_grp)
                    for s4 in range(4):
                        blk = g8 * 4 + s4
                        T.c("matmul", b_ea[:, s4 * 128:(s4 + 1) * 128],
                            c_hsel[:, blk * 128:(blk + 1) * 128],
                            dcb[:, 0:CH], start=True, stop=True)
                    for s4 in range(4):
                        blk = g8 * 4 + s4
                        T.c("matmul", b_yo[:, s4 * 128:(s4 + 1) * 128],
                            stotT[:, blk, :], c_ld[:],
                            start=True, stop=True,
                            then=(s_t.inc() if s4 == 3 else ()))
                    t_grp = s_t.n
                    V.w(s_t, t_grp)
                    V.w(s_yst, r_yp)
                    V.w(s_zs0, r_z)
                    V.c("tensor_copy", ea2_sb[:, 0:512], b_ea[:, 0:512],
                        then=s_v.inc())
                    SC.w(s_v, s_v.n)
                    for s4 in range(4):
                        blk = g8 * 4 + s4
                        SC.c("activation", pb_ld2[:, blk, :],
                             pb_ld2[:, blk, :], AF.Silu,
                             then=(s_sc.inc() if s4 == 3 else ()))
                    sc_z = s_sc.n
                    V.c("tensor_tensor", scr1[:, 0:512],
                        b_yo[:, 0:512],
                        ea2_sb[:, 0:512], ALU.mult)
                    V.c("tensor_tensor", scr1[:, 0:512], scr1[:, 0:512],
                        pb_ld3[:, g8 * 4:(g8 + 1) * 4, :].rearrange(
                            "q a b -> q (a b)"), ALU.add)
                    V.w(s_sc, sc_z)
                    V.c("tensor_tensor",
                        pb_ld3[:, g8 * 4:(g8 + 1) * 4, :].rearrange(
                            "q a b -> q (a b)"),
                        scr1[:, 0:512],
                        pb_ld2[:, g8 * 4:(g8 + 1) * 4, :].rearrange(
                            "q a b -> q (a b)"), ALU.mult)
                    V.c("tensor_tensor", scr1b[:, 0:512],
                        pb_ld3[:, g8 * 4:(g8 + 1) * 4, :].rearrange(
                            "q a b -> q (a b)"),
                        pb_ld3[:, g8 * 4:(g8 + 1) * 4, :].rearrange(
                            "q a b -> q (a b)"), ALU.mult,
                        then=s_v.inc())
                    v_grp = s_v.n
                    T.w(s_v, v_grp)
                    for s4 in range(4):
                        T.c("matmul", pbk[0][0:1, 0:CH], ones_cbb[:],
                            scr1b[:, s4 * 128:(s4 + 1) * 128],
                            start=(g8 == 0 and s4 == 0),
                            stop=(g8 == 7 and s4 == 3),
                            then=(s_t.inc() if (g8 == 7 and s4 == 3) else ()))
                t_ssq = s_t.n
                ld_free["z"] = s_v.n
                ld_free["zsc"] = s_sc.n
                ld_free["st"] = v_stot
                ld_free["ct"] = s_t.n

                V.w(s_t, t_ssq)
                V.c("tensor_scalar_mul", scr3[0:1, 0:CH], pbk[0][0:1, 0:CH],
                    1.0 / DIN, then=s_v.inc())
                SC.w(s_v, s_v.n)
                SC.c("activation", scr3[0:1, 256:256 + CH], scr3[0:1, 0:CH],
                     AF.Sqrt, bias=cst[0:1, 0:1], then=s_sc.inc())
                V.w(s_sc, s_sc.n)
                V.c("reciprocal", scr3[0:1, 0:CH], scr3[0:1, 256:256 + CH],
                    then=s_v.inc())
                T.w(s_v, s_v.n)
                T.c("matmul", pbk[1][:, 0:CH], ones_r[:], scr3[0:1, 0:CH],
                    start=True, stop=True, then=s_t.inc())
                t_rs = s_t.n
                V.w(s_t, t_rs)
                for blk in range(KD):
                    V.c("scalar_tensor_tensor", ynb[:, blk, ci * CH:(ci + 1) * CH],
                        pb_ld3[:, blk, :], nw_sb[:, blk:blk + 1],
                        pbk[1][:, 0:CH], ALU.mult, ALU.mult,
                        then=(s_v.inc() if blk == KD - 1 else ()))
                ld_free["yp"] = s_v.n
            v_ynb = s_v.n
            if "d_ynorm" in dout:
                SY.w(s_v, v_ynb)
                SY.c("dma_start",
                     out=dout["d_ynorm"].rearrange("k q t -> q k t")[:, :, c0:c0 + pass_t],
                     in_=ynb[:], then=s_dbg.inc(16))

            firstb = (p2 == 0)
            SY.w(s_dbg, s_dbg.n)
            SY.w(s_gsp, s_gsp.n)
            wout_panel(0, firstb)
            ev_out = {}
            for m in range(KH):
                if m + 1 < KH:
                    wout_panel(m + 1, firstb)
                buf = wp_a if m % 2 == 0 else wp_b
                sem, sv = wo_rdy[m]
                nslc = nslices(pass_t, 512)
                for nb, (oo, wnb) in enumerate(nslc):
                    bank = pbk[(m + nb) % 2]
                    T.w(sem, sv)
                    T.w(s_v, v_ynb)
                    prev = (m * len(nslc) + nb) - 2
                    if prev >= 0 and prev in ev_out:
                        T.w(s_v, ev_out[prev])
                    for k in range(KD):
                        T.c("matmul", bank[:, 0:wnb], buf[:, k, 0:128],
                            ynb[:, k, oo:oo + wnb], start=(k == 0),
                            stop=(k == KD - 1),
                            then=(s_t.inc() if k == KD - 1 else ()))
                    wo_mmd[m] = s_t.n
                    SY.w(s_v, ev_out.get(prev, 0))
                    SY.c("dma_start", out=gxl[:, m % 2, 0:wnb],
                         in_=g_sp[m, :, c0 + oo:c0 + oo + wnb],
                         then=s_gl.inc(16))
                    r_g = s_gl.n
                    SY.c("dma_start", out=outst[:, m % 2, 0:wnb],
                         in_=x_fm[m * 128:(m + 1) * 128,
                                  3 + c0 + oo:3 + c0 + oo + wnb],
                         then=s_x.inc(16))
                    r_x = s_x.n
                    V.w(s_t, wo_mmd[m])
                    V.w(s_gl, r_g)
                    V.w(s_x, r_x)
                    V.w(s_o, s_o.n)
                    V.c("tensor_tensor", scr1[:, 0:wnb], bank[:, 0:wnb],
                        gxl[:, m % 2, 0:wnb], ALU.mult)
                    V.c("tensor_tensor", scr2[:, 0:wnb], scr1[:, 0:wnb],
                        outst[:, m % 2, 0:wnb], ALU.add, then=s_v.inc())
                    ev_out[m * len(nslc) + nb] = s_v.n
                    SY.w(s_v, s_v.n)
                    SY.c("dma_start",
                         out=out_fm[m * 128:(m + 1) * 128, c0 + oo:c0 + oo + wnb],
                         in_=scr2[:, 0:wnb], then=s_o.inc(16))

    if "d_z" in dout:
        zt = scr1[:].bitcast(F16)
        SY.w(s_zs0, s_zs0.n)
        SY.w(s_zs1, s_zs1.n)
        for kk in range(KD):
            SY.c("dma_start", out=zt[:, 0:t_core], in_=z_sp[kk],
                 then=s_dbg.inc(16))
            SY.w(s_dbg, s_dbg.n)
            SY.c("dma_start", out=dout["d_z"][kk], in_=zt[:, 0:t_core],
                 then=s_dbg.inc(16))
            SY.w(s_dbg, s_dbg.n)

    if "d_g" in dout:
        zt3 = scr1[:].bitcast(F16)
        SY.w(s_gsp, s_gsp.n)
        for kk in range(KH):
            SY.c("dma_start", out=zt3[:, 0:t_core], in_=g_sp[kk],
                 then=s_dbg.inc(16))
            SY.w(s_dbg, s_dbg.n)
            SY.c("dma_start", out=dout["d_g"][kk], in_=zt3[:, 0:t_core],
                 then=s_dbg.inc(16))
            SY.w(s_dbg, s_dbg.n)

    if "d_acs" in dout:
        SY.w(s_v, s_v.n)
        SY.c("dma_start", out=dout["d_acs"][:, :], in_=acs_sb[0:64, :],
             then=s_dbg.inc(16))
    if "d_state" in dout:
        G.w(s_v, s_v.n)
        G.c("dma_start", out=dout["d_state"][:, 0:DIN],
            in_=s_state[:].rearrange("p a b -> p (a b)"), then=s_gb.inc(16))
        V.c("memset", scr3[:, 0:2], 0.0, then=s_v.inc())
        SY.w(s_v, s_v.n)
        SY.c("dma_start", out=dout["d_state"][:, DIN:DIN + 2],
             in_=scr3[:, 0:2], then=s_dbg.inc(16))
    if "d_yp" in dout:
        zt2 = scr1[:].bitcast(F16)
        SY.w(s_yst, s_yst.n)
        for kk in range(KD):
            SY.c("dma_start", out=zt2[:, 0:t_core], in_=yp_sp[kk],
                 then=s_dbg.inc(16))
            SY.w(s_dbg, s_dbg.n)
            SY.c("dma_start", out=dout["d_yp"][kk], in_=zt2[:, 0:t_core],
                 then=s_dbg.inc(16))
            SY.w(s_dbg, s_dbg.n)

    if stop_after in ("inproj", "conv", "ssda", "ffn"):
        V.c("memset", scr2[:, 0:512], 0.0, then=s_v.inc())
        SY.w(s_v, s_v.n)
        SY.w(s_dbg, s_dbg.n)
        for k in range(KH):
            for o, w in nslices(t_core):
                SY.c("dma_start",
                     out=out_fm.rearrange("(k q) t -> q k t", q=128)[:, k, o:o + w],
                     in_=scr2[:, 0:w], then=s_o.inc(16))
    SY.w(s_o, s_o.n)
    SY.w(s_dbg, s_dbg.n)
    G.w(s_gb, s_gb.n)

    with nc.Block() as block:
        block.tensor(lambda e: T.replay(e))
        block.vector(lambda e: V.replay(e))
        block.scalar(lambda e: SC.replay(e))
        block.gpsimd(lambda e: G.replay(e))
        block.sync(lambda e: SY.replay(e))

    ctx.close()
    return nc



def stage_weight_panels(w):
    outF, inF = w.shape
    n_p = cdiv(outF, 128)
    pad = n_p * 128 - outF
    wt = np.ascontiguousarray(w.T)
    if pad:
        wt = np.concatenate([wt, np.zeros((inF, pad), w.dtype)], 1)
    return np.ascontiguousarray(wt.reshape(inF, n_p, 128).transpose(1, 0, 2))


def stage_all(inputs, t_core=2048, n_cores=8):
    import ml_dtypes
    x = np.asarray(inputs["L0_attn_hidden_states"])
    B, L, D = x.shape
    segs = n_cores // B
    sh = {}
    sh["w_in_p"] = stage_weight_panels(np.asarray(inputs["L0_attn_in_proj_w"]))
    sh["w1_p"] = stage_weight_panels(np.asarray(inputs["L0_ffn_w1"]))
    sh["w2_p"] = stage_weight_panels(np.asarray(inputs["L0_ffn_w2"]))
    sh["wout_p"] = stage_weight_panels(np.asarray(inputs["L0_attn_out_proj_w"]))
    cw = np.asarray(inputs["L0_attn_conv_w"])
    sh["conv_w"] = np.ascontiguousarray(
        cw.reshape(KC, 128, KCONV).transpose(1, 0, 2)).reshape(128, KC * KCONV)
    sh["conv_b"] = np.ascontiguousarray(
        np.asarray(inputs["L0_attn_conv_b"]).reshape(KC, 128).T)
    sh["dtb"] = np.asarray(inputs["L0_attn_dt_bias"]).reshape(NH, 1).astype(np.float32)
    sh["alog"] = np.asarray(inputs["L0_attn_A_log"]).reshape(NH, 1).astype(np.float32)
    sh["dvec"] = np.asarray(inputs["L0_attn_D"]).reshape(NH, 1).astype(np.float32)
    sh["nw"] = np.ascontiguousarray(
        np.asarray(inputs["L0_attn_norm_w"]).reshape(KD, 128).T)
    sh["b1"] = np.ascontiguousarray(
        np.asarray(inputs["L0_ffn_b1"]).reshape(KF, 128).T)
    sh["ga"] = np.ascontiguousarray(
        np.asarray(inputs["L0_ffn_gamma"]).reshape(KF, 128).T)
    sh["be"] = np.ascontiguousarray(
        np.asarray(inputs["L0_ffn_beta"]).reshape(KF, 128).T)
    sh["b2"] = np.ascontiguousarray(
        np.asarray(inputs["L0_ffn_b2"]).reshape(KH, 128).T)
    sh["idf"] = np.eye(128, dtype=np.float32)
    sh["idb"] = np.eye(128, dtype=ml_dtypes.bfloat16)
    sh["up01"] = np.triu(np.ones((CH, CH))).astype(ml_dtypes.bfloat16)
    hsel = np.zeros((NH, DIN), dtype=ml_dtypes.bfloat16)
    for h in range(NH):
        hsel[h, h * HDIM:(h + 1) * HDIM] = 1
    sh["hsel"] = hsel
    psel = np.zeros((NH, 128), dtype=ml_dtypes.bfloat16)
    for h in range(NH):
        psel[h, (h % 2) * 64:(h % 2) * 64 + 64] = 1
    sh["psel"] = psel
    bsel = np.zeros((NH, KD), dtype=ml_dtypes.bfloat16)
    for h in range(NH):
        bsel[h, h // 2] = 1
    sh["bsel"] = bsel

    cores = []
    for c in range(n_cores):
        b, q = divmod(c, segs)
        t0 = q * t_core
        xs = x[b, max(0, t0 - 3):t0 + t_core]
        if t0 == 0:
            xs = np.concatenate([np.zeros((3, D), x.dtype), xs], 0)
        d = dict(sh)
        d["x_fm"] = np.ascontiguousarray(xs.T)
        am = np.zeros((128, 4), np.float32)
        dm = np.zeros((NH, 4), np.float32)
        am[:, :q] = 1.0
        dm[:, :q] = 1.0
        d["amask"] = am
        d["dmask"] = dm
        cores.append(d)
    return cores


def assemble_output(results, t_core=2048, n_cores=8):
    B, L, D = 2, 8192, 2048
    segs = n_cores // B
    out = np.empty((B, L, D), np.float32)
    for c in range(n_cores):
        b, q = divmod(c, segs)
        out[b, q * t_core:(q + 1) * t_core] = results[c]["out_fm"].T
    return out


def kernel(**inputs):
    t_core = 2048
    cores = stage_all(inputs, t_core=t_core, n_cores=8)
    nc = build_program(t_core=t_core, pass_t=512, fb=512)
    res = run_bass_kernel_spmd(nc, cores, core_ids=list(range(8)))
    return assemble_output(res.results, t_core=t_core, n_cores=8)


# version 09
# speedup vs baseline: 1.0354x; 1.0271x over previous
import numpy as np
import concourse.bass as bass
import concourse.mybir as mybir
from concourse.bass_utils import run_bass_kernel_spmd

F32 = mybir.dt.float32
BF16 = mybir.dt.bfloat16
F16 = mybir.dt.float16
AF = mybir.ActivationFunctionType
ALU = mybir.AluOpType
AX = mybir.AxisListType

HID, DIN, DSTATE = 2048, 4096, 128
CONVD = DIN + 2 * DSTATE
NH, HDIM, FF, KCONV = 64, 64, 8192, 4
EPS = 1e-5
MAGIC = 12582912.0
CH = 128
KH, KD, KC, KF, M_IN = 16, 32, 34, 64, 67


def cdiv(a, b):
    return (a + b - 1) // b


def nslices(total, step=512, off=0):
    out = []
    o = 0
    while o < total:
        out.append((off + o, min(step, total - o)))
        o += step
    return out


class Sem:
    def __init__(self, h):
        self.h = h
        self.n = 0

    def inc(self, k=1):
        self.n += k
        return ((self, k),)


_ESZ = {"dt.float32": 4, "dt.bfloat16": 2, "dt.float16": 2, "dt.int32": 4,
        "dt.uint8": 1, "dt.int8": 1, "dt.float8e4": 1, "dt.uint32": 4}


def _ap_span(a):
    try:
        name = a.tensor.name
        pairs = list(a.ap)
    except AttributeError:
        return None
    esz = _ESZ.get(str(a.dtype), 4)
    ext = 1
    if len(pairs) > 1:
        ext = sum(s * (n - 1) for s, n in pairs[1:]) + 1
    off = a.offset
    return (name, off * esz, (off + ext) * esz)


def _overlap(fp, fps):
    for name, lo, hi in fps:
        if fp[0] == name and fp[1] < hi and fp[2] > lo:
            return True
    return False


class EngQ:
    def __init__(self, auto_drain=False):
        self.ops = []
        self.auto = auto_drain
        self.pend = []
        self.pendr = []

    def c(self, method, *a, then=(), **kw):
        if self.auto:
            wr, rd = [], []
            outs = []
            if "out" in kw:
                outs.append(kw["out"])
            if a:
                outs.append(a[0])
            for i, x in enumerate(a[1:]):
                s = _ap_span(x)
                if s:
                    rd.append(s)
            for key in ("in_", "bias", "scale"):
                if key in kw:
                    s = _ap_span(kw[key])
                    if s:
                        rd.append(s)
            for x in kw.get("ins", ()):
                s = _ap_span(x)
                if s:
                    rd.append(s)
            for x in kw.get("outs", ()):
                outs.append(x)
            for x in outs:
                s = _ap_span(x)
                if s:
                    wr.append(s)
            hit = any(_overlap(s, self.pend) for s in rd) or \
                any(_overlap(s, self.pend) for s in wr) or \
                any(_overlap(s, self.pendr) for s in wr)
            if hit:
                if self.ops and self.ops[-1][0] != "drain":
                    self.ops.append(("drain", (), {}, ()))
                self.pend = []
                self.pendr = []
            self.pend.extend(wr)
            self.pendr.extend(rd)
        self.ops.append((method, a, kw, tuple(then)))

    def w(self, sem, n=None):
        n = sem.n if n is None else n
        if n > 0:
            self.ops.append(("wait_ge", (sem.h, n), {}, ()))

    def replay(self, eng):
        for m, a, kw, thens in self.ops:
            r = getattr(eng, m)(*a, **kw)
            for sem, k in thens:
                r.then_inc(sem.h, k)


MARKS = []


def build_program(t_core=2048, pass_t=1024, fb=512, stop_after=None, debug=(), rg=None):
    MARKS.clear()
    assert t_core % pass_t == 0 and pass_t % CH == 0 and pass_t % fb == 0
    n_pass = t_core // pass_t
    n_chunks = t_core // CH
    nch_pass = pass_t // CH
    Te = pass_t + 3
    dbg = set(debug)

    nc = bass.Bass("TRN2", target_bir_lowering=False, debug=False)
    DT = nc.dram_tensor

    x_fm = DT("x_fm", [HID, t_core + 3], F32, kind="ExternalInput").ap()
    w_in_p = DT("w_in_p", [M_IN, HID, 128], F32, kind="ExternalInput").ap()
    w1_p = DT("w1_p", [KF, HID, 128], F32, kind="ExternalInput").ap()
    w2_p = DT("w2_p", [KH, FF, 128], F32, kind="ExternalInput").ap()
    wout_p = DT("wout_p", [KH, DIN, 128], F32, kind="ExternalInput").ap()
    conv_w_in = DT("conv_w", [128, KC * KCONV], F32, kind="ExternalInput").ap()
    conv_b_in = DT("conv_b", [128, KC], F32, kind="ExternalInput").ap()
    dtb_in = DT("dtb", [NH, 1], F32, kind="ExternalInput").ap()
    alog_in = DT("alog", [NH, 1], F32, kind="ExternalInput").ap()
    dvec_in = DT("dvec", [NH, 1], F32, kind="ExternalInput").ap()
    nw_in = DT("nw", [128, KD], F32, kind="ExternalInput").ap()
    b1_in = DT("b1", [128, KF], F32, kind="ExternalInput").ap()
    ga_in = DT("ga", [128, KF], F32, kind="ExternalInput").ap()
    be_in = DT("be", [128, KF], F32, kind="ExternalInput").ap()
    b2_in = DT("b2", [128, KH], F32, kind="ExternalInput").ap()
    idf_in = DT("idf", [128, 128], F32, kind="ExternalInput").ap()
    idb_in = DT("idb", [128, 128], BF16, kind="ExternalInput").ap()
    up01_in = DT("up01", [CH, CH], BF16, kind="ExternalInput").ap()
    hsel_in = DT("hsel", [NH, DIN], BF16, kind="ExternalInput").ap()
    psel_in = DT("psel", [NH, 128], BF16, kind="ExternalInput").ap()
    bsel_in = DT("bsel", [NH, KD], BF16, kind="ExternalInput").ap()
    amask_in = DT("amask", [128, 4], F32, kind="ExternalInput").ap()
    dmask_in = DT("dmask", [NH, 4], F32, kind="ExternalInput").ap()

    out_fm = DT("out_fm", [HID, t_core], F32, kind="ExternalOutput").ap()
    dbg_specs = {
        "d_z": ([KD, 128, t_core], F16),
        "d_xbc": ([KC, 128, t_core], BF16),
        "d_dt": ([NH, t_core], F32),
        "d_acs": ([NH, t_core], F32),
        "d_yp": ([KD, 128, t_core], F16),
        "d_state": ([128, DIN + 2], F32),
        "d_xq": ([KH, 128, t_core], F16),
        "d_h": ([KF, 128, t_core], F16),
        "d_g": ([KH, 128, t_core], F16),
        "d_ynorm": ([KD, 128, t_core], F16),
        "d_sin": ([128, DIN], F32),
        "d_xdtt": ([128, DIN], BF16),
        "d_mp": ([128, NH * CH], BF16),
        "d_xn": ([KH, 128, t_core], F16),
    }
    dout = {}
    for name in dbg:
        shp, dt = dbg_specs[name]
        dout[name] = DT(name, shp, dt, kind="ExternalOutput").ap()

    wbf_in = DT("wbf_in", [M_IN, HID, 128], F16, kind="Internal").ap()
    wbf_1 = DT("wbf_1", [KF, HID, 128], F16, kind="Internal").ap()
    wbf_2 = DT("wbf_2", [KH, FF, 128], F16, kind="Internal").ap()
    wbf_o = DT("wbf_o", [KH, DIN, 128], F16, kind="Internal").ap()
    z_sp = DT("z_sp", [KD, 128, t_core], F16, kind="Internal").ap()
    yp_sp = DT("yp_sp", [KD, 128, t_core], F16, kind="Internal").ap()
    g_sp = DT("g_sp", [KH, 128, t_core], F16, kind="Internal").ap()
    st_sp = DT("st_sp", [n_chunks, 128, DIN], F16, kind="Internal").ap()
    xq_sp = DT("xq_sp", [KH, 128, t_core], F16, kind="Internal").ap()
    c_sp = DT("c_sp", [n_chunks, 128, CH], BF16, kind="Internal").ap()
    cc_in = DT("cc_in", [128, DIN + 2], F32, kind="Internal").ap()
    cc_out = DT("cc_out", [4, 128, DIN + 2], F32, kind="Internal",
                addr_space="Local").ap()

    from contextlib import ExitStack
    ctx = ExitStack()

    def sb(name, shape, dt):
        return ctx.enter_context(nc.sbuf_tensor(name, shape, dt))

    def ps_(name, shape, dt):
        return ctx.enter_context(nc.psum_tensor(name, shape, dt))

    def prod(xs):
        r = 1
        for x in xs:
            r *= x
        return r

    def vw(arena, off_bytes, dtype, shape):
        esz = 4 if dtype == F32 else 2
        assert off_bytes % 8 == 0
        ap = arena[:] if dtype == F32 else arena[:].bitcast(dtype)
        start = off_bytes // esz
        per = prod(shape[1:])
        flat = ap[:, start:start + per]
        if len(shape) == 3:
            return flat.rearrange("q (a b) -> q a b", a=shape[1])
        return flat

    wA = max(KH * Te * 4, KC * Te * 2, KF * fb * 2, KD * pass_t * 2)
    wA = (wA + 31) // 32 * 32
    arA = sb("arA", [128, wA // 4], F32)
    xf32 = vw(arA, 0, F32, [128, KH, Te])
    big1 = vw(arA, 0, BF16, [128, KC, Te])
    h_sb = vw(arA, 0, F16, [128, KF, fb])
    ynb = vw(arA, 0, F16, [128, KD, pass_t])

    wB = max(KH * Te * 2,
             ((DIN + 2) * 4 + 31) // 32 * 32 + 2 * KD * 256)
    wB = (wB + 31) // 32 * 32
    arB = sb("arB", [128, wB // 4], F32)
    xn_sb = vw(arB, 0, F16, [128, KH, Te])
    ccld = vw(arB, 0, F32, [128, DIN + 2])
    gxl = vw(arB, 0, F16, [128, KH, 512])
    pb_ld2 = vw(arB, ((DIN + 2) * 4 + 31) // 32 * 32, F16, [128, KD, 128])
    pb_ld3 = vw(arB, ((DIN + 2) * 4 + 31) // 32 * 32 + KD * 256, F16,
                [128, KD, 128])


    arC = sb("arC", [128, 8192], F32)
    xdtT = vw(arC, 0, BF16, [128, KD, 128])
    mp_sb = vw(arC, 8192, BF16, [128, NH, CH])
    ystage = vw(arC, 24576, F16, [128, 8, 128])
    zstage = vw(arC, 26624, F16, [128, 2, 512])
    xqstage = zstage
    xq_sb = vw(arC, 0, F16, [128, KH, fb])
    wb2 = vw(arC, 0, F16, [128, KF, 128])
    ffnbc = vw(arC, 16384, F32, [128, 8, 512])
    stot = vw(arC, 0, F32, [128, KD, 128])
    stotT = vw(arC, 16384, BF16, [128, KD, 128])
    pb_ld = vw(arC, 24576, F16, [128, KD, 128])

    c_idf = sb("c_idf", [128, 128], F32)
    c_idb = sb("c_idb", [128, 128], BF16)
    c_up01 = sb("c_up01", [CH, CH], BF16)
    c_hsel = sb("c_hsel", [NH, DIN], BF16)
    c_psel = sb("c_psel", [NH, 128], BF16)
    c_bsel = sb("c_bsel", [NH, KD], BF16)
    cw_sb = sb("cw_sb", [128, KC * KCONV], F32)
    cb_sb = sb("cb_sb", [128, KC], F32)
    smallv = sb("smallv", [NH, 8], F32)
    small_bf = sb("small_bf", [NH, 4], BF16)
    seld = sb("seld", [NH, KD], BF16)
    ga_sb = sb("ga_sb", [128, KF], F32)
    be_sb = sb("be_sb", [128, KF], F32)

    stat4T = sb("stat4T", [16, 128], F32)
    thr_sb = sb("thr_sb", [128, 4], F32)
    nw_sb = sb("nw_sb", [128, KD], F32)
    b1_sb = sb("b1_sb", [128, KF], F32)
    b2_sb = sb("b2_sb", [128, KH], F32)
    amask_sb = sb("amask_sb", [128, 4], F32)
    dmask_sb = sb("dmask_sb", [NH, 4], F32)
    ones_r = sb("ones_r", [1, 128], F32)
    ones_c = sb("ones_c", [128, 1], F32)
    ones_cb = sb("ones_cb", [128, 1], F16)
    ones_cbb = sb("ones_cbb", [128, 1], BF16)
    ones64 = sb("ones64", [NH, CH], F32)
    dexp_sb = sb("dexp_sb", [128, KD], F32)
    mw_col = sb("mw_col", [128, 2], F32)
    wsacc = sb("wsacc", [128, 2], F32)
    acs_sb = sb("acs_sb", [NH, t_core], F32)
    cend_sb = sb("cend_sb", [NH, 48], F32)
    s_state = sb("s_state", [128, KD, 128], F16)
    wp_a = sb("wp_a", [128, KF, 128], F16)
    wp_b = sb("wp_b", [128, KD, 128], F16)
    gstage = wp_b[:].rearrange("q a b -> q (a b)").rearrange(
        "q (a b) -> q a b", a=2)[:, :, 0:512]
    scr1 = sb("scr1", [128, 1032], F32)
    scr2 = sb("scr2", [128, 1032], F32)
    scr3 = sb("scr3", [128, 1032], F32)
    ug_sb = scr3[:, 0:512]
    sq_sb = scr3[:].bitcast(BF16)[:, 1024:2048]
    ea2_sb = scr3[:].bitcast(BF16)[:, 1024:1536]
    flat_sb = sb("flat_sb", [1, 1032], F32)
    shB = sb("shB", [128, 1032], F32)
    dt_raw = shB[:]
    outst = shB[:, 0:1024].rearrange("q (a b) -> q a b", a=2)
    dtA = dt_raw
    acsT = sb("acsT", [128, NH], F32)
    tb1 = sb("tb1", [128, NH], BF16)
    tb2 = sb("tb2", [128, NH], BF16)
    gm_sb = sb("gm_sb", [128, CH], BF16)
    dcy_sb = sb("dcy_sb", [NH, CH], F32)
    adt_sb = dcy_sb
    stat4 = sb("stat4", [128, 16], F32)
    dcb = sb("dcb", [NH, 2 * CH], BF16)
    bt_sb = sb("bt_sb", [128, CH], BF16)
    c_ld = bt_sb
    dexp2_sb = sb("dexp2_sb", [128, KD], F32)
    cst = sb("cst", [128, 4], F32)

    pbk = [ps_(f"pb{i}", [128, 512], F32) for i in range(6)]
    pbf = [ps_(f"pbf{i}", [128, 1024], BF16) for i in range(2)]

    T = EngQ()
    V = EngQ(auto_drain=True)
    SC = EngQ(auto_drain=True)
    G = EngQ(auto_drain=True)
    SY = EngQ()

    def mark(label):
        MARKS.append((label, s_t.n, s_v.n, s_sc.n, s_gc.n))
    def mksem(name):
        return Sem(ctx.enter_context(nc.semaphore(name)))
    s_t, s_v, s_sc = mksem("s_t"), mksem("s_v"), mksem("s_sc")
    s_small = mksem("s_small")
    s_x = mksem("s_x")
    s_wpa = mksem("s_wpa")
    s_wpb = mksem("s_wpb")
    s_wba = mksem("s_wba")
    s_wbb = mksem("s_wbb")
    s_wpc = mksem("s_wpc")
    s_wpd = mksem("s_wpd")
    s_wbc = mksem("s_wbc")
    s_wbd = mksem("s_wbd")
    s_zs0 = mksem("s_zs0")
    s_zs1 = mksem("s_zs1")
    s_dbg = mksem("s_dbg")
    s_o = mksem("s_o")
    s_yst = mksem("s_yst")
    s_stp = mksem("s_stp")
    s_gb = mksem("s_gb")
    s_cs = mksem("s_cs")
    s_cc = mksem("s_cc")
    s_ld2 = mksem("s_ld2")
    s_fl0 = mksem("s_fl0")
    s_fl1 = mksem("s_fl1")
    s_pre0 = mksem("s_pre0")
    s_pre1 = mksem("s_pre1")
    s_gsp = mksem("s_gsp")
    s_gl = mksem("s_gl")
    s_gc = mksem("s_gc")
    s_xq0 = mksem("s_xq0")
    s_xq1 = mksem("s_xq1")
    s_xql = mksem("s_xql")

    for dst, src in [
        (c_idf[:], idf_in), (c_idb[:], idb_in), (c_up01[:], up01_in),
        (c_hsel[:], hsel_in), (c_psel[:], psel_in), (c_bsel[:], bsel_in),
        (cw_sb[:], conv_w_in), (cb_sb[:], conv_b_in),
        (smallv[:, 0:1], dtb_in), (smallv[:, 1:2], alog_in),
        (smallv[:, 3:4], dvec_in),
        (nw_sb[:], nw_in), (b1_sb[:], b1_in), (b2_sb[:], b2_in),
        (ga_sb[:], ga_in), (be_sb[:], be_in),
        (amask_sb[:], amask_in), (dmask_sb[:], dmask_in),
    ]:
        SY.c("dma_start", out=dst, in_=src, then=s_small.inc(16))
    y_small = s_small.n

    V.c("memset", cst[:, 0:1], EPS)
    V.c("memset", cst[:, 1:2], 1.0)
    V.c("memset", cst[:, 2:3], -1.0)
    V.c("memset", cst[:, 3:4], 0.0)
    V.c("memset", ones_r[:], 1.0)
    V.c("memset", ones_c[:], 1.0)
    V.c("memset", ones_cb[:], 1.0)
    V.c("memset", ones_cbb[:], 1.0)
    V.c("memset", ones64[:], 1.0)
    V.c("memset", s_state[:], 0.0, then=s_v.inc())

    SC.w(s_small, y_small)
    SC.c("activation", smallv[:, 2:3], smallv[:, 1:2], AF.Exp, then=s_sc.inc())
    V.w(s_sc, s_sc.n)
    V.c("tensor_scalar_mul", smallv[:, 2:3], smallv[:, 2:3], -1.0)
    V.c("tensor_copy", small_bf[:, 0:1], smallv[:, 3:4], then=s_v.inc())

    def expand_hp(colvec_f32, dst_sb, bank):
        V.w(s_t, s_t.n)
        V.c("tensor_scalar", seld[:], c_bsel[:], colvec_f32, None, ALU.mult,
            then=s_v.inc())
        T.w(s_v, s_v.n)
        T.c("matmul", pbk[bank][:, 0:KD], c_psel[:], seld[:], start=True,
            stop=True, then=s_t.inc())
        V.w(s_t, s_t.n)
        V.c("tensor_copy", dst_sb, pbk[bank][:, 0:KD], then=s_v.inc())

    expand_hp(smallv[:, 3:4], dexp_sb[:], 4)

    wpf = wp_a[:].rearrange("q a b -> q (a b)").bitcast(F32)
    prep_units = [("w1", i, 0) for i in range(KF)] + \
                 [("w2", i, q) for i in range(KH) for q in range(4)]
    prep_state = {"i": 0, "vsum": 0}

    def wprep_step(n_units):
        for _ in range(n_units):
            i = prep_state["i"]
            if i >= len(prep_units):
                return
            prep_state["i"] = i + 1
            u = prep_units[i]
            half = i % 2
            hsem = s_pre0 if half == 0 else s_pre1
            dst = wpf[:, half * 2048:half * 2048 + 2048]
            dst2 = dst.rearrange("q (k c) -> q k c", c=128)
            if u[0] == "w1":
                src = w1_p[u[1]].rearrange("(k q) c -> q k c", q=128)
                col = 0
            else:
                src = w2_p[u[1]].rearrange("(k q) c -> q k c", q=128)[
                    :, u[2] * 16:u[2] * 16 + 16, :]
                col = 1
            if i == 0:
                SY.w(s_t, s_t.n)
                SY.w(s_wba, s_wba.n)
                SY.w(s_wbb, s_wbb.n)
                V.c("memset", wsacc[:], 0.0)
            SY.w(s_v, prep_state["vsum"])
            SY.c("dma_start", out=dst2, in_=src, then=hsem.inc(16))
            V.w(hsem, hsem.n)
            V.c("tensor_reduce", scr2[:, 1030:1031], dst, AX.X, ALU.add,
                apply_absolute_value=True)
            V.c("tensor_tensor", wsacc[:, col:col + 1], wsacc[:, col:col + 1],
                scr2[:, 1030:1031], ALU.add, then=s_v.inc())
            prep_state["vsum"] = s_v.n

    def wprep_finalize():
        V.w(s_v, prep_state["vsum"])
        T.w(s_v, prep_state["vsum"])
        T.c("matmul", pbk[0][0:1, 504:506], ones_c[:], wsacc[:, 0:2],
            start=True, stop=True, then=s_t.inc())
        V.w(s_t, s_t.n)
        V.c("tensor_scalar", scr3[0:1, 1024:1026], pbk[0][0:1, 504:506],
            1.0 / (FF * HID), EPS, ALU.mult, ALU.max, then=s_v.inc())
        T.w(s_v, s_v.n)
        T.c("matmul", pbk[1][:, 504:506], ones_r[:], scr3[0:1, 1024:1026],
            start=True, stop=True, then=s_t.inc())
        V.w(s_t, s_t.n)
        V.c("tensor_scalar_mul", thr_sb[:, 0:1], pbk[1][:, 504:505], 0.5)
        V.c("tensor_scalar_mul", thr_sb[:, 1:2], pbk[1][:, 504:505], -0.5)
        V.c("tensor_scalar_mul", thr_sb[:, 2:3], pbk[1][:, 505:506], 0.5)
        V.c("tensor_scalar_mul", thr_sb[:, 3:4], pbk[1][:, 505:506], -0.5)
        V.c("tensor_copy", mw_col[:, 0:2], pbk[1][:, 504:506],
            then=s_v.inc())

    def stage_a(p):
        c0 = p * pass_t
        first = (p == 0)
        sl_x = nslices(Te)
        sl_a = nslices(pass_t, off=3)

        if gate:
            SY.w(s_v, gate["v"])
            V.w(s_t, gate["t"])
            V.w(s_sc, gate["sc"])
            SC.w(s_v, gate["v"])
        SY.c("dma_start", out=xf32[:],
             in_=x_fm.rearrange("(k q) t -> q k t", q=128)[:, :, c0:c0 + Te],
             then=s_x.inc(16))
        x_rdy = s_x.n

        V.w(s_x, x_rdy)
        V.c("tensor_tensor", scr1[:, 0:Te], xf32[:, 0, :], xf32[:, 1, :], ALU.add)
        for k in range(2, KH):
            V.c("tensor_tensor", scr1[:, 0:Te], scr1[:, 0:Te], xf32[:, k, :],
                ALU.add, then=(s_v.inc() if k == KH - 1 else ()))
        for k in range(KH):
            V.c("tensor_tensor", scr2[:, 0:Te], xf32[:, k, :], xf32[:, k, :],
                ALU.mult)
            if k == 0:
                V.c("tensor_copy", scr3[:, 0:Te], scr2[:, 0:Te])
            else:
                V.c("tensor_tensor", scr3[:, 0:Te], scr3[:, 0:Te],
                    scr2[:, 0:Te], ALU.add,
                    then=(s_v.inc() if k == KH - 1 else ()))
        v_stats = s_v.n

        T.w(s_v, v_stats)
        regions = {}
        items = []
        for stat, src, bks in [(0, scr1, (2, 3)), (1, scr3, (4, 5))]:
            for si, (o, w) in enumerate(sl_x):
                if si == 0:
                    reg = pbk[bks[0]][0:1, 0:w]
                elif si == 1:
                    reg = pbk[bks[1]][0:1, 0:w]
                else:
                    reg = pbk[bks[0]][0:1, 505:505 + w]
                regions[(stat, si)] = (reg, o, w)
                items.append((reg, src, o, w))
        for i, (reg, src, o, w) in enumerate(items):
            T.c("matmul", reg, ones_c[:], src[:, o:o + w], start=True,
                stop=True, then=(s_t.inc() if i == len(items) - 1 else ()))
        t_stats = s_t.n

        V.w(s_t, t_stats)
        for stat, dst in [(0, scr2), (1, flat_sb)]:
            for si in range(len(sl_x)):
                reg, o, w = regions[(stat, si)]
                V.c("tensor_scalar_mul", dst[0:1, o:o + w], reg, 1.0 / HID)
        V.c("tensor_tensor", dt_raw[0:1, 0:Te], scr2[0:1, 0:Te],
            scr2[0:1, 0:Te], ALU.mult)
        V.c("tensor_tensor", flat_sb[0:1, 0:Te], flat_sb[0:1, 0:Te],
            dt_raw[0:1, 0:Te], ALU.subtract, then=s_v.inc())
        SC.w(s_v, s_v.n)
        SC.c("activation", dt_raw[0:1, 0:Te], flat_sb[0:1, 0:Te], AF.Sqrt,
             bias=cst[0:1, 0:1], then=s_sc.inc())
        V.w(s_sc, s_sc.n)
        V.c("reciprocal", flat_sb[0:1, 0:Te], dt_raw[0:1, 0:Te])
        V.c("tensor_tensor", dt_raw[0:1, 0:Te], scr2[0:1, 0:Te],
            flat_sb[0:1, 0:Te], ALU.mult)
        V.c("tensor_scalar_mul", dt_raw[0:1, 0:Te], dt_raw[0:1, 0:Te], -1.0,
            then=s_v.inc())
        v_ab = s_v.n
        mark('ln_stats')

        T.w(s_v, v_ab)
        v_apply = 0
        for si, (o, w) in enumerate(sl_x):
            if si > 0:
                T.w(s_v, v_apply)
            T.c("matmul", pbk[4][:, 0:w], ones_r[:], flat_sb[0:1, o:o + w],
                start=True, stop=True)
            T.c("matmul", pbk[5][:, 0:w], ones_r[:], dt_raw[0:1, o:o + w],
                start=True, stop=True, then=s_t.inc())
            V.w(s_t, s_t.n)
            for k in range(KH):
                hof = 512 * (k % 2)
                V.c("tensor_tensor", scr3[:, hof:hof + w],
                    xf32[:, k, o:o + w], pbk[4][:, 0:w], ALU.mult)
                V.c("tensor_tensor", xf32[:, k, o:o + w],
                    scr3[:, hof:hof + w], pbk[5][:, 0:w], ALU.add,
                    then=s_v.inc())
                SC.w(s_v, s_v.n)
                SC.c("activation", xn_sb[:, k, o:o + w],
                     xf32[:, k, o:o + w], AF.Copy, bias=0.0,
                     then=s_sc.inc())
            xsw = xf32[:, :, o:o + w].rearrange("q k t -> q t k")
            V.c("tensor_reduce", scr1[:, o:o + w], xsw, AX.X, ALU.max)
            V.c("tensor_reduce", scr3[:, 0:w], xsw, AX.X, ALU.min)
            V.c("scalar_tensor_tensor", scr1[:, o:o + w], scr3[:, 0:w],
                -1.0, scr1[:, o:o + w], ALU.mult, ALU.max,
                then=s_v.inc())
            v_apply = s_v.n
        v_xn = s_v.n
        sc_xn = s_sc.n
        mark('ln_apply')

        nj = pass_t // 128
        for jg in range(cdiv(nj, 4)):
            ng = min(4, nj - jg * 4)
            T.w(s_v, v_xn if jg == 0 else s_v.n)
            for j4 in range(ng):
                j = jg * 4 + j4
                T.c("transpose", pbk[4][:, j4 * 128:(j4 + 1) * 128],
                    scr1[:, 3 + j * 128:3 + (j + 1) * 128], c_idf[:],
                    then=(s_t.inc() if j4 == ng - 1 else ()))
            V.w(s_t, s_t.n)
            for j4 in range(ng):
                j = jg * 4 + j4
                V.c("tensor_reduce", stat4[:, 8 + j:9 + j],
                    pbk[4][:, j4 * 128:(j4 + 1) * 128], AX.X, ALU.max,
                    then=(s_v.inc() if j4 == ng - 1 else ()))
        T.w(s_v, s_v.n)
        T.c("transpose", pbk[4][0:nj, 0:128], stat4[:, 8:8 + nj], c_idf[:],
            then=s_t.inc())
        V.w(s_t, s_t.n)
        V.c("tensor_copy", stat4T[0:nj, :], pbk[4][0:nj, 0:128],
            then=s_v.inc())
        G.w(s_v, s_v.n)
        G.c("dma_start", out=scr3[0:1, 0:pass_t], in_=stat4T[0:nj, :],
            then=s_gb.inc(16))
        V.w(s_gb, s_gb.n)
        V.c("tensor_scalar_max", scr3[0:1, 0:pass_t], scr3[0:1, 0:pass_t],
            EPS)
        V.c("reciprocal", scr2[0:1, 0:pass_t], scr3[0:1, 0:pass_t])
        V.c("tensor_scalar_mul", dt_raw[0:1, 3:3 + pass_t],
            scr2[0:1, 0:pass_t], 127.0, then=s_v.inc())
        v_qrows = s_v.n

        for si2, (oq, wq) in enumerate(nslices(pass_t, 512, off=3)):
            T.w(s_v, v_qrows if si2 == 0 else s_v.n)
            T.c("matmul", pbk[4][:, 0:wq], ones_r[:],
                dt_raw[0:1, oq:oq + wq], start=True, stop=True,
                then=s_t.inc())
            V.w(s_t, s_t.n)
            for k in range(KH):
                slot = k % 2
                hof = 512 * slot
                qsem = s_xq0 if slot == 0 else s_xq1
                V.c("tensor_tensor", scr3[:, hof:hof + wq],
                    xf32[:, k, oq:oq + wq], pbk[4][:, 0:wq], ALU.mult)
                if qsem.n:
                    V.w(qsem, qsem.n)
                V.c("tensor_scalar", xqstage[:, slot, 0:wq],
                    scr3[:, hof:hof + wq],
                    MAGIC, -MAGIC, ALU.add, ALU.add, then=s_v.inc())
                SY.w(s_v, s_v.n)
                SY.c("dma_start",
                     out=xq_sp[k, :, c0 + oq - 3:c0 + oq - 3 + wq],
                     in_=xqstage[:, slot, 0:wq], then=qsem.inc(16))
        v_xq_done = s_v.n
        mark('xq')

        if "d_xn" in dout:
            SY.w(s_v, v_xn)
            SY.c("dma_start",
                 out=dout["d_xn"].rearrange("k q t -> q k t")[:, :, c0:c0 + pass_t],
                 in_=xn_sb[:, :, 3:Te], then=s_dbg.inc(16))

        V.w(s_xq0, s_xq0.n)
        V.w(s_xq1, s_xq1.n)
        SC.w(s_xq0, s_xq0.n)
        SC.w(s_xq1, s_xq1.n)
        panel_rdy = {}
        mm_done = {}
        evac_done = {}
        zspill = {}

        def load_panel(m):
            buf = wp_a if m % 2 == 0 else wp_b
            sem = (s_wpa if m % 2 == 0 else s_wpb) if first else \
                  (s_wpc if m % 2 == 0 else s_wpd)
            wbsem = s_wba if m % 2 == 0 else s_wbb
            if m - 2 in mm_done:
                (G if first else SY).w(s_t, mm_done[m - 2])
            src_f = w_in_p[m].rearrange("(k p) c -> p k c", p=128)
            src_b = wbf_in[m].rearrange("(k p) c -> p k c", p=128)
            if first:
                G.w(wbsem, wbsem.n)
                G.c("dma_start", out=buf[:, 0:KH, :], in_=src_f,
                    then=sem.inc(16))
                panel_rdy[m] = (sem, sem.n)
                G.w(sem, sem.n)
                G.c("dma_start", out=src_b, in_=buf[:, 0:KH, :],
                    then=wbsem.inc(16))
            else:
                SY.w(s_wba, s_wba.n)
                SY.w(s_wbb, s_wbb.n)
                SY.w(s_wbc, s_wbc.n)
                SY.w(s_wbd, s_wbd.n)
                SY.c("dma_start", out=buf[:, 0:KH, :], in_=src_b,
                    then=sem.inc(16))
                panel_rdy[m] = (sem, sem.n)

        load_panel(0)
        for m in range(M_IN):
            if m + 1 < M_IN:
                load_panel(m + 1)
            buf = wp_a if m % 2 == 0 else wp_b
            mcols = 128 if m < M_IN - 1 else 64
            kind = "z" if m < KD else ("xbc" if m < KD + KC else "dt")
            qsem, qv = panel_rdy[m]
            T.w(qsem, qv)
            T.w(s_v, v_xn)
            T.w(s_sc, sc_xn)
            if m - 2 in evac_done:
                T.w(s_sc, evac_done[m - 2])
            bsel = (0, 1) if m % 2 == 0 else (2, 3)
            tbank = pbk[4] if m % 2 == 0 else pbk[5]
            sls = sl_x if kind == "xbc" else sl_a
            for si, (o, w) in enumerate(sls):
                if si < 2:
                    reg = pbk[bsel[si]][0:mcols, 0:w]
                else:
                    reg = tbank[0:mcols, 0:w]
                for k in range(KH):
                    T.c("matmul", reg, buf[:, k, 0:mcols],
                        xn_sb[:, k, o:o + w], start=(k == 0),
                        stop=(k == KH - 1),
                        then=(s_t.inc() if (k == KH - 1 and si == len(sls) - 1)
                              else ()))
            mm_done[m] = s_t.n

            SC.w(s_t, mm_done[m])
            for si, (o, w) in enumerate(sls):
                if si < 2:
                    reg = pbk[bsel[si]][0:mcols, 0:w]
                else:
                    reg = tbank[0:mcols, 0:w]
                if kind == "z":
                    zsem = s_zs0 if si == 0 else s_zs1
                    if zspill.get(si):
                        SC.w(zsem, zspill[si])
                    SC.c("activation", zstage[:, si, 0:w], reg, AF.Copy,
                         bias=0.0, then=s_sc.inc())
                    SY.w(s_sc, s_sc.n)
                    SY.c("dma_start", out=z_sp[m, :, c0 + o - 3:c0 + o - 3 + w],
                         in_=zstage[:, si, 0:w], then=zsem.inc(16))
                    zspill[si] = zsem.n
                elif kind == "xbc":
                    SC.c("activation", big1[:, m - KD, o:o + w], reg,
                         AF.Copy, bias=0.0,
                         then=(s_sc.inc() if si == len(sls) - 1 else ()))
                else:
                    SC.c("activation", dt_raw[0:64, o:o + w], reg, AF.Copy,
                         bias=0.0,
                         then=(s_sc.inc() if si == len(sls) - 1 else ()))
            evac_done[m] = s_sc.n
        v_inproj = s_v.n
        sc_inproj = s_sc.n
        mark('inproj')

        if stop_after == "inproj":
            return

        conv_sc = {}
        for c in range(KC):
            if c % 2 == 0:
                acc = scr1
                V.w(s_sc, evac_done[KD + c])
                if c - 2 in conv_sc:
                    V.w(s_sc, conv_sc[c - 2])
                V.c("tensor_scalar_mul", acc[:, 0:pass_t],
                    big1[:, c, 0:pass_t], cw_sb[:, c * KCONV:c * KCONV + 1])
                for i in range(1, KCONV):
                    V.c("scalar_tensor_tensor", acc[:, 0:pass_t],
                        big1[:, c, i:i + pass_t],
                        cw_sb[:, c * KCONV + i:c * KCONV + i + 1],
                        acc[:, 0:pass_t], ALU.mult, ALU.add,
                        then=(s_v.inc() if i == KCONV - 1 else ()))
                SC.w(s_v, s_v.n)
            else:
                acc = scr3
                tmp = scr3[:, 512:512 + pass_t] if pass_t <= 512 else None
                G.w(s_sc, evac_done[KD + c])
                if c - 2 in conv_sc:
                    G.w(s_sc, conv_sc[c - 2])

                def cwb(i):
                    return cw_sb[:, c * KCONV + i:c * KCONV + i + 1]\
                        .broadcast_to([128, pass_t])
                G.c("tensor_tensor", acc[:, 0:pass_t],
                    big1[:, c, 0:pass_t], cwb(0), ALU.mult)
                for i in range(1, KCONV):
                    G.c("tensor_tensor", tmp, big1[:, c, i:i + pass_t],
                        cwb(i), ALU.mult)
                    G.c("tensor_tensor", acc[:, 0:pass_t], acc[:, 0:pass_t],
                        tmp, ALU.add,
                        then=(s_gc.inc() if i == KCONV - 1 else ()))
                SC.w(s_gc, s_gc.n)
            SC.c("activation", big1[:, c, 3:Te], acc[:, 0:pass_t], AF.Silu,
                 bias=cb_sb[:, c:c + 1], then=s_sc.inc())
            conv_sc[c] = s_sc.n
        sc_conv = s_sc.n
        mark('conv')

        SY.w(s_sc, sc_conv)
        for ci in range(nch_pass):
            SY.c("dma_start", out=c_sp[p * nch_pass + ci],
                 in_=big1[:, 33, 3 + ci * CH:3 + (ci + 1) * CH],
                 then=s_cs.inc(16))

        V.w(s_sc, sc_inproj)
        V.w(s_sc, sc_conv)
        V.c("tensor_scalar", scr1[0:64, 0:pass_t], dt_raw[0:64, 3:Te],
            smallv[:, 0:1], None, ALU.add, then=s_v.inc())
        SC.w(s_v, s_v.n)
        SC.c("activation", scr2[0:64, 0:pass_t], scr1[0:64, 0:pass_t], AF.Abs)
        SC.c("activation", scr3[0:64, 0:pass_t], scr2[0:64, 0:pass_t], AF.Exp,
             scale=cst[0:64, 2:3])
        SC.c("activation", scr2[0:64, 0:pass_t], scr3[0:64, 0:pass_t], AF.Ln,
             bias=cst[0:64, 1:2], then=s_sc.inc())
        V.w(s_sc, s_sc.n)
        V.c("tensor_scalar_max", dtA[0:64, 0:pass_t], scr1[0:64, 0:pass_t],
            0.0)
        V.c("tensor_tensor", dtA[0:64, 0:pass_t], dtA[0:64, 0:pass_t],
            scr2[0:64, 0:pass_t], ALU.add, then=s_v.inc())
        v_dt = s_v.n
        mark('dt')

        if stop_after == "conv":
            if "d_dt" in dout:
                SY.w(s_v, v_dt)
                SY.c("dma_start", out=dout["d_dt"][:, c0:c0 + pass_t],
                     in_=dtA[0:64, 0:pass_t], then=s_dbg.inc(16))
            if "d_xbc" in dout:
                SY.w(s_sc, sc_conv)
                SY.c("dma_start",
                     out=dout["d_xbc"].rearrange("k q t -> q k t")[:, :, c0:c0 + pass_t],
                     in_=big1[:, :, 3:Te], then=s_dbg.inc(16))
            return

        for ci in range(nch_pass):
            if p == 0:
                wprep_step(cdiv(128, nch_pass))
            gc = p * nch_pass + ci
            tc = 3 + ci * CH
            cc = ci * CH
            gcol = gc * CH
            t_chunk_start = s_t.n
            mark('ssd_chunk')

            V.w(s_v, v_dt)
            V.w(s_t, t_chunk_start)
            V.c("tensor_scalar", adt_sb[0:64, :], dtA[0:64, cc:cc + CH],
                smallv[:, 2:3], None, ALU.mult)
            V.c("tensor_tensor_scan", acs_sb[0:64, gcol:gcol + CH],
                ones64[:, :], adt_sb[0:64, :], 0.0, ALU.mult, ALU.add)
            V.c("tensor_scalar", dcy_sb[0:64, :], acs_sb[0:64, gcol:gcol + CH],
                acs_sb[0:64, gcol + CH - 1:gcol + CH], -1.0,
                ALU.subtract, ALU.mult)
            V.c("tensor_copy", cend_sb[:, gc:gc + 1],
                acs_sb[0:64, gcol + CH - 1:gcol + CH])
            V.c("tensor_copy", dcb[:, 0:CH], dtA[0:64, cc:cc + CH],
                then=s_v.inc())
            v_sc1 = s_v.n
            SC.w(s_v, v_sc1)
            SC.c("activation", dcb[:, CH:2 * CH], dcy_sb[0:64, :], AF.Exp)
            SC.c("activation", smallv[:, 6:7],
                 acs_sb[0:64, gcol + CH - 1:gcol + CH], AF.Exp,
                 then=s_sc.inc())
            sc_sc1 = s_sc.n
            V.w(s_sc, sc_sc1)
            expand_hp(smallv[:, 6:7], dexp2_sb[:], 4)
            v_dexp2 = s_v.n

            T.w(s_v, v_sc1)
            T.c("transpose", pbk[5][:, 0:64], acs_sb[0:64, gcol:gcol + CH],
                c_idf[0:64, 0:64], then=s_t.inc())
            V.w(s_t, s_t.n)
            V.c("tensor_copy", acsT[:, 0:64], pbk[5][:, 0:64], then=s_v.inc())
            v_acsT = s_v.n

            T.w(s_sc, sc_conv)
            T.w(s_v, v_acsT)
            T.c("matmul", pbk[5][:, 128:256], big1[:, 32, tc:tc + CH],
                big1[:, 33, tc:tc + CH], start=True, stop=True,
                then=s_t.inc())
            V.w(s_t, s_t.n)
            V.c("tensor_tensor", gm_sb[:], pbk[5][:, 128:256], c_up01[:],
                ALU.mult, then=s_v.inc())
            v_gm = s_v.n

            T.w(s_sc, sc_sc1)
            T.c("transpose", pbf[0][:, 0:64], dcb[:, 0:CH],
                c_idb[0:64, 0:64])
            T.c("transpose", pbf[0][:, 128:192], dcb[:, CH:2 * CH],
                c_idb[0:64, 0:64])
            T.c("transpose", pbf[0][:, 256:384], big1[:, 32, tc:tc + CH],
                c_idb[:], then=s_t.inc())
            t_r0 = s_t.n
            V.w(s_t, t_r0)
            V.c("tensor_copy", tb1[:, 0:64], pbf[0][:, 0:64])
            V.c("tensor_copy", tb2[:, 0:64], pbf[0][:, 128:192])
            V.c("tensor_copy", bt_sb[:], pbf[0][:, 256:384], then=s_v.inc())
            v_r0 = s_v.n

            v_xtt = {}
            for r in range(4):
                bk = pbf[r % 2]
                if r >= 2:
                    T.w(s_v, v_xtt[r - 2])
                else:
                    T.w(s_v, v_r0)
                for j in range(8):
                    blk = r * 8 + j
                    T.c("transpose", bk[:, j * 128:(j + 1) * 128],
                        big1[:, blk, tc:tc + CH], c_idb[:],
                        then=(s_t.inc() if j == 7 else ()))
                t_r = s_t.n
                V.w(s_t, t_r)
                for gsub in range(2):
                    b0 = r * 8 + gsub * 4
                    src = bk[:, gsub * 512:(gsub + 1) * 512].rearrange(
                        "p (g o) -> p g o", g=8)
                    dst = xdtT[:, b0:b0 + 4, :].rearrange(
                        "p b (h o) -> p (b h) o", h=2)
                    mult = tb1[:, b0 * 2:b0 * 2 + 8].rearrange(
                        "p (h o) -> p h o", o=1).broadcast_to([128, 8, 64])
                    V.c("tensor_tensor", dst, src, mult, ALU.mult,
                        then=(s_v.inc() if gsub == 1 else ()))
                v_xtt[r] = s_v.n
            v_xdtT = s_v.n

            v_seg = {}
            sc_exp = {}
            t_bc_of = {}
            for g4 in range(16):
                h0 = g4 * 4
                slot = g4 % 2
                flsem = s_fl0 if slot == 0 else s_fl1
                foff = slot * 512
                if g4 >= 2:
                    G.w(s_t, t_bc_of[g4 - 2])
                G.w(s_v, v_acsT)
                G.c("dma_start", out=flat_sb[0:1, foff:foff + 512],
                    in_=acs_sb[h0:h0 + 4, gcol:gcol + CH], then=flsem.inc(16))
                T.w(flsem, flsem.n)
                if g4 >= 2:
                    T.w(s_v, v_seg[g4 - 2])
                T.c("matmul", pbk[slot][:, 0:512], ones_r[:],
                    flat_sb[0:1, foff:foff + 512], start=True, stop=True,
                    then=s_t.inc())
                t_bc = s_t.n
                t_bc_of[g4] = t_bc
                V.w(s_t, t_bc)
                if g4 >= 2:
                    V.w(s_sc, sc_exp.get(g4 - 2, 0))
                aw = acsT[:, h0:h0 + 4].rearrange(
                    "p (g o) -> p g o", o=1).broadcast_to([128, 4, CH])
                s2 = scr2[:, foff:foff + 512]
                V.c("tensor_tensor",
                    s2.rearrange("p (g t) -> p g t", g=4),
                    pbk[slot][:, 0:512].rearrange("p (g t) -> p g t", g=4),
                    aw, ALU.subtract)
                V.c("tensor_scalar_min", s2, s2, 0.0, then=s_v.inc())
                v_seg[g4] = s_v.n
                SC.w(s_v, v_seg[g4])
                SC.c("activation",
                     mp_sb[:, h0:h0 + 4, :].rearrange("p a b -> p (a b)"),
                     s2, AF.Exp, then=s_sc.inc())
                sc_exp[g4] = s_sc.n
            V.w(s_sc, s_sc.n)
            V.w(s_v, v_gm)
            V.c("tensor_tensor", mp_sb[:, 0:64, :], mp_sb[:, 0:64, :],
                gm_sb[:].rearrange("p (o t) -> p o t", o=1).broadcast_to(
                    [128, 64, CH]),
                ALU.mult, then=s_v.inc())
            v_mp = s_v.n

            yd_done = {}
            v_yd = {}
            for grp in range(8):
                bank = pbk[2 + grp % 2]
                T.w(s_v, v_mp)
                T.w(s_v, v_xdtT)
                if grp >= 2:
                    T.w(s_v, v_yd[grp - 2])
                for slotj in range(4):
                    blk = grp * 4 + slotj
                    h_even, h_odd = 2 * blk, 2 * blk + 1
                    for par, h in ((0, h_even), (1, h_odd)):
                        lhs = xdtT[:, blk, par * 64:par * 64 + 64]
                        T.c("matmul", bank[par * 64:par * 64 + 64,
                                           slotj * 128:(slotj + 1) * 128],
                            lhs, mp_sb[:, h, :], start=True, stop=True,
                            tile_position=(0, par * 64),
                            then=(s_t.inc() if (slotj == 3 and par == 1)
                                  else ()))
                yd_done[grp] = s_t.n
                V.w(s_t, yd_done[grp])
                V.w(s_yst, s_yst.n)
                for slotj in range(4):
                    blk = grp * 4 + slotj
                    V.c("scalar_tensor_tensor",
                        ystage[:, (grp % 2) * 4 + slotj, :],
                        big1[:, blk, tc:tc + CH], dexp_sb[:, blk:blk + 1],
                        bank[:, slotj * 128:(slotj + 1) * 128],
                        ALU.mult, ALU.add,
                        then=(s_v.inc() if slotj == 3 else ()))
                v_yd[grp] = s_v.n
                if grp % 2 == 1:
                    b0 = (grp - 1) * 4
                    SY.w(s_v, v_yd[grp])
                    SY.c("dma_start",
                         out=yp_sp[b0:b0 + 8, :, gcol:gcol + CH].rearrange(
                             "b q t -> q b t"),
                         in_=ystage[:], then=s_yst.inc(16))

            V.w(s_t, yd_done[7])
            for g in range(8):
                b0 = g * 4
                dst = xdtT[:, b0:b0 + 4, :].rearrange(
                    "p b (h o) -> p (b h) o", h=2)
                mult = tb2[:, b0 * 2:b0 * 2 + 8].rearrange(
                    "p (h o) -> p h o", o=1).broadcast_to([128, 8, 64])
                V.c("tensor_tensor", dst, dst, mult, ALU.mult,
                    then=(s_v.inc() if g == 7 else ()))
            v_xdtd = s_v.n

            G.w(s_v, v_dexp2)
            G.c("dma_start", out=st_sp[gc], in_=s_state[:].rearrange(
                "p a b -> p (a b)"), then=s_stp.inc(16))
            cs_done = {}
            for grp in range(8):
                bank = pbk[4 + grp % 2]
                T.w(s_v, v_xdtd)
                T.w(s_v, v_r0)
                if grp == 0:
                    T.w(s_v, v_dexp2)
                if grp >= 2:
                    T.w(s_v, s_v.n)
                for slotj in range(4):
                    blk = grp * 4 + slotj
                    T.c("matmul", bank[:, slotj * 128:(slotj + 1) * 128],
                        xdtT[:, blk, :], bt_sb[:], start=True, stop=True,
                        then=(s_t.inc() if slotj == 3 else ()))
                cs_done[grp] = s_t.n
                V.w(s_t, cs_done[grp])
                V.w(s_stp, s_stp.n)
                for slotj in range(4):
                    blk = grp * 4 + slotj
                    V.c("scalar_tensor_tensor", s_state[:, blk, :],
                        s_state[:, blk, :], dexp2_sb[:, blk:blk + 1],
                        bank[:, slotj * 128:(slotj + 1) * 128],
                        ALU.mult, ALU.add,
                        then=(s_v.inc() if slotj == 3 else ()))
        v_ssda = s_v.n
        mark('ssda')
        if p == 0:
            wprep_step(128)
            wprep_finalize()

        if stop_after == "ssda":
            if "d_dt" in dout:
                SY.w(s_v, v_dt)
                SY.c("dma_start", out=dout["d_dt"][:, c0:c0 + pass_t],
                     in_=dtA[0:64, 0:pass_t], then=s_dbg.inc(16))
            return

        scr1b = scr1[:].bitcast(F16)
        scr2b = scr2[:].bitcast(F16)
        V.w(s_yst, s_yst.n)
        V.w(s_zs0, s_zs0.n)
        V.w(s_zs1, s_zs1.n)
        V.w(s_xq0, s_xq0.n)
        V.w(s_xq1, s_xq1.n)
        V.w(s_cs, s_cs.n)
        SC.w(s_cs, s_cs.n)
        nfb = pass_t // fb

        def col_stats_to_row(src_scr, op, flat_off, v_ready):
            nj = fb // 128
            T.w(s_v, v_ready)
            for j in range(nj):
                T.c("transpose", pbk[5][:, j * 128:(j + 1) * 128],
                    src_scr[:, j * 128:(j + 1) * 128], c_idf[:],
                    then=(s_t.inc() if j == nj - 1 else ()))
            V.w(s_t, s_t.n)
            for j in range(nj):
                V.c("tensor_reduce", stat4[:, j:j + 1],
                    pbk[5][:, j * 128:(j + 1) * 128], AX.X, op,
                    then=(s_v.inc() if j == nj - 1 else ()))
            T.w(s_v, s_v.n)
            T.c("transpose", pbk[4][0:nj, 0:128], stat4[:, 0:nj],
                c_idf[:], then=s_t.inc())
            V.w(s_t, s_t.n)
            V.c("tensor_copy", stat4T[0:nj, :], pbk[4][0:nj, 0:128],
                then=s_v.inc())
            G.w(s_v, s_v.n)
            G.c("dma_start", out=flat_sb[0:1, flat_off:flat_off + fb],
                in_=stat4T[0:nj, :], then=s_gb.inc(16))

        def bcast_row(row_ap, slot, v_ready):
            T.w(s_v, v_ready)
            T.c("matmul", pbk[4][:, 0:fb], ones_r[:], row_ap,
                start=True, stop=True, then=s_t.inc())
            V.w(s_t, s_t.n)
            V.c("tensor_copy", ffnbc[:, slot, 0:fb], pbk[4][:, 0:fb],
                then=s_v.inc())

        for fbi in range(nfb):
            mark('ffn_blk')
            fbg = p * nfb + fbi
            fc = 3 + fbi * fb
            gf = c0 + fbi * fb

            nj2 = fb // 128
            jb = 8 + fbi * nj2
            T.w(s_v, v_ssda)
            T.c("transpose", pbk[4][0:nj2, 0:128], stat4[:, jb:jb + nj2],
                c_idf[:], then=s_t.inc())
            V.w(s_t, s_t.n)
            V.c("tensor_copy", stat4T[0:nj2, :], pbk[4][0:nj2, 0:128],
                then=s_v.inc())
            G.w(s_t, s_t.n)
            G.w(s_v, s_v.n)
            G.c("dma_start", out=flat_sb[0:1, 0:fb], in_=stat4T[0:nj2, :],
                then=s_gb.inc(16))
            V.w(s_gb, s_gb.n)
            V.c("tensor_scalar_max", flat_sb[0:1, 0:fb], flat_sb[0:1, 0:fb],
                EPS)
            V.c("tensor_scalar_mul", flat_sb[0:1, 0:fb], flat_sb[0:1, 0:fb],
                1.0 / 127.0)
            V.c("tensor_tensor", flat_sb[0:1, 0:fb], flat_sb[0:1, 0:fb],
                scr3[0:1, 1024:1025].broadcast_to([1, fb]), ALU.mult,
                then=s_v.inc())
            bcast_row(flat_sb[0:1, 0:fb], 1, s_v.n)
            v_bc01 = s_v.n

            SY.w(s_xq0, s_xq0.n)
            SY.w(s_xq1, s_xq1.n)
            SY.w(s_v, v_ssda)
            SY.w(s_t, s_t.n)
            SY.c("dma_start", out=xq_sb[:, :, 0:fb],
                 in_=xq_sp.rearrange("k q t -> q k t")[:, :, gf:gf + fb],
                 then=s_xql.inc(16))
            v_xq = s_xql.n
            if "d_xq" in dout:
                SY.w(s_xql, s_xql.n)
                SY.c("dma_start",
                     out=dout["d_xq"].rearrange("k q t -> q k t")[:, :, gf:gf + fb],
                     in_=xq_sb[:, :, 0:fb], then=s_dbg.inc(16))

            p_rdy = {}
            mmd = {}
            evd = {}

            arCf = arC[:]
            w2half = {"v": 0}

            def ffn_panel(mi, which):
                if which == "w2":
                    if fbg == 0:
                        buf = wp_a
                    else:
                        buf = wp_a if mi % 2 == 0 else wb2
                else:
                    if fbg == 0:
                        buf = wp_a
                    else:
                        buf = wp_a if mi % 2 == 0 else wp_b
                sem = s_wpc if mi % 2 == 0 else s_wpd
                wbsem = s_wbc if mi % 2 == 0 else s_wbd
                if which == "w1":
                    dep, raw, cache = KH, w1_p, wbf_1
                else:
                    dep, raw, cache = KF, w2_p, wbf_2
                idx = mi if which == "w1" else mi - KF
                if fbg == 0:
                    if mi - 1 in mmd:
                        SY.w(s_t, mmd[mi - 1])
                elif mi - 2 in mmd:
                    SY.w(s_t, mmd[mi - 2])
                src_f = raw[idx].rearrange("(k q) c -> q k c", q=128)
                src_b = cache[idx].rearrange("(k q) c -> q k c", q=128)
                if fbg > 0:
                    SY.w(s_wbc, s_wbc.n)
                    SY.w(s_wbd, s_wbd.n)
                    SY.w(s_gsp, s_gsp.n)
                    if which == "w2" and mi % 2 == 1:
                        SY.w(s_t, mmd.get(KF - 1, 0))
                        SY.w(s_v, s_v.n)
                    SY.c("dma_start", out=buf[:, 0:dep, :], in_=src_b,
                        then=sem.inc(16))
                    p_rdy[mi] = (sem, sem.n)
                    return
                tcol = 0 if which == "w1" else 2
                SY.w(s_wbc, s_wbc.n)
                SY.w(s_wbd, s_wbd.n)
                if which == "w1":
                    SY.w(s_v, s_v.n)
                bflat = buf[:].rearrange("q a b -> q (a b)")
                if which == "w1":
                    fst = bflat.bitcast(F32)[:, 2048:4096].rearrange(
                        "q (k c) -> q k c", c=128)
                    SY.c("dma_start", out=fst, in_=src_f, then=sem.inc(16))
                    V.w(sem, sem.n)
                    V.w(s_t, s_t.n)
                    fsl = bflat.bitcast(F32)[:, 2048:4096]
                    V.c("tensor_scalar", scr1b[:, 0:2048], fsl,
                        thr_sb[:, tcol:tcol + 1], None, ALU.is_gt)
                    V.c("tensor_scalar", scr2b[:, 0:2048], fsl,
                        thr_sb[:, tcol + 1:tcol + 2], None, ALU.is_lt)
                    V.c("tensor_tensor", bflat[:, 0:2048], scr1b[:, 0:2048],
                        scr2b[:, 0:2048], ALU.subtract, then=s_v.inc())
                else:
                    for hf in range(4):
                        slot2 = hf % 2
                        hsem = s_pre0 if slot2 == 0 else s_pre1
                        SY.w(s_v, w2half["v"])
                        SY.w(s_t, mmd.get(KF - 1, 0))
                        SY.w(s_dbg, s_dbg.n)
                        SY.c("dma_start",
                             out=arCf[:, slot2 * 2048:(slot2 + 1) * 2048].rearrange(
                                 "q (k c) -> q k c", c=128),
                             in_=src_f[:, hf * 16:(hf + 1) * 16, :],
                             then=hsem.inc(16))
                        V.w(hsem, hsem.n)
                        V.w(s_t, s_t.n)
                        fsl = arCf[:, slot2 * 2048:(slot2 + 1) * 2048]
                        V.c("tensor_scalar", scr1b[:, 0:2048], fsl,
                            thr_sb[:, tcol:tcol + 1], None, ALU.is_gt)
                        V.c("tensor_scalar", scr2b[:, 0:2048], fsl,
                            thr_sb[:, tcol + 1:tcol + 2], None, ALU.is_lt)
                        V.c("tensor_tensor",
                            bflat[:, hf * 2048:hf * 2048 + 2048],
                            scr1b[:, 0:2048], scr2b[:, 0:2048], ALU.subtract,
                            then=s_v.inc())
                        w2half["v"] = s_v.n
                p_rdy[mi] = ("v", s_v.n)
                SY.w(s_v, s_v.n)
                SY.c("dma_start", out=src_b, in_=buf[:, 0:dep, :],
                     then=wbsem.inc(16))

            t_sq_mm = {}
            if fbg > 0:
                ffn_panel(0, "w1")
            for m in range(KF):
                if fbg == 0:
                    ffn_panel(m, "w1")
                elif m + 1 < KF:
                    ffn_panel(m + 1, "w1")
                if fbg == 0:
                    buf = wp_a
                else:
                    buf = wp_a if m % 2 == 0 else wp_b
                bank = pbk[m % 2]
                sr, sv = p_rdy[m]
                if sr == "v":
                    T.w(s_v, sv)
                else:
                    T.w(sr, sv)
                T.w(s_xql, v_xq)
                T.w(s_v, v_bc01)
                if m - 2 in evd:
                    T.w(s_v, evd[m - 2])
                for k in range(KH):
                    T.c("matmul", bank[:, 0:fb], buf[:, k, 0:128],
                        xq_sb[:, k, :], start=(k == 0), stop=(k == KH - 1),
                        then=(s_t.inc() if k == KH - 1 else ()))
                mmd[m] = s_t.n
                V.w(s_t, mmd[m])
                V.w(s_sc, s_sc.n)
                V.c("tensor_tensor", ug_sb[:, 0:fb], bank[:, 0:fb],
                    ffnbc[:, 1, 0:fb], ALU.mult, then=s_v.inc())
                evd[m] = s_v.n
                SC.w(s_v, evd[m])
                SC.c("activation", h_sb[:, m, :], ug_sb[:, 0:fb], AF.Gelu,
                     bias=b1_sb[:, m:m + 1], then=s_sc.inc())
            sc_gelu = s_sc.n
            sc_sq = {}
            for m in range(KF):
                half = (m % 2) * fb
                if m >= 2:
                    SC.w(s_t, t_sq_mm[m - 2])
                SC.c("activation", sq_sb[:, half:half + fb], h_sb[:, m, :],
                     AF.Square, bias=cst[:, 3:4], then=s_sc.inc())
                sc_sq[m] = s_sc.n
                T.w(s_sc, sc_sq[m])
                T.c("matmul", pbk[2][0:1, 0:fb], ones_cb[:], h_sb[:, m, :],
                    start=(m == 0), stop=(m == KF - 1))
                T.c("matmul", pbk[3][0:1, 0:fb], ones_cbb[:],
                    sq_sb[:, half:half + fb],
                    start=(m == 0), stop=(m == KF - 1), then=s_t.inc())
                t_sq_mm[m] = s_t.n
            t_hstats = s_t.n

            V.w(s_t, t_hstats)
            V.c("tensor_scalar_mul", scr3[0:1, 0:fb], pbk[2][0:1, 0:fb],
                1.0 / FF)
            V.c("tensor_scalar_mul", scr3[0:1, 512:512 + fb],
                pbk[3][0:1, 0:fb], 1.0 / FF)
            V.c("tensor_tensor", dt_raw[0:1, 0:fb], scr3[0:1, 0:fb],
                scr3[0:1, 0:fb], ALU.mult)
            V.c("tensor_tensor", scr3[0:1, 512:512 + fb],
                scr3[0:1, 512:512 + fb], dt_raw[0:1, 0:fb],
                ALU.subtract, then=s_v.inc())
            SC.w(s_v, s_v.n)
            SC.c("activation", dt_raw[0:1, 0:fb], scr3[0:1, 512:512 + fb],
                 AF.Sqrt, bias=cst[0:1, 0:1], then=s_sc.inc())
            V.w(s_sc, s_sc.n)
            V.c("reciprocal", flat_sb[0:1, 512:512 + fb],
                dt_raw[0:1, 0:fb])
            V.c("tensor_tensor", scr3[0:1, 0:fb], scr3[0:1, 0:fb],
                flat_sb[0:1, 512:512 + fb], ALU.mult)
            V.c("tensor_scalar_mul", scr3[0:1, 0:fb], scr3[0:1, 0:fb],
                -1.0, then=s_v.inc())
            bcast_row(flat_sb[0:1, 512:512 + fb], 2, s_v.n)
            bcast_row(scr3[0:1, 0:fb], 3, s_v.n)
            v_bc23 = s_v.n

            for m in range(KF):
                stg = scr1b if m % 2 == 0 else scr2b
                V.c("tensor_tensor", stg[:, 0:fb], h_sb[:, m, :],
                    ffnbc[:, 2, 0:fb], ALU.mult)
                V.c("tensor_tensor", h_sb[:, m, :], stg[:, 0:fb],
                    ffnbc[:, 3, 0:fb], ALU.add, then=s_v.inc())
            v_h2 = s_v.n
            SC.w(s_v, v_h2)
            for m in range(KF):
                SC.c("activation", h_sb[:, m, :], h_sb[:, m, :],
                     AF.Identity, bias=be_sb[:, m:m + 1],
                     scale=ga_sb[:, m:m + 1],
                     then=(s_sc.inc() if m == KF - 1 else ()))
            sc_hn = s_sc.n
            V.w(s_sc, sc_hn)
            if "d_h" in dout:
                SY.w(s_sc, sc_hn)
                SY.c("dma_start",
                     out=dout["d_h"].rearrange("k q t -> q k t")[:, :, gf:gf + fb],
                     in_=h_sb[:], then=s_dbg.inc(16))

            hsw = h_sb[:, :, 0:fb].rearrange("q k t -> q t k")
            V.w(s_gc, s_gc.n)
            V.c("tensor_reduce", scr3[:, 0:fb], hsw, AX.X, ALU.max)
            V.c("tensor_reduce", scr2[:, 0:fb], hsw, AX.X, ALU.min)
            V.c("scalar_tensor_tensor", scr3[:, 0:fb], scr2[:, 0:fb],
                -1.0, scr3[:, 0:fb], ALU.mult, ALU.max, then=s_v.inc())
            col_stats_to_row(scr3, ALU.max, 0, s_v.n)
            V.w(s_gb, s_gb.n)
            V.c("tensor_scalar_max", scr3[0:1, 0:fb], flat_sb[0:1, 0:fb], EPS)
            V.c("reciprocal", scr3[0:1, 512:512 + fb], scr3[0:1, 0:fb])
            V.c("tensor_scalar_mul", flat_sb[0:1, 512:512 + fb],
                scr3[0:1, 512:512 + fb], 127.0)
            V.c("tensor_scalar_mul", scr3[0:1, 0:fb], scr3[0:1, 0:fb],
                1.0 / 127.0)
            V.c("tensor_tensor", flat_sb[0:1, 0:fb],
                scr3[0:1, 0:fb],
                scr3[0:1, 1025:1026].broadcast_to([1, fb]), ALU.mult,
                then=s_v.inc())
            bcast_row(flat_sb[0:1, 512:512 + fb], 4, s_v.n)
            bcast_row(flat_sb[0:1, 0:fb], 5, s_v.n)

            if "d_h" in dout:
                V.w(s_dbg, s_dbg.n)
            for m in range(KF):
                stg = scr1 if m % 2 == 0 else scr2
                V.c("tensor_tensor", stg[:, 0:fb], h_sb[:, m, :],
                    ffnbc[:, 4, 0:fb], ALU.mult)
                V.c("tensor_scalar", h_sb[:, m, :], stg[:, 0:fb], MAGIC,
                    -MAGIC, ALU.add, ALU.add,
                    then=(s_v.inc() if m == KF - 1 else ()))
            v_hq = s_v.n
            mark('ffn1_hq')

            evd2 = {}
            if fbg > 0:
                ffn_panel(KF + 0, "w2")
            for m in range(KH):
                if fbg == 0:
                    ffn_panel(KF + m, "w2")
                elif m + 1 < KH:
                    ffn_panel(KF + m + 1, "w2")
                if fbg == 0:
                    buf = wp_a
                else:
                    buf = wp_a if m % 2 == 0 else wb2
                bank = pbk[m % 2]
                sr, sv = p_rdy[KF + m]
                if sr == "v":
                    T.w(s_v, sv)
                else:
                    T.w(sr, sv)
                T.w(s_v, v_hq)
                if m - 2 >= 0 and (m - 2) in evd2:
                    T.w(s_v, evd2[m - 2])
                for k in range(KF):
                    T.c("matmul", bank[:, 0:fb], buf[:, k, 0:128],
                        h_sb[:, k, :], start=(k == 0), stop=(k == KF - 1),
                        then=(s_t.inc() if k == KF - 1 else ()))
                mmd[KF + m] = s_t.n
                V.w(s_t, mmd[KF + m])
                V.w(s_sc, s_sc.n)
                V.c("tensor_tensor", ug_sb[:, 0:fb], bank[:, 0:fb],
                    ffnbc[:, 5, 0:fb], ALU.mult, then=s_v.inc())
                evd2[m] = s_v.n
                SC.w(s_v, evd2[m])
                SC.w(s_gsp, s_gsp.n)
                SC.c("activation", gstage[:, m % 2, 0:fb], ug_sb[:, 0:fb],
                     AF.Silu, bias=b2_sb[:, m:m + 1], then=s_sc.inc())
                SY.w(s_sc, s_sc.n)
                SY.c("dma_start", out=g_sp[m, :, gf:gf + fb],
                     in_=gstage[:, m % 2, 0:fb], then=s_gsp.inc(16))

        if "d_dt" in dout:
            SY.w(s_v, v_dt)
            SY.c("dma_start", out=dout["d_dt"][:, c0:c0 + pass_t],
                 in_=dtA[0:64, 0:pass_t], then=s_dbg.inc(16))
        if "d_xbc" in dout:
            SY.w(s_sc, sc_conv)
            SY.c("dma_start",
                 out=dout["d_xbc"].rearrange("k q t -> q k t")[:, :, c0:c0 + pass_t],
                 in_=big1[:, :, 3:Te], then=s_dbg.inc(16))

    gate = {}
    for p in range(n_pass):
        stage_a(p)
        gate = {"t": s_t.n, "v": s_v.n, "sc": s_sc.n}
        if stop_after == "inproj" and p == 0:
            break

    if stop_after not in ("inproj", "conv", "ssda", "ffn"):
        scr1b = scr1[:].bitcast(BF16)
        V.w(s_sc, s_sc.n)
        V.c("tensor_reduce", smallv[:, 5:6], cend_sb[:, 0:n_chunks], AX.X,
            ALU.add)
        V.c("tensor_tensor_scan", cend_sb[:, 17:17 + n_chunks],
            ones64[:, 0:n_chunks], cend_sb[:, 0:n_chunks], 0.0,
            ALU.mult, ALU.add)
        V.c("memset", cend_sb[:, 16:17], 0.0)
        V.c("memset", scr3[:, 0:2], 0.0)
        V.c("tensor_copy", scr3[0:64, 0:1], smallv[:, 5:6], then=s_v.inc())
        v_cc = s_v.n
        G.w(s_v, v_cc)
        G.c("dma_start", out=cc_in[:, 0:DIN],
            in_=s_state[:].rearrange("p a b -> p (a b)"), then=s_gb.inc(16))
        G.c("dma_start", out=cc_in[:, DIN:DIN + 2], in_=scr3[:, 0:2],
            then=s_gb.inc(16))
        G.w(s_gb, s_gb.n)
        G.c("collective_compute", "AllGather", ALU.bypass,
            replica_groups=(rg if rg is not None else [[0, 1, 2, 3], [4, 5, 6, 7]]),
            ins=[cc_in[:]], outs=[cc_out[:]], then=s_cc.inc())
        cc_done = s_cc.n
        mark('collective')

        V.w(s_cc, cc_done)
        V.c("memset", s_state[:], 0.0, then=s_v.inc())
        for j in range(4):
            SY.w(s_cc, cc_done)
            if j >= 1:
                SY.w(s_v, s_v.n)
            SY.w(s_dbg, s_dbg.n)
            SY.c("dma_start", out=ccld[:], in_=cc_out[j], then=s_ld2.inc(16))
            V.w(s_ld2, s_ld2.n)
            V.c("tensor_scalar", smallv[:, 5:6], ccld[0:64, DIN:DIN + 1],
                dmask_sb[:, j:j + 1], None, ALU.mult, then=s_v.inc())
            SC.w(s_v, s_v.n)
            SC.c("activation", smallv[:, 6:7], smallv[:, 5:6], AF.Exp,
                 then=s_sc.inc())
            V.w(s_sc, s_sc.n)
            expand_hp(smallv[:, 6:7], dexp2_sb[:], 4)
            for g8 in range(8):
                V.c("tensor_scalar",
                    scr1[:, 0:512], ccld[:, g8 * 512:(g8 + 1) * 512],
                    amask_sb[:, j:j + 1], None, ALU.mult)
                for s4 in range(4):
                    blk = g8 * 4 + s4
                    V.c("scalar_tensor_tensor", s_state[:, blk, :],
                        s_state[:, blk, :], dexp2_sb[:, blk:blk + 1],
                        scr1[:, s4 * 128:(s4 + 1) * 128], ALU.mult, ALU.add,
                        then=(s_v.inc() if s4 == 3 else ()))
        v_sin = s_v.n
        if "d_sin" in dout:
            SY.w(s_v, v_sin)
            SY.c("dma_start", out=dout["d_sin"][:],
                 in_=s_state[:].rearrange("p a b -> p (a b)"),
                 then=s_dbg.inc(16))

        wo_rdy = {}
        wo_mmd = {}

        def wout_panel(m, firstb):
            buf = wp_a if m % 2 == 0 else wp_b
            wbsem = s_wba if m % 2 == 0 else s_wbb
            src_f = wout_p[m].rearrange("(k q) c -> q k c", q=128)
            src_b = wbf_o[m].rearrange("(k q) c -> q k c", q=128)
            if m - 2 in wo_mmd:
                (G if firstb else SY).w(s_t, wo_mmd[m - 2])
            else:
                (G if firstb else SY).w(s_t, s_t.n)
            if firstb:
                G.w(wbsem, wbsem.n)
                G.w(s_wbc, s_wbc.n)
                G.w(s_wbd, s_wbd.n)
                G.w(s_gsp, s_gsp.n)
                G.c("dma_start", out=buf[:, 0:KD, :], in_=src_f,
                    then=s_wpa.inc(16) if m % 2 == 0 else s_wpb.inc(16))
                sem = s_wpa if m % 2 == 0 else s_wpb
                wo_rdy[m] = (sem, sem.n)
                G.w(sem, sem.n)
                G.c("dma_start", out=src_b, in_=buf[:, 0:KD, :],
                    then=wbsem.inc(16))
            else:
                SY.w(s_wba, s_wba.n)
                SY.w(s_wbb, s_wbb.n)
                SY.w(s_wbc, s_wbc.n)
                SY.w(s_wbd, s_wbd.n)
                SY.w(s_gsp, s_gsp.n)
                sem = s_wpc if m % 2 == 0 else s_wpd
                SY.c("dma_start", out=buf[:, 0:KD, :], in_=src_b,
                    then=sem.inc(16))
                wo_rdy[m] = (sem, sem.n)

        ld_free = {"yp": v_sin, "z": v_sin, "st": v_sin}
        for p2 in range(n_pass):
            c0 = p2 * pass_t
            if "d_ynorm" in dout:
                V.w(s_dbg, s_dbg.n)
            for ci in range(nch_pass):
                gc = p2 * nch_pass + ci
                gcol = gc * CH
                SY.w(s_v, ld_free["st"])
                SY.w(s_stp, s_stp.n)
                SY.c("dma_start", out=pb_ld[:],
                     in_=st_sp[gc].rearrange("q (a b) -> q a b", a=KD),
                     then=s_ld2.inc(16))
                r_st = s_ld2.n
                SY.w(s_v, ld_free["z"])
                SY.w(s_sc, ld_free.get("zsc", 0))
                SY.c("dma_start", out=pb_ld2[:],
                     in_=z_sp.rearrange("a q t -> q a t")[:, :, gcol:gcol + CH],
                     then=s_zs0.inc(16))
                r_z = s_zs0.n
                SY.w(s_v, ld_free["yp"])
                SY.c("dma_start", out=pb_ld3[:],
                     in_=yp_sp.rearrange("a q t -> q a t")[:, :, gcol:gcol + CH],
                     then=s_yst.inc(16))
                r_yp = s_yst.n
                SY.w(s_t, ld_free.get("ct", 0))
                SY.w(s_cs, s_cs.n)
                SY.c("dma_start", out=c_ld[:], in_=c_sp[gc],
                     then=s_cs.inc(16))
                r_c = s_cs.n

                V.w(s_v, v_sin)
                V.c("tensor_copy", smallv[:, 5:6], cend_sb[:, 16 + gc:17 + gc],
                    then=s_v.inc())
                SC.w(s_v, s_v.n)
                SC.c("activation", smallv[:, 6:7], smallv[:, 5:6], AF.Exp,
                     then=s_sc.inc())
                V.w(s_sc, s_sc.n)
                expand_hp(smallv[:, 6:7], dexp2_sb[:], 4)

                V.w(s_ld2, r_st)
                for blk in range(KD):
                    V.c("scalar_tensor_tensor", stot[:, blk, :],
                        s_state[:, blk, :], dexp2_sb[:, blk:blk + 1],
                        pb_ld[:, blk, :], ALU.mult, ALU.add,
                        then=(s_v.inc() if blk == KD - 1 else ()))
                v_stot = s_v.n

                for tg in range(8):
                    bank = pbk[4 + tg % 2]
                    T.w(s_v, v_stot)
                    if tg >= 2:
                        T.w(s_v, s_v.n)
                    for s4 in range(4):
                        blk = tg * 4 + s4
                        T.c("transpose", bank[:, s4 * 128:(s4 + 1) * 128],
                            stot[:, blk, :], c_idf[:],
                            then=(s_t.inc() if s4 == 3 else ()))
                    V.w(s_t, s_t.n)
                    for s4 in range(4):
                        blk = tg * 4 + s4
                        V.c("tensor_copy", stotT[:, blk, :],
                            bank[:, s4 * 128:(s4 + 1) * 128],
                            then=(s_v.inc() if s4 == 3 else ()))
                v_stT = s_v.n

                SC.w(s_v, v_stT)
                SC.c("activation", dcb[:, 0:CH], acs_sb[0:64, gcol:gcol + CH],
                     AF.Exp, then=s_sc.inc())
                sc_ea = s_sc.n

                v_grp = 0
                for g8 in range(8):
                    b_ea = pbk[2]
                    b_yo = pbk[3]
                    T.w(s_sc, sc_ea)
                    T.w(s_v, v_stT)
                    T.w(s_cs, r_c)
                    if g8 >= 1:
                        T.w(s_v, v_grp)
                    for s4 in range(4):
                        blk = g8 * 4 + s4
                        T.c("matmul", b_ea[:, s4 * 128:(s4 + 1) * 128],
                            c_hsel[:, blk * 128:(blk + 1) * 128],
                            dcb[:, 0:CH], start=True, stop=True)
                    for s4 in range(4):
                        blk = g8 * 4 + s4
                        T.c("matmul", b_yo[:, s4 * 128:(s4 + 1) * 128],
                            stotT[:, blk, :], c_ld[:],
                            start=True, stop=True,
                            then=(s_t.inc() if s4 == 3 else ()))
                    t_grp = s_t.n
                    V.w(s_t, t_grp)
                    V.w(s_yst, r_yp)
                    V.w(s_zs0, r_z)
                    V.c("tensor_copy", ea2_sb[:, 0:512], b_ea[:, 0:512],
                        then=s_v.inc())
                    SC.w(s_v, s_v.n)
                    for s4 in range(4):
                        blk = g8 * 4 + s4
                        SC.c("activation", pb_ld2[:, blk, :],
                             pb_ld2[:, blk, :], AF.Silu,
                             then=(s_sc.inc() if s4 == 3 else ()))
                    sc_z = s_sc.n
                    V.c("tensor_tensor", scr1[:, 0:512],
                        b_yo[:, 0:512],
                        ea2_sb[:, 0:512], ALU.mult)
                    V.c("tensor_tensor", scr1[:, 0:512], scr1[:, 0:512],
                        pb_ld3[:, g8 * 4:(g8 + 1) * 4, :].rearrange(
                            "q a b -> q (a b)"), ALU.add)
                    V.w(s_sc, sc_z)
                    V.c("tensor_tensor",
                        pb_ld3[:, g8 * 4:(g8 + 1) * 4, :].rearrange(
                            "q a b -> q (a b)"),
                        scr1[:, 0:512],
                        pb_ld2[:, g8 * 4:(g8 + 1) * 4, :].rearrange(
                            "q a b -> q (a b)"), ALU.mult)
                    V.c("tensor_tensor", scr1b[:, 0:512],
                        pb_ld3[:, g8 * 4:(g8 + 1) * 4, :].rearrange(
                            "q a b -> q (a b)"),
                        pb_ld3[:, g8 * 4:(g8 + 1) * 4, :].rearrange(
                            "q a b -> q (a b)"), ALU.mult,
                        then=s_v.inc())
                    v_grp = s_v.n
                    T.w(s_v, v_grp)
                    for s4 in range(4):
                        T.c("matmul", pbk[0][0:1, 0:CH], ones_cbb[:],
                            scr1b[:, s4 * 128:(s4 + 1) * 128],
                            start=(g8 == 0 and s4 == 0),
                            stop=(g8 == 7 and s4 == 3),
                            then=(s_t.inc() if (g8 == 7 and s4 == 3) else ()))
                t_ssq = s_t.n
                ld_free["z"] = s_v.n
                ld_free["zsc"] = s_sc.n
                ld_free["st"] = v_stot
                ld_free["ct"] = s_t.n

                V.w(s_t, t_ssq)
                V.c("tensor_scalar_mul", scr3[0:1, 0:CH], pbk[0][0:1, 0:CH],
                    1.0 / DIN, then=s_v.inc())
                SC.w(s_v, s_v.n)
                SC.c("activation", scr3[0:1, 256:256 + CH], scr3[0:1, 0:CH],
                     AF.Sqrt, bias=cst[0:1, 0:1], then=s_sc.inc())
                V.w(s_sc, s_sc.n)
                V.c("reciprocal", scr3[0:1, 0:CH], scr3[0:1, 256:256 + CH],
                    then=s_v.inc())
                T.w(s_v, s_v.n)
                T.c("matmul", pbk[1][:, 0:CH], ones_r[:], scr3[0:1, 0:CH],
                    start=True, stop=True, then=s_t.inc())
                t_rs = s_t.n
                V.w(s_t, t_rs)
                for blk in range(KD):
                    V.c("scalar_tensor_tensor", ynb[:, blk, ci * CH:(ci + 1) * CH],
                        pb_ld3[:, blk, :], nw_sb[:, blk:blk + 1],
                        pbk[1][:, 0:CH], ALU.mult, ALU.mult,
                        then=(s_v.inc() if blk == KD - 1 else ()))
                ld_free["yp"] = s_v.n
            v_ynb = s_v.n
            if "d_ynorm" in dout:
                SY.w(s_v, v_ynb)
                SY.c("dma_start",
                     out=dout["d_ynorm"].rearrange("k q t -> q k t")[:, :, c0:c0 + pass_t],
                     in_=ynb[:], then=s_dbg.inc(16))

            mark('phB_norm')
            firstb = (p2 == 0)
            SY.w(s_dbg, s_dbg.n)
            SY.w(s_gsp, s_gsp.n)
            wout_panel(0, firstb)
            ev_out = {}
            for m in range(KH):
                if m + 1 < KH:
                    wout_panel(m + 1, firstb)
                buf = wp_a if m % 2 == 0 else wp_b
                sem, sv = wo_rdy[m]
                nslc = nslices(pass_t, 512)
                for nb, (oo, wnb) in enumerate(nslc):
                    bank = pbk[(m + nb) % 2]
                    T.w(sem, sv)
                    T.w(s_v, v_ynb)
                    prev = (m * len(nslc) + nb) - 2
                    if prev >= 0 and prev in ev_out:
                        T.w(s_v, ev_out[prev])
                    for k in range(KD):
                        T.c("matmul", bank[:, 0:wnb], buf[:, k, 0:128],
                            ynb[:, k, oo:oo + wnb], start=(k == 0),
                            stop=(k == KD - 1),
                            then=(s_t.inc() if k == KD - 1 else ()))
                    wo_mmd[m] = s_t.n
                    SY.w(s_v, ev_out.get(prev, 0))
                    SY.c("dma_start", out=gxl[:, m % 2, 0:wnb],
                         in_=g_sp[m, :, c0 + oo:c0 + oo + wnb],
                         then=s_gl.inc(16))
                    r_g = s_gl.n
                    SY.c("dma_start", out=outst[:, m % 2, 0:wnb],
                         in_=x_fm[m * 128:(m + 1) * 128,
                                  3 + c0 + oo:3 + c0 + oo + wnb],
                         then=s_x.inc(16))
                    r_x = s_x.n
                    V.w(s_t, wo_mmd[m])
                    V.w(s_gl, r_g)
                    V.w(s_x, r_x)
                    V.w(s_o, s_o.n)
                    V.c("tensor_tensor", scr1[:, 0:wnb], bank[:, 0:wnb],
                        gxl[:, m % 2, 0:wnb], ALU.mult)
                    V.c("tensor_tensor", scr2[:, 0:wnb], scr1[:, 0:wnb],
                        outst[:, m % 2, 0:wnb], ALU.add, then=s_v.inc())
                    ev_out[m * len(nslc) + nb] = s_v.n
                    SY.w(s_v, s_v.n)
                    SY.c("dma_start",
                         out=out_fm[m * 128:(m + 1) * 128, c0 + oo:c0 + oo + wnb],
                         in_=scr2[:, 0:wnb], then=s_o.inc(16))

    if "d_z" in dout:
        zt = scr1[:].bitcast(F16)
        SY.w(s_zs0, s_zs0.n)
        SY.w(s_zs1, s_zs1.n)
        for kk in range(KD):
            SY.c("dma_start", out=zt[:, 0:t_core], in_=z_sp[kk],
                 then=s_dbg.inc(16))
            SY.w(s_dbg, s_dbg.n)
            SY.c("dma_start", out=dout["d_z"][kk], in_=zt[:, 0:t_core],
                 then=s_dbg.inc(16))
            SY.w(s_dbg, s_dbg.n)

    if "d_g" in dout:
        zt3 = scr1[:].bitcast(F16)
        SY.w(s_gsp, s_gsp.n)
        for kk in range(KH):
            SY.c("dma_start", out=zt3[:, 0:t_core], in_=g_sp[kk],
                 then=s_dbg.inc(16))
            SY.w(s_dbg, s_dbg.n)
            SY.c("dma_start", out=dout["d_g"][kk], in_=zt3[:, 0:t_core],
                 then=s_dbg.inc(16))
            SY.w(s_dbg, s_dbg.n)

    if "d_acs" in dout:
        SY.w(s_v, s_v.n)
        SY.c("dma_start", out=dout["d_acs"][:, :], in_=acs_sb[0:64, :],
             then=s_dbg.inc(16))
    if "d_state" in dout:
        G.w(s_v, s_v.n)
        G.c("dma_start", out=dout["d_state"][:, 0:DIN],
            in_=s_state[:].rearrange("p a b -> p (a b)"), then=s_gb.inc(16))
        V.c("memset", scr3[:, 0:2], 0.0, then=s_v.inc())
        SY.w(s_v, s_v.n)
        SY.c("dma_start", out=dout["d_state"][:, DIN:DIN + 2],
             in_=scr3[:, 0:2], then=s_dbg.inc(16))
    if "d_yp" in dout:
        zt2 = scr1[:].bitcast(F16)
        SY.w(s_yst, s_yst.n)
        for kk in range(KD):
            SY.c("dma_start", out=zt2[:, 0:t_core], in_=yp_sp[kk],
                 then=s_dbg.inc(16))
            SY.w(s_dbg, s_dbg.n)
            SY.c("dma_start", out=dout["d_yp"][kk], in_=zt2[:, 0:t_core],
                 then=s_dbg.inc(16))
            SY.w(s_dbg, s_dbg.n)

    if stop_after in ("inproj", "conv", "ssda", "ffn"):
        V.c("memset", scr2[:, 0:512], 0.0, then=s_v.inc())
        SY.w(s_v, s_v.n)
        SY.w(s_dbg, s_dbg.n)
        for k in range(KH):
            for o, w in nslices(t_core):
                SY.c("dma_start",
                     out=out_fm.rearrange("(k q) t -> q k t", q=128)[:, k, o:o + w],
                     in_=scr2[:, 0:w], then=s_o.inc(16))
    SY.w(s_o, s_o.n)
    SY.w(s_dbg, s_dbg.n)
    G.w(s_gb, s_gb.n)

    with nc.Block() as block:
        block.tensor(lambda e: T.replay(e))
        block.vector(lambda e: V.replay(e))
        block.scalar(lambda e: SC.replay(e))
        block.gpsimd(lambda e: G.replay(e))
        block.sync(lambda e: SY.replay(e))

    ctx.close()
    return nc



def stage_weight_panels(w):
    outF, inF = w.shape
    n_p = cdiv(outF, 128)
    pad = n_p * 128 - outF
    wt = np.ascontiguousarray(w.T)
    if pad:
        wt = np.concatenate([wt, np.zeros((inF, pad), w.dtype)], 1)
    return np.ascontiguousarray(wt.reshape(inF, n_p, 128).transpose(1, 0, 2))


def stage_all(inputs, t_core=2048, n_cores=8):
    import ml_dtypes
    x = np.asarray(inputs["L0_attn_hidden_states"])
    B, L, D = x.shape
    segs = n_cores // B
    sh = {}
    sh["w_in_p"] = stage_weight_panels(np.asarray(inputs["L0_attn_in_proj_w"]))
    sh["w1_p"] = stage_weight_panels(np.asarray(inputs["L0_ffn_w1"]))
    sh["w2_p"] = stage_weight_panels(np.asarray(inputs["L0_ffn_w2"]))
    sh["wout_p"] = stage_weight_panels(np.asarray(inputs["L0_attn_out_proj_w"]))
    cw = np.asarray(inputs["L0_attn_conv_w"])
    sh["conv_w"] = np.ascontiguousarray(
        cw.reshape(KC, 128, KCONV).transpose(1, 0, 2)).reshape(128, KC * KCONV)
    sh["conv_b"] = np.ascontiguousarray(
        np.asarray(inputs["L0_attn_conv_b"]).reshape(KC, 128).T)
    sh["dtb"] = np.asarray(inputs["L0_attn_dt_bias"]).reshape(NH, 1).astype(np.float32)
    sh["alog"] = np.asarray(inputs["L0_attn_A_log"]).reshape(NH, 1).astype(np.float32)
    sh["dvec"] = np.asarray(inputs["L0_attn_D"]).reshape(NH, 1).astype(np.float32)
    sh["nw"] = np.ascontiguousarray(
        np.asarray(inputs["L0_attn_norm_w"]).reshape(KD, 128).T)
    sh["b1"] = np.ascontiguousarray(
        np.asarray(inputs["L0_ffn_b1"]).reshape(KF, 128).T)
    sh["ga"] = np.ascontiguousarray(
        np.asarray(inputs["L0_ffn_gamma"]).reshape(KF, 128).T)
    sh["be"] = np.ascontiguousarray(
        np.asarray(inputs["L0_ffn_beta"]).reshape(KF, 128).T)
    sh["b2"] = np.ascontiguousarray(
        np.asarray(inputs["L0_ffn_b2"]).reshape(KH, 128).T)
    sh["idf"] = np.eye(128, dtype=np.float32)
    sh["idb"] = np.eye(128, dtype=ml_dtypes.bfloat16)
    sh["up01"] = np.triu(np.ones((CH, CH))).astype(ml_dtypes.bfloat16)
    hsel = np.zeros((NH, DIN), dtype=ml_dtypes.bfloat16)
    for h in range(NH):
        hsel[h, h * HDIM:(h + 1) * HDIM] = 1
    sh["hsel"] = hsel
    psel = np.zeros((NH, 128), dtype=ml_dtypes.bfloat16)
    for h in range(NH):
        psel[h, (h % 2) * 64:(h % 2) * 64 + 64] = 1
    sh["psel"] = psel
    bsel = np.zeros((NH, KD), dtype=ml_dtypes.bfloat16)
    for h in range(NH):
        bsel[h, h // 2] = 1
    sh["bsel"] = bsel

    cores = []
    for c in range(n_cores):
        b, q = divmod(c, segs)
        t0 = q * t_core
        xs = x[b, max(0, t0 - 3):t0 + t_core]
        if t0 == 0:
            xs = np.concatenate([np.zeros((3, D), x.dtype), xs], 0)
        d = dict(sh)
        d["x_fm"] = np.ascontiguousarray(xs.T)
        am = np.zeros((128, 4), np.float32)
        dm = np.zeros((NH, 4), np.float32)
        am[:, :q] = 1.0
        dm[:, :q] = 1.0
        d["amask"] = am
        d["dmask"] = dm
        cores.append(d)
    return cores


def assemble_output(results, t_core=2048, n_cores=8):
    B, L, D = 2, 8192, 2048
    segs = n_cores // B
    out = np.empty((B, L, D), np.float32)
    for c in range(n_cores):
        b, q = divmod(c, segs)
        out[b, q * t_core:(q + 1) * t_core] = results[c]["out_fm"].T
    return out


def kernel(**inputs):
    t_core = 2048
    cores = stage_all(inputs, t_core=t_core, n_cores=8)
    nc = build_program(t_core=t_core, pass_t=512, fb=512)
    res = run_bass_kernel_spmd(nc, cores, core_ids=list(range(8)))
    return assemble_output(res.results, t_core=t_core, n_cores=8)
